# version 17
# baseline (speedup 1.0000x reference)
import numpy as np
import concourse.bass as bass
import concourse.mybir as mybir
from concourse.alu_op_type import AluOpType as OP
from contextlib import ExitStack
from concourse.bass_utils import run_bass_kernel_spmd

F32 = mybir.dt.float32
BF16 = mybir.dt.bfloat16
AF = mybir.ActivationFunctionType
AX = mybir.AxisListType


ENGS = ("sync", "tensor", "vector", "scalar", "gpsimd")


def _reg_const(nc, val, ap):
    try:
        nc.const_aps.register(F32, val, ap)
    except Exception:
        pass


class Plan:

    def __init__(self, nc):
        self.nc = nc
        self.q = {e: [] for e in ENGS}
        self.sems = {}
        self.semv = {}
        self.stack = ExitStack()
        self.root_stack = self.stack
        self.scope_id = 0
        self.fence = None

    def scope(self):
        from contextlib import contextmanager

        @contextmanager
        def _s():
            old = self.stack
            self.scope_id += 1
            self.stack = ExitStack()
            try:
                yield
            finally:
                self.stack.close()
                self.stack = old
        return _s()

    def barrier(self):
        if self.fence is None:
            old = self.stack
            self.stack = self.root_stack
            self.fence = self.sbuf("fencetile", [1, 2], F32)
            self.stack = old
        ft = self.fence
        self.emit("vector", lambda h: h.memset(ft[0:1, 0:1], 0.0), inc=("v", 1))
        self.emit("scalar", lambda h: h.activation(out=ft[0:1, 1:2], in_=ft[0:1, 0:1],
                                                   func=AF.Identity), inc=("a", 1))
        self.emit("gpsimd", lambda h: h.memset(ft[0:1, 1:2], 0.0), inc=("g", 1))
        pairs = [(s, v) for s, v in self.semv.items()]
        for e in ENGS:
            self.wait(e, *pairs)

    def sem(self, name):
        h = self.stack.enter_context(self.nc.semaphore(name))
        self.sems[name] = h
        self.semv[name] = 0
        return h

    def sbuf(self, name, shape, dtype):
        return self.stack.enter_context(
            self.nc.sbuf_tensor(f"s{self.scope_id}_{name}", shape, dtype))

    def psum(self, name, shape, dtype=F32):
        return self.stack.enter_context(
            self.nc.psum_tensor(f"s{self.scope_id}_{name}", shape, dtype))

    def emit(self, eng, fn, wait=None, inc=None):
        if inc is not None:
            self.semv[inc[0]] += inc[1]
        sems = self.sems

        def f(h):
            if wait:
                for s, v in wait:
                    h.wait_ge(sems[s], v)
            r = fn(h)
            if inc is not None:
                r.then_inc(sems[inc[0]], inc[1])
        self.q[eng].append(f)

    def wait(self, eng, *pairs):
        sems = self.sems

        def f(h):
            for s, v in pairs:
                h.wait_ge(sems[s], v)
        self.q[eng].append(f)

    def val(self, semname):
        return self.semv[semname]

    def after_all(self, semname):
        return (semname, self.semv[semname])

    def finalize(self):
        nc = self.nc
        q = self.q
        with nc.Block() as block:
            @block.sync
            def _(h):
                for f in q["sync"]:
                    f(h)

            @block.tensor
            def _(h):
                for f in q["tensor"]:
                    f(h)

            @block.vector
            def _(h):
                for f in q["vector"]:
                    f(h)

            @block.scalar
            def _(h):
                for f in q["scalar"]:
                    f(h)

            @block.gpsimd
            def _(h):
                for f in q["gpsimd"]:
                    f(h)
        self.stack.close()


T = 2048
NCH = 8
NTT = 4
NTS = 16
NPAIR = 8


def emit_l0(nc, P, d, nseq=2):
    xT = d["xT"]
    wqkvT = d["wqkvT"]
    woutT = d["woutT"]
    cbT = d["cbT"]
    bout = d["bout"]
    scb = d["scb"]
    fbias = d["fbias"]
    x1T = d["x1T"]

    xn = P.sbuf("xn", [128, NCH, T], BF16)
    wbuf = P.sbuf("wbuf", [128, NCH, 2048], BF16)
    qp = P.sbuf("qp", [128, NCH, T], BF16)
    kp = P.sbuf("kp", [128, NTS, 1024], BF16)
    vt = P.sbuf("vt", [128, NTS, 16 * 65], BF16)
    kvs = P.sbuf("kvs", [128, NPAIR * 130], BF16)
    U = P.sbuf("U", [128, NCH, 512], F32)
    SP = P.sbuf("SP", [128, 8192], BF16)
    sD = P.sbuf("sD", [128, 512], F32)
    sE = P.sbuf("sE", [128, 512], F32)
    sF = P.sbuf("sF", [128, 512], F32)
    sG = P.sbuf("sG", [128, 512], F32)
    ones128 = P.sbuf("ones128", [128, 128], BF16)
    ones1f = P.sbuf("ones1f", [1, 64], F32)
    ksumC = P.sbuf("ksumC", [128, NPAIR, 64], BF16)
    boutS = P.sbuf("boutS", [128, 8], F32)
    scbS = P.sbuf("scbS", [128, 8], F32)
    fbS = P.sbuf("fbS", [128, 8], F32)
    ea = sF
    ef = sG

    xtile = U
    sA = U[:, 0:2, :].rearrange("p a b -> p (a b)")
    xb = SP[:, 0:4096].rearrange("p (a b) -> p a b", b=512)
    sqb = SP[:, 4096:8192].rearrange("p (a b) -> p a b", b=512)
    sB = SP[:, 0:1024]
    sC = SP[:, 1024:2048]

    psBig = P.psum("psBig", [128, 2048])
    psB2 = P.psum("psB2", [128, 1024])
    psQ1 = P.psum("psQ1", [128, 512])
    psQ2 = P.psum("psQ2", [128, 512])

    E = P.emit

    E("vector", lambda h: h.memset(ones128[:, :], 1.0))
    E("vector", lambda h: h.memset(ones1f[:, :], 1.0))
    E("sync", lambda h: h.dma_start(out=boutS[:, :], in_=bout), inc=("lp", 16))
    E("sync", lambda h: h.dma_start(out=scbS[:, :], in_=scb), inc=("lp", 16))
    E("sync", lambda h: h.dma_start(out=fbS[:, :], in_=fbias), inc=("lp", 16))
    ld_params = P.val("lp")
    E("vector", lambda h: h.tensor_scalar(out=scbS[:, :], in0=scbS[:, :], scalar1=1.0 / 127.0,
                                          scalar2=None, op0=OP.mult),
      wait=[("lp", ld_params)], inc=("v", 1))
    v_scb = P.val("v")
    ln_war = 0
    vt_ones_v = None
    a_kvfree = None
    v_vt = None

    for s in range(nseq):
        t_wb = P.val("t")
        for c in range(8):
            E("gpsimd", lambda h, c=c: h.dma_start(out=wbuf[:, c, :],
                                                   in_=wqkvT[c * 128:(c + 1) * 128, 1024:3072]),
              wait=[("t", t_wb)] if (c == 0 and s > 0) else None, inc=("w", 16))
        w_kv = P.val("w")
        for tt in range(NTT):
            tsl = slice(tt * 512, (tt + 1) * 512)
            st_now = P.val("st")
            v_now = P.val("v")
            for c in range(8):
                E("sync", lambda h, c=c, s=s, tsl=tsl: h.dma_start(
                    out=xtile[:, c, :], in_=xT[s, c * 128:(c + 1) * 128, tsl]),
                  wait=[("st", st_now), ("v", v_now)] if c == 0 else None, inc=("ld", 16))
            ld_x = P.val("ld")
            for c in range(8):
                w0 = [("ld", ld_x), ("t", ln_war)] if c == 0 else None
                E("vector", lambda h, c=c: h.tensor_copy(out=xb[:, c, :], in_=xtile[:, c, :]), wait=w0)
                E("vector", lambda h, c=c: h.tensor_tensor(
                    out=sqb[:, c, :], in0=xb[:, c, :], in1=xb[:, c, :], op=OP.mult),
                  inc=("v", 1) if c == 7 else None)
            v_cast = P.val("v")
            for c in range(8):
                E("tensor", lambda h, c=c: h.matmul(psB2[:, 0:512], ones128[:, :], xb[:, c, :],
                                                    start=(c == 0), stop=(c == 7)),
                  wait=[("v", v_cast)] if c == 0 else None)
            for c in range(8):
                E("tensor", lambda h, c=c: h.matmul(psB2[:, 512:1024], ones128[:, :], sqb[:, c, :],
                                                    start=(c == 0), stop=(c == 7)),
                  inc=("t", 1) if c == 7 else None)
            t_stats = P.val("t")
            ln_war = t_stats
            E("vector", lambda h: h.tensor_scalar(out=sD[:, :], in0=psB2[:, 0:512],
                                                  scalar1=1.0 / 1024.0, scalar2=None, op0=OP.mult),
              wait=[("t", t_stats)])
            E("vector", lambda h: h.tensor_scalar(out=sG[:, :], in0=psB2[:, 512:1024],
                                                  scalar1=1.0 / 1024.0, scalar2=None, op0=OP.mult))
            E("vector", lambda h: h.tensor_tensor(out=sE[:, :], in0=sD[:, :], in1=sD[:, :], op=OP.mult))
            E("vector", lambda h: h.tensor_tensor(out=sE[:, :], in0=sG[:, :], in1=sE[:, :],
                                                  op=OP.subtract), inc=("v", 1))
            v_var = P.val("v")
            E("scalar", lambda h: h.activation(out=sG[:, :], in_=sE[:, :], func=AF.Ln, bias=1e-5),
              wait=[("v", v_var)], inc=("a", 1))
            a_ln = P.val("a")
            E("scalar", lambda h: h.activation(out=sE[:, :], in_=sG[:, :], func=AF.Exp, scale=-0.5),
              wait=[("a", a_ln)], inc=("a", 1))
            a_rstd = P.val("a")
            E("vector", lambda h: h.tensor_tensor(out=sF[:, :], in0=sD[:, :], in1=sE[:, :], op=OP.mult),
              wait=[("a", a_rstd)])
            for c in range(8):
                E("vector", lambda h, c=c: h.tensor_tensor(out=xtile[:, c, :], in0=xtile[:, c, :],
                                                           in1=sE[:, :], op=OP.mult))
                E("vector", lambda h, c=c, tsl=tsl: h.tensor_tensor(
                    out=xn[:, c, tsl], in0=xtile[:, c, :], in1=sF[:, :], op=OP.subtract),
                  inc=("v", 1) if c == 7 else None)
        v_xn = P.val("v")
        if vt_ones_v is None:
            E("vector", lambda h: h.memset(
                vt[:, :, :].rearrange("p a (h w) -> p a h w", w=65)[:, :, :, 64:65], 1.0),
              inc=("v", 1))
            vt_ones_v = P.val("v")

        kv_add_prev = v_xn
        for ts in range(NTS):
            ssl = slice(ts * 128, (ts + 1) * 128)
            for c in range(8):
                for g4 in range(4):
                    w0 = None
                    if c == 0 and g4 == 0:
                        if ts == 0:
                            w0 = [("v", v_xn), ("w", w_kv)]
                            if a_kvfree is not None:
                                w0 += [("a", a_kvfree), ("v", v_vt)]
                        else:
                            w0 = [("a", a_kvfree), ("v", v_vt)]
                    E("tensor", lambda h, c=c, g4=g4, ssl=ssl: h.matmul(
                        psBig[:, g4 * 512:(g4 + 1) * 512], xn[:, c, ssl],
                        wbuf[:, c, g4 * 512:(g4 + 1) * 512],
                        start=(c == 0), stop=(c == 7)), wait=w0,
                        inc=("t", 1) if (c == 7 and g4 == 3) else None)
            t_kv = P.val("t")
            E("scalar", lambda h: h.activation(out=sA[:, :], in_=psBig[:, 0:1024],
                                               func=AF.Relu, scale=-1.0),
              wait=[("t", t_kv), ("v", kv_add_prev)], inc=("a", 1))
            a_r1 = P.val("a")
            E("scalar", lambda h: h.activation(out=sC[:, :], in_=psBig[:, 0:1024], func=AF.Relu),
              inc=("a", 1))
            a_kvfree = P.val("a")
            E("scalar", lambda h: h.activation(out=sB[:, :], in_=sA[:, :], func=AF.Exp, scale=-1.0),
              wait=[("a", a_r1)], inc=("a", 1))
            a_exp = P.val("a")
            E("vector", lambda h, ts=ts: h.tensor_tensor(out=kp[:, ts, :], in0=sB[:, :], in1=sC[:, :],
                                                         op=OP.add),
              wait=[("a", a_exp), ("v", vt_ones_v)], inc=("v", 1))
            kv_add_prev = P.val("v")
            E("vector", lambda h, ts=ts: h.tensor_copy(
                out=vt[:, ts, :].rearrange("p (h w) -> p h w", w=65)[:, :, 0:64],
                in_=psBig[:, 1024:2048].rearrange("p (h w) -> p h w", w=64)),
              inc=("v", 1))
            v_vt = P.val("v")
        v_kv_done = P.val("v")

        t_all_kv = P.val("t")
        for c in range(8):
            E("gpsimd", lambda h, c=c: h.dma_start(out=wbuf[:, c, 0:1024],
                                                   in_=wqkvT[c * 128:(c + 1) * 128, 0:1024]),
              wait=[("t", t_all_kv)] if c == 0 else None, inc=("w", 16))
        w_q = P.val("w")
        a_start = P.val("a")
        ps_free = {id(psQ1): a_start, id(psQ2): a_start}
        q_add_prev = v_kv_done
        for m in range(8):
            for tt in range(NTT):
                ps = psQ1 if (m * NTT + tt) % 2 == 0 else psQ2
                tsl = slice(tt * 512, (tt + 1) * 512)
                for c in range(8):
                    w0 = None
                    if c == 0:
                        w0 = [("w", w_q)]
                        if ps_free[id(ps)] is not None:
                            w0.append(("a", ps_free[id(ps)]))
                    E("tensor", lambda h, c=c, m=m, tsl=tsl, ps=ps: h.matmul(
                        ps[:, :], wbuf[:, c, m * 128:(m + 1) * 128], xn[:, c, tsl],
                        start=(c == 0), stop=(c == 7)), wait=w0,
                        inc=("t", 1) if c == 7 else None)
                t_q = P.val("t")
                E("scalar", lambda h, ps=ps: h.activation(out=sA[:, 0:512], in_=ps[:, :],
                                                          func=AF.Relu, scale=-1.0),
                  wait=[("t", t_q), ("v", q_add_prev)], inc=("a", 1))
                a_r1 = P.val("a")
                E("scalar", lambda h, ps=ps: h.activation(out=sC[:, 0:512], in_=ps[:, :], func=AF.Relu),
                  inc=("a", 1))
                ps_free[id(ps)] = P.val("a")
                E("scalar", lambda h: h.activation(out=sB[:, 0:512], in_=sA[:, 0:512],
                                                   func=AF.Exp, scale=-1.0),
                  wait=[("a", a_r1)], inc=("a", 1))
                a_q = P.val("a")
                E("vector", lambda h, m=m, tsl=tsl: h.tensor_tensor(
                    out=qp[:, m, tsl], in0=sB[:, 0:512], in1=sC[:, 0:512], op=OP.add),
                  wait=[("a", a_q)], inc=("v", 1))
                q_add_prev = P.val("v")
        v_q_done = P.val("v")
        a_q_done = P.val("a")

        kvs_evac = None
        for p in range(NPAIR):
            csl = slice(p * 130, (p + 1) * 130)
            for ts in range(NTS):
                w0 = None
                if ts == 0:
                    w0 = [("v", v_kv_done)]
                    if kvs_evac is not None:
                        w0.append(("v", kvs_evac))
                E("tensor", lambda h, p=p, ts=ts, csl=csl: h.matmul(
                    psB2[:, 0:130], kp[:, ts, p * 128:(p + 1) * 128], vt[:, ts, csl],
                    start=(ts == 0), stop=(ts == NTS - 1)), wait=w0,
                    inc=("t", 1) if ts == NTS - 1 else None)
            t_p = P.val("t")
            E("vector", lambda h, csl=csl: h.tensor_copy(out=kvs[:, csl], in_=psB2[:, 0:130]),
              wait=[("t", t_p)], inc=("v", 1))
            kvs_evac = P.val("v")
        v_kvs = P.val("v")

        for p in range(NPAIR):
            E("vector", lambda h, p=p: h.tensor_copy(
                out=ksumC[:, p, :],
                in_=kvs[:, p * 130 + 64:p * 130 + 65].broadcast_to([128, 64])),
              wait=[("v", v_kvs)] if p == 0 else None, inc=("v", 1) if p == 7 else None)
        v_ksumC = P.val("v")
        mult_done = [None] * 4
        addz_done = [None] * 2
        for p in range(NPAIR):
            for tt in range(NTT):
                idx = p * NTT + tt
                tsl = slice(tt * 512, (tt + 1) * 512)
                psO = psBig[:, (idx % 4) * 512:(idx % 4) * 512 + 512]
                psZ2 = psB2[:, (idx % 2) * 512:(idx % 2) * 512 + 512]
                sZ = sD if idx % 2 == 0 else sE
                w0 = [("v", v_ksumC), ("a", a_q_done)] if idx == 0 else []
                if addz_done[idx % 2] is not None:
                    w0.append(("a", addz_done[idx % 2]))
                E("tensor", lambda h, p=p, tsl=tsl, psZ2=psZ2: h.matmul(
                    psZ2[0:64, :], ksumC[0:64, p, :], qp[0:64, p, tsl],
                    start=True, stop=True, tile_position=(0, 0)), wait=w0 or None)
                E("tensor", lambda h, p=p, tsl=tsl, psZ2=psZ2: h.matmul(
                    psZ2[64:128, :], ksumC[64:128, p, :], qp[64:128, p, tsl],
                    start=True, stop=True, tile_position=(64, 64)), inc=("t", 1))
                t_z = P.val("t")
                w1 = []
                if mult_done[idx % 4] is not None:
                    w1.append(("v", mult_done[idx % 4]))
                E("tensor", lambda h, p=p, tsl=tsl, psO=psO: h.matmul(
                    psO[0:64, :], kvs[0:64, p * 130:p * 130 + 64], qp[0:64, p, tsl],
                    start=True, stop=True, tile_position=(0, 0)), wait=w1 or None)
                E("tensor", lambda h, p=p, tsl=tsl, psO=psO: h.matmul(
                    psO[64:128, :], kvs[64:128, p * 130 + 65:p * 130 + 129], qp[64:128, p, tsl],
                    start=True, stop=True, tile_position=(64, 64)), inc=("t", 1))
                t_o = P.val("t")
                E("scalar", lambda h, psZ2=psZ2, sZ=sZ: h.activation(
                    out=sZ[:, :], in_=psZ2[:, :], func=AF.Ln, bias=1e-6),
                  wait=[("t", t_z)])
                E("scalar", lambda h, sZ=sZ: h.activation(
                    out=sZ[:, :], in_=sZ[:, :], func=AF.Exp, scale=-1.0), inc=("a", 1))
                addz_done[idx % 2] = P.val("a")
                a_z = addz_done[idx % 2]
                E("vector", lambda h, p=p, tsl=tsl, psO=psO, sZ=sZ: h.tensor_tensor(
                    out=qp[:, p, tsl], in0=psO[:, :], in1=sZ[:, :], op=OP.mult),
                  wait=[("t", t_o), ("a", a_z)], inc=("v", 1))
                mult_done[idx % 4] = P.val("v")
        v_o_done = P.val("v")

        t_now = P.val("t")
        for c in range(8):
            E("gpsimd", lambda h, c=c: h.dma_start(out=kp[:, c, :],
                                                   in_=woutT[c * 128:(c + 1) * 128, :]),
              wait=[("t", t_now), ("v", v_o_done)] if c == 0 else None, inc=("w", 16))
            E("gpsimd", lambda h, c=c: h.dma_start(out=kp[:, 8 + c, :],
                                                   in_=cbT[c * 128:(c + 1) * 128, :]),
              inc=("w", 16))
        w_epi = P.val("w")
        epi_free = None
        epi_v = None
        for tt in range(NTT):
            tsl = slice(tt * 512, (tt + 1) * 512)
            st_now = P.val("st")
            for c in range(8):
                w0 = [("st", st_now), ("a", a_q_done)] if c == 0 else None
                E("sync", lambda h, c=c, s=s, tsl=tsl: h.dma_start(
                    out=xtile[:, c, :], in_=xT[s, c * 128:(c + 1) * 128, tsl]),
                  wait=w0, inc=("ld", 16))
            ld_x2 = P.val("ld")
            for m in range(8):
                for c in range(8):
                    w0 = None
                    if c == 0:
                        w0 = [("w", w_epi)]
                        if epi_free is not None:
                            w0.append(("a", epi_free))
                    E("tensor", lambda h, c=c, m=m, tsl=tsl: h.matmul(
                        psQ1[:, :], kp[:, c, m * 128:(m + 1) * 128], qp[:, c, tsl],
                        start=(c == 0), stop=(c == 7)), wait=w0)
                    E("tensor", lambda h, c=c, m=m, tsl=tsl: h.matmul(
                        psQ2[:, :], kp[:, 8 + c, m * 128:(m + 1) * 128], xn[:, c, tsl],
                        start=(c == 0), stop=(c == 7)),
                        inc=("t", 1) if c == 7 else None)
                t_epi = P.val("t")
                w_ea = [("t", t_epi), ("v", v_scb)]
                if epi_v is not None:
                    w_ea.append(("v", epi_v))
                E("scalar", lambda h, m=m: h.activation(out=ea[:, :], in_=psQ1[:, :], func=AF.Identity,
                                                        bias=boutS[:, m:m + 1]),
                  wait=w_ea, inc=("a", 1))
                E("scalar", lambda h, m=m: h.activation(out=ef[:, :], in_=psQ2[:, :], func=AF.Identity,
                                                        scale=scbS[:, m:m + 1], bias=fbS[:, m:m + 1]),
                  wait=[("v", v_scb)], inc=("a", 1))
                epi_free = P.val("a")
                a_e = P.val("a")
                E("vector", lambda h, m=m: h.tensor_tensor(out=xtile[:, m, :], in0=xtile[:, m, :],
                                                           in1=ea[:, :], op=OP.add),
                  wait=[("a", a_e), ("ld", ld_x2)])
                E("vector", lambda h, m=m: h.tensor_tensor(out=xtile[:, m, :], in0=xtile[:, m, :],
                                                           in1=ef[:, :], op=OP.add), inc=("v", 1))
                epi_v = P.val("v")
                v_e = P.val("v")
                E("sync", lambda h, m=m, s=s, tsl=tsl: h.dma_start(
                    out=x1T[s, m * 128:(m + 1) * 128, tsl], in_=xtile[:, m, :]),
                  wait=[("v", v_e)], inc=("st", 16))
        ln_war = P.val("t")





def emit_ln(P, nc, xT, s, xtile, xb, sqb, sD, sE, sF, sG, xn, ones128, psSt, state, W=512):
    E = P.emit
    for tt in range(T // W):
        tsl = slice(tt * W, (tt + 1) * W)
        st_now = P.val("st")
        v_now = P.val("v")
        for c in range(8):
            E("sync", lambda h, c=c, s=s, tsl=tsl: h.dma_start(
                out=xtile[:, c, :], in_=xT[s, c * 128:(c + 1) * 128, tsl]),
              wait=[("st", st_now), ("v", v_now)] if c == 0 else None, inc=("ld", 16))
        ld_x = P.val("ld")
        for c in range(8):
            w0 = [("ld", ld_x), ("t", state.get("ln_war", 0))] if c == 0 else None
            E("vector", lambda h, c=c: h.tensor_copy(out=xb[:, c, :], in_=xtile[:, c, :]), wait=w0)
            E("vector", lambda h, c=c: h.tensor_tensor(
                out=sqb[:, c, :], in0=xb[:, c, :], in1=xb[:, c, :], op=OP.mult),
              inc=("v", 1) if c == 7 else None)
        v_cast = P.val("v")
        for c in range(8):
            E("tensor", lambda h, c=c: h.matmul(psSt[:, 0:W], ones128[:, :], xb[:, c, :],
                                                start=(c == 0), stop=(c == 7)),
              wait=[("v", v_cast), ("a", state.get("ps_war", 0))] if c == 0 else None)
        for c in range(8):
            E("tensor", lambda h, c=c: h.matmul(psSt[:, 512:512 + W], ones128[:, :], sqb[:, c, :],
                                                start=(c == 0), stop=(c == 7)),
              inc=("t", 1) if c == 7 else None)
        t_stats = P.val("t")
        state["ln_war"] = t_stats
        E("vector", lambda h: h.tensor_scalar(out=sD[:, 0:W], in0=psSt[:, 0:W],
                                              scalar1=1.0 / 1024.0, scalar2=None, op0=OP.mult),
          wait=[("t", t_stats)])
        E("vector", lambda h: h.tensor_scalar(out=sG[:, 0:W], in0=psSt[:, 512:512 + W],
                                              scalar1=1.0 / 1024.0, scalar2=None, op0=OP.mult))
        E("vector", lambda h: h.tensor_tensor(out=sE[:, 0:W], in0=sD[:, 0:W], in1=sD[:, 0:W],
                                              op=OP.mult))
        E("vector", lambda h: h.tensor_tensor(out=sE[:, 0:W], in0=sG[:, 0:W], in1=sE[:, 0:W],
                                              op=OP.subtract), inc=("v", 1))
        v_var = P.val("v")
        E("scalar", lambda h: h.activation(out=sG[:, 0:W], in_=sE[:, 0:W], func=AF.Ln, bias=1e-5),
          wait=[("v", v_var)], inc=("a", 1))
        a_ln = P.val("a")
        E("scalar", lambda h: h.activation(out=sE[:, 0:W], in_=sG[:, 0:W], func=AF.Exp, scale=-0.5),
          wait=[("a", a_ln)], inc=("a", 1))
        a_rstd = P.val("a")
        E("vector", lambda h: h.tensor_tensor(out=sF[:, 0:W], in0=sD[:, 0:W], in1=sE[:, 0:W],
                                              op=OP.mult),
          wait=[("a", a_rstd)])
        for c in range(8):
            E("vector", lambda h, c=c: h.tensor_tensor(out=xtile[:, c, :], in0=xtile[:, c, :],
                                                       in1=sE[:, 0:W], op=OP.mult))
            E("vector", lambda h, c=c, tsl=tsl: h.tensor_tensor(
                out=xn[:, c, tsl], in0=xtile[:, c, :], in1=sF[:, 0:W], op=OP.subtract),
              inc=("v", 1) if c == 7 else None)
    return P.val("v")


def emit_normalize(P, ps, zps, dnb, rcb, zbb, out_ap, ones1f, t_o, eps):
    E = P.emit
    E("vector", lambda h, ps=ps, dnb=dnb: h.tensor_copy(out=dnb[0:1, :], in_=ps[64:65, :]),
      wait=[("t", t_o)], inc=("v", 1))
    v_dn = P.val("v")
    E("scalar", lambda h, dnb=dnb, rcb=rcb: h.activation(
        out=rcb[0:1, :], in_=dnb[0:1, :], func=AF.Ln, bias=eps),
      wait=[("v", v_dn)], inc=("a", 1))
    a_ln2 = P.val("a")
    E("scalar", lambda h, rcb=rcb: h.activation(
        out=rcb[0:1, :], in_=rcb[0:1, :], func=AF.Exp, scale=-1.0),
      wait=[("a", a_ln2)], inc=("a", 1))
    a_rc = P.val("a")
    E("tensor", lambda h, rcb=rcb, zps=zps: h.matmul(
        zps[0:64, :], ones1f[0:1, 0:64], rcb[0:1, :], start=True, stop=True),
      wait=[("a", a_rc)], inc=("t", 1))
    t_z = P.val("t")
    E("vector", lambda h, zbb=zbb, zps=zps: h.tensor_copy(out=zbb[:, :], in_=zps[0:64, :]),
      wait=[("t", t_z)], inc=("v", 1))
    E("vector", lambda h, ps=ps, zbb=zbb, out_ap=out_ap: h.tensor_tensor(
        out=out_ap, in0=ps[0:64, :], in1=zbb[:, :], op=OP.mult), inc=("v", 1))
    return P.val("v")


DBGP = 1

def emit_l1a(nc, P, d, nseq=2):
    xT = d["xT"]
    wqT = d["wqT"]
    wkT = d["wkT"]
    wvT = d["wvT"]
    woT = d["woT"]
    bqf = d["bqf"]
    bkf = d["bkf"]
    bvr = d["bvr"]
    bof = d["bof"]
    attnT = d["attnT"]

    xn = P.sbuf("xn", [128, NCH, T], BF16)
    wq = P.sbuf("wq", [128, NCH, 256], BF16)
    wk = P.sbuf("wk", [128, NCH, 256], BF16)
    wv = P.sbuf("wv", [128, NCH, 1024], BF16)
    qf = P.sbuf("qf", [128, T], BF16)
    kf = P.sbuf("kf", [128, T], BF16)
    vt2 = P.sbuf("vt2", [128, 16, 448], BF16)
    PT1 = P.sbuf("PT1", [128, 8192], BF16)
    PT2 = P.sbuf("PT2", [128, 8192], BF16)
    PT3 = P.sbuf("PT3", [128, 8192], BF16)
    PT4 = P.sbuf("PT4", [128, 8192], BF16)
    Obuf = P.sbuf("Obuf", [128, NCH, T], BF16)
    U = P.sbuf("U", [128, NCH, 256], F32)
    SP = P.sbuf("SP", [128, 4096], BF16)
    sD = P.sbuf("sD", [128, 512], F32)
    sE = P.sbuf("sE", [128, 512], F32)
    sF = P.sbuf("sF", [128, 512], F32)
    sG = P.sbuf("sG", [128, 512], F32)
    zbuf = P.sbuf("zbuf", [128, 512], F32)
    bvB = P.sbuf("bvB", [128, 1024], BF16)
    ones128 = P.sbuf("ones128", [128, 128], BF16)
    ones1f = P.sbuf("ones1f", [1, 128], F32)
    bqS = P.sbuf("bqS", [128, 8], F32)
    bkS = P.sbuf("bkS", [128, 8], F32)
    boS = P.sbuf("boS", [128, 8], F32)
    ident = P.sbuf("ident", [128, 128], F32)
    swapF = P.sbuf("swapF", [128, 128], F32)
    mx = P.sbuf("mx", [128, 2], F32)
    mrow = P.sbuf("mrow", [1, 128], F32)
    mrow2 = P.sbuf("mrow2", [1, 128], F32)
    cb = P.sbuf("cb", [128, 1], F32)

    xtile = U
    xb = SP[:, 0:2048].rearrange("p (a b) -> p a b", b=256)
    sqb = SP[:, 2048:4096].rearrange("p (a b) -> p a b", b=256)
    bvF = PT1[:, :].bitcast(F32)[0:1, 0:1024]

    psA = P.psum("psA", [128, 2048])
    psB = P.psum("psB", [128, 2048])
    SLOT = [psA[:, 0:1024], psA[:, 1024:2048]]
    psAVE = psB[:, 1024:1536]
    psAVO = psB[:, 1536:2048]
    psW = psB[:, 0:512]
    psTR1 = psB[0:1, 1024:1152]
    psTR2 = psB[0:1, 1152:1280]
    psCB = psB[:, 1280:1281]

    E = P.emit
    state = {}

    E("vector", lambda h: h.memset(ones128[:, :], 1.0))
    E("vector", lambda h: h.memset(ones1f[:, :], 1.0))
    E("gpsimd", lambda h: h.memset(ident[:, :], 0.0), inc=("g", 1))
    E("gpsimd", lambda h: h.affine_select(out=ident[:, :], in_=ident[:, :],
                                          compare_op=OP.not_equal, fill=1.0, base=0,
                                          pattern=[[-1, 128]], channel_multiplier=1),
      wait=[("g", P.val("g"))], inc=("g", 1))
    g_id = P.val("g")
    E("gpsimd", lambda h: h.memset(swapF[:, :], 0.0))
    E("gpsimd", lambda h: h.affine_select(out=swapF[:, :], in_=swapF[:, :],
                                          compare_op=OP.not_equal, fill=1.0, base=64,
                                          pattern=[[-1, 128]], channel_multiplier=1))
    E("gpsimd", lambda h: h.affine_select(out=swapF[:, :], in_=swapF[:, :],
                                          compare_op=OP.not_equal, fill=1.0, base=-64,
                                          pattern=[[-1, 128]], channel_multiplier=1),
      inc=("g", 1))
    g_sw = P.val("g")
    vt7 = vt2[:, :, :].rearrange("p m (k w) -> p m k w", w=64)
    E("vector", lambda h: h.memset(vt7[:, :, 0:1, :], 1.0))
    E("vector", lambda h: h.memset(vt7[:, :, 3:4, :], 1.0))
    E("vector", lambda h: h.memset(vt7[:, :, 6:7, :], 1.0), inc=("v", 1))
    E("sync", lambda h: h.dma_start(out=bqS[:, :], in_=bqf), inc=("lp", 16))
    E("sync", lambda h: h.dma_start(out=bkS[:, :], in_=bkf), inc=("lp", 16))
    E("sync", lambda h: h.dma_start(out=boS[:, :], in_=bof), inc=("lp", 16))
    E("sync", lambda h: h.dma_start(out=bvF, in_=bvr), inc=("lp", 16))
    lp = P.val("lp")
    E("vector", lambda h: h.tensor_scalar(out=bqS[:, :], in0=bqS[:, :], scalar1=0.125,
                                          scalar2=None, op0=OP.mult),
      wait=[("lp", lp)], inc=("v", 1))
    v_params = P.val("v")
    for c in range(8):
        E("gpsimd", lambda h, c=c: h.dma_start(out=wq[:, c, 0:128],
                                               in_=wqT[c * 128:(c + 1) * 128, 0:128]),
          inc=("w", 16))
        E("gpsimd", lambda h, c=c: h.dma_start(out=wk[:, c, 0:128],
                                               in_=wkT[c * 128:(c + 1) * 128, 0:128]),
          inc=("w", 16))
    w_pair = {0: P.val("w")}
    tqp_hist = {}
    tkp_hist = {}
    E("tensor", lambda h: h.matmul(psB[:, 0:512], ones1f[0:1, 0:128], bvF[:, 0:512],
                                   start=True, stop=True),
      wait=[("lp", lp), ("v", v_params)])
    E("tensor", lambda h: h.matmul(psB[:, 512:1024], ones1f[0:1, 0:128], bvF[:, 512:1024],
                                   start=True, stop=True), inc=("t", 1))
    t_bvb = P.val("t")
    E("vector", lambda h: h.tensor_copy(out=bvB[:, :], in_=psB[:, 0:1024]),
      wait=[("t", t_bvb)], inc=("v", 1))
    v_bvb = P.val("v")

    hz = {
        "slot": [0, 0],
        "t_pt": [0, 0],
        "t_avden": 0,
        "v_norm": 0,
        "v_rc": 0,
        "t_swap": 0,
        "a_epi": 0,
    }

    for s in range(nseq):
        t_wv = P.val("t")
        for c in range(8):
            E("gpsimd", lambda h, c=c: h.dma_start(out=wv[:, c, :],
                                                   in_=wvT[c * 128:(c + 1) * 128, :]),
              wait=[("t", t_wv)] if (c == 0 and s > 0) else None, inc=("w", 16))
        w_v = P.val("w")

        v_xn = emit_ln(P, nc, xT, s, xtile, xb, sqb, sD, sE, sF, sG, xn, ones128, psA, state,
                       W=256)
        for g in range(4):
            pend0 = state.pop("l1a_pend", None) or []
            while pend0:
                pend0.pop(0)()
            fsl = slice(g * 256, (g + 1) * 256)
            v_evac_prev = [None, None]
            for ts in range(16):
                ssl = slice(ts * 128, (ts + 1) * 128)
                psV = psB[:, (ts % 2) * 512:(ts % 2) * 512 + 256]
                w0 = []
                if ts == 0:
                    w0 += [("v", v_xn), ("w", w_v),
                           ("v", max(hz["v_norm"], hz["v_rc"])), ("v", v_bvb)]
                if v_evac_prev[ts % 2] is not None:
                    w0.append(("v", v_evac_prev[ts % 2]))
                for c in range(8):
                    E("tensor", lambda h, c=c, ssl=ssl, fsl=fsl, psV=psV: h.matmul(
                        psV, xn[:, c, ssl], wv[:, c, fsl], start=(c == 0), stop=(c == 7)),
                        wait=(w0 or None) if c == 0 else None,
                        inc=("t", 1) if c == 7 else None)
                t_v = P.val("t")
                w1 = [("t", t_v)]
                if ts == 0:
                    w1.append(("t", hz["t_avden"]))
                E("vector", lambda h, ts=ts, psV=psV, g=g: h.tensor_tensor(
                    out=vt2[:, ts, 64:192], in0=psV[:, 0:128],
                    in1=bvB[:, g * 256:g * 256 + 128], op=OP.add),
                  wait=w1)
                E("vector", lambda h, ts=ts, psV=psV, g=g: h.tensor_tensor(
                    out=vt2[:, ts, 256:384], in0=psV[:, 128:256],
                    in1=bvB[:, g * 256 + 128:g * 256 + 256], op=OP.add),
                  inc=("v", 1))
                v_evac_prev[ts % 2] = P.val("v")
            v_vt2 = P.val("v")

            for pp in range(2):
                p = 2 * g + pp
                vi = s * 8 + p
                sl = vi % 2
                pend_prev = state.pop("l1a_pend", None) or []
                if vi + 1 < nseq * 8:
                    pn = (vi + 1) % 8
                    sln = (vi + 1) % 2
                    for c in range(8):
                        w0 = [("t", tqp_hist.get(vi - 1, 0))] if c == 0 else None
                        E("gpsimd", lambda h, c=c, pn=pn, sln=sln: h.dma_start(
                            out=wq[:, c, sln * 128:(sln + 1) * 128],
                            in_=wqT[c * 128:(c + 1) * 128, pn * 128:(pn + 1) * 128]),
                          wait=w0, inc=("w", 16))
                    for c in range(8):
                        w0 = [("t", tkp_hist.get(vi - 1, 0))] if c == 0 else None
                        E("gpsimd", lambda h, c=c, pn=pn, sln=sln: h.dma_start(
                            out=wk[:, c, sln * 128:(sln + 1) * 128],
                            in_=wkT[c * 128:(c + 1) * 128, pn * 128:(pn + 1) * 128]),
                          wait=w0, inc=("w", 16))
                    w_pair[vi + 1] = P.val("w")
                for g4 in range(4):
                    for c in range(8):
                        w0 = None
                        if g4 == 0 and c == 0:
                            w0 = [("w", w_pair[vi]), ("a", max(hz["slot"][0], hz["slot"][1])),
                                  ("a", hz["a_epi"])]
                        E("tensor", lambda h, c=c, g4=g4, sl=sl: h.matmul(
                            psA[:, g4 * 512:(g4 + 1) * 512], wq[:, c, sl * 128:(sl + 1) * 128],
                            xn[:, c, g4 * 512:(g4 + 1) * 512],
                            start=(c == 0), stop=(c == 7)), wait=w0,
                            inc=("t", 1) if (c == 7 and g4 == 3) else None)
                    for _ in range(4):
                        if pend_prev:
                            pend_prev.pop(0)()
                while pend_prev:
                    pend_prev.pop(0)()
                t_qp = P.val("t")
                tqp_hist[vi] = t_qp
                E("vector", lambda h, p=p: h.tensor_scalar(
                    out=qf[:, :], in0=psA[:, 0:2048], scalar1=0.125,
                    scalar2=bqS[:, p:p + 1], op0=OP.mult, op1=OP.add),
                  wait=[("t", t_qp), ("v", v_params)], inc=("v", 1))
                v_qe = P.val("v")
                for g4 in range(4):
                    for c in range(8):
                        w0 = None
                        if g4 == 0 and c == 0:
                            w0 = [("v", max(v_vt2, hz["v_norm"], hz["v_rc"]))]
                        E("tensor", lambda h, c=c, g4=g4, sl=sl: h.matmul(
                            psB[:, g4 * 512:(g4 + 1) * 512], wk[:, c, sl * 128:(sl + 1) * 128],
                            xn[:, c, g4 * 512:(g4 + 1) * 512],
                            start=(c == 0), stop=(c == 7)), wait=w0,
                            inc=("t", 1) if (c == 7 and g4 == 3) else None)
                t_kp = P.val("t")
                tkp_hist[vi] = t_kp
                E("vector", lambda h, p=p: h.tensor_scalar(
                    out=kf[:, :], in0=psB[:, 0:2048], scalar1=bkS[:, p:p + 1],
                    scalar2=None, op0=OP.add),
                  wait=[("t", t_kp)], inc=("v", 1))
                v_ke = P.val("v")
                E("vector", lambda h: h.tensor_reduce(out=mx[:, 0:1], in_=qf[:, :], axis=AX.X,
                                                      op=OP.max, apply_absolute_value=True))
                E("vector", lambda h: h.tensor_reduce(out=mx[:, 1:2], in_=kf[:, :], axis=AX.X,
                                                      op=OP.max, apply_absolute_value=True),
                  inc=("v", 1))
                v_mx = P.val("v")
                E("tensor", lambda h: h.transpose(psTR1, mx[:, 0:1], ident[:, :]),
                  wait=[("v", v_mx), ("g", g_id)])
                E("tensor", lambda h: h.transpose(psTR2, mx[:, 1:2], ident[:, :]),
                  inc=("t", 1))
                t_tr = P.val("t")
                E("vector", lambda h: h.tensor_copy(out=mrow[0:1, :], in_=psTR1),
                  wait=[("t", t_tr)])
                E("vector", lambda h: h.tensor_copy(out=mrow2[0:1, :], in_=psTR2))
                E("vector", lambda h: h.tensor_reduce(out=mrow[0:1, 0:1], in_=mrow[0:1, 0:128],
                                                      axis=AX.X, op=OP.max))
                E("vector", lambda h: h.tensor_reduce(out=mrow2[0:1, 0:1], in_=mrow2[0:1, 0:128],
                                                      axis=AX.X, op=OP.max))
                E("vector", lambda h: h.tensor_tensor(out=mrow[0:1, 0:1], in0=mrow[0:1, 0:1],
                                                      in1=mrow2[0:1, 0:1], op=OP.mult))
                E("vector", lambda h: h.tensor_scalar(out=mrow[0:1, 0:1], in0=mrow[0:1, 0:1],
                                                      scalar1=-64.0, scalar2=None, op0=OP.mult),
                  inc=("v", 1))
                v_negc = P.val("v")
                E("tensor", lambda h: h.matmul(psCB, ones1f[0:1, :], mrow[0:1, 0:1],
                                               start=True, stop=True),
                  wait=[("v", v_negc)], inc=("t", 1))
                t_cb = P.val("t")
                E("vector", lambda h: h.tensor_copy(out=cb[:, :], in_=psCB),
                  wait=[("t", t_cb)], inc=("v", 1))
                v_cb = P.val("v")

                PTbufs = [(PT1, PT2), (PT3, PT4)]

                def mk_pend(qb, pp=pp, p=p):
                    PTe, PTo = PTbufs[qb % 2]
                    a14 = hz["slot"][0]
                    a15 = hz["slot"][1]
                    v_norm_gate = hz["v_norm"]
                    v_dve_gate = v_cb
                    qsl_ = slice(qb * 512, (qb + 1) * 512)
                    thunks = []

                    def quad(m, a14=a14, a15=a15, v_norm_gate=v_norm_gate,
                             v_dve_gate=v_dve_gate, PTe=PTe, PTo=PTo, qb=qb):
                        w0h0 = w0h1 = None
                        if m == 0:
                            w0h0 = [("a", a14), ("v", v_dve_gate), ("v", v_norm_gate)]
                            w0h1 = [("a", a15)]
                        E("tensor", lambda h, m=m: h.matmul(
                            psAVE[:, :], vt2[:, m, pp * 192:pp * 192 + 128],
                            PTe[:, m * 512:(m + 1) * 512],
                            start=(m == 0), stop=(m == 15)),
                          wait=w0h0)
                        E("tensor", lambda h, m=m: h.matmul(
                            psAVO[:, :], vt2[:, m, pp * 192 + 128:pp * 192 + 256],
                            PTo[:, m * 512:(m + 1) * 512],
                            start=(m == 0), stop=(m == 15)),
                          wait=w0h1, inc=("t", 1) if m == 15 else None)
                        if m == 15:
                            tv = P.val("t")
                            hz["t_pt"][qb % 2] = tv
                            hz["t_avden"] = tv

                    for m in range(16):
                        thunks.append(lambda m=m: quad(m))

                    def tail(qb=qb, p=p, qsl_=qsl_):
                        t_qd = hz["t_avden"]
                        v_ng = hz["v_norm"]
                        dnb = sD
                        E("vector", lambda h: h.tensor_copy(
                            out=dnb[0:64, :], in_=psAVE[0:64, :]),
                          wait=[("t", t_qd), ("t", hz["t_swap"]),
                                ("st", P.val("st"))])
                        E("vector", lambda h: h.tensor_copy(
                            out=dnb[64:128, :], in_=psAVO[64:128, :]), inc=("v", 1))
                        v_dnb = P.val("v")
                        E("tensor", lambda h: h.matmul(
                            psW[:, :], swapF[:, :], dnb[:, :], start=True, stop=True),
                          wait=[("v", v_dnb), ("v", hz["v_rc"]), ("g", g_sw)],
                          inc=("t", 1))
                        hz["t_swap"] = P.val("t")
                        t_sw = hz["t_swap"]
                        E("vector", lambda h: h.tensor_scalar(
                            out=zbuf[:, :], in0=psW[:, :], scalar1=1e-30, scalar2=None,
                            op0=OP.add),
                          wait=[("t", t_sw), ("v", v_ng)], inc=("v", 1))
                        hz["v_rc"] = P.val("v")
                        E("vector", lambda h: h.reciprocal(out=zbuf[:, :], in_=zbuf[:, :]),
                          inc=("v", 1))
                        v_rc2 = P.val("v")
                        E("vector", lambda h: h.tensor_tensor(
                            out=Obuf[0:64, p, qsl_], in0=psAVO[0:64, :], in1=zbuf[0:64, :],
                            op=OP.mult),
                          wait=[("t", t_qd), ("v", v_rc2)])
                        E("vector", lambda h: h.tensor_tensor(
                            out=Obuf[64:128, p, qsl_], in0=psAVE[64:128, :],
                            in1=zbuf[64:128, :], op=OP.mult), inc=("v", 1))
                        hz["v_norm"] = P.val("v")
                    thunks.append(tail)
                    return thunks

                pend = state.get("l1a_pend") or []
                for qb in range(NTT):
                    qsl = slice(qb * 512, (qb + 1) * 512)
                    a_slot = [0] * 16
                    for rr in range(8):
                        w0 = [("a", a_slot[2 * rr - 1] if rr > 0
                               else max(hz["slot"][0], hz["slot"][1]))]
                        if rr == 0 and qb == 0:
                            w0.append(("v", v_ke))
                        for ii, (hh, j) in enumerate(((0, 0), (1, 0), (0, 1), (1, 1))):
                            m = 2 * rr + j
                            base = hh * 64
                            ps = SLOT[hh]
                            E("tensor", lambda h, m=m, j=j, ps=ps, base=base, qsl=qsl:
                              h.matmul(ps[:, j * 512:(j + 1) * 512],
                                       kf[base:base + 64, m * 128:(m + 1) * 128],
                                       qf[base:base + 64, qsl], start=True, stop=True,
                                       tile_position=(base, 0)),
                              wait=w0 if ii == 0 else None,
                              inc=("t", 1) if ii == 3 else None)
                        t_sc = P.val("t")
                        for hh in (0, 1):
                            PT = PTbufs[qb % 2][hh]
                            ps = SLOT[hh]
                            w0e = [("t", t_sc)]
                            if rr == 0 and hh == 0:
                                w0e.append(("v", v_cb))
                                w0e.append(("t", hz["t_pt"][qb % 2]))
                            E("scalar", lambda h, rr=rr, ps=ps, PT=PT: h.activation(
                                out=PT[:, rr * 1024:(rr + 1) * 1024], in_=ps[:, 0:1024],
                                func=AF.Exp, bias=cb[:, 0:1]),
                              wait=w0e, inc=("a", 1))
                            a_slot[2 * rr + hh] = P.val("a")
                            hz["slot"][hh] = a_slot[2 * rr + hh]
                        for _ in range(2):
                            if pend:
                                pend.pop(0)()
                        if rr == 7:
                            while pend:
                                pend.pop(0)()
                    pend = mk_pend(qb)
                state["l1a_pend"] = pend

        pendx = state.pop("l1a_pend", None) or []
        while pendx:
            pendx.pop(0)()
        v_obuf = P.val("v")
        t_now2 = P.val("t")
        for c in range(8):
            E("gpsimd", lambda h, c=c: h.dma_start(out=wv[0:64, c, :],
                                                   in_=woT[c * 128 + 64:(c + 1) * 128, :]),
              wait=[("t", t_now2)] if c == 0 else None, inc=("w", 16))
            E("gpsimd", lambda h, c=c: h.dma_start(out=wv[64:128, c, :],
                                                   in_=woT[c * 128:c * 128 + 64, :]),
              inc=("w", 16))
        w_o = P.val("w")
        a_evac = [None, None]
        for tt in range(NTT):
            tsl = slice(tt * 512, (tt + 1) * 512)
            for m in range(8):
                mi = tt * 8 + m
                psE_ = psA[:, (mi % 2) * 512:(mi % 2) * 512 + 512]
                w0 = None
                if mi == 0:
                    w0 = [("w", w_o), ("v", v_obuf),
                          ("a", max(hz["slot"][0], hz["slot"][1]))]
                elif a_evac[mi % 2] is not None:
                    w0 = [("a", a_evac[mi % 2])]
                for c in range(8):
                    E("tensor", lambda h, c=c, m=m, tsl=tsl, psE_=psE_: h.matmul(
                        psE_, wv[:, c, m * 128:(m + 1) * 128], Obuf[:, c, tsl],
                        start=(c == 0), stop=(c == 7)), wait=w0 if c == 0 else None,
                        inc=("t", 1) if c == 7 else None)
                t_ep = P.val("t")
                w_ea = [("t", t_ep)]
                if "epi_st" in state:
                    w_ea.append(("st", state["epi_st"]))
                eb = sD if mi % 2 == 0 else sE
                E("scalar", lambda h, psE_=psE_, eb=eb, m=m: h.activation(
                    out=eb[:, :], in_=psE_, func=AF.Identity, bias=boS[:, m:m + 1]),
                  wait=w_ea, inc=("a", 1))
                a_evac[mi % 2] = P.val("a")
                a_ev = P.val("a")
                E("sync", lambda h, m=m, s=s, tsl=tsl, eb=eb: h.dma_start(
                    out=attnT[s, m * 128:(m + 1) * 128, tsl], in_=eb[:, :]),
                  wait=[("a", a_ev)], inc=("st", 16))
                state["epi_st"] = P.val("st")
        hz["a_epi"] = P.val("a")
        state["ps_war"] = hz["a_epi"]




NPAIR_F = 4
NQ = 4
QT = 512
NCK = 4





def emit_l1f(nc, P, d, nseq=2, dbg=False):
    x1T = d["x1T"]
    attnT = d["attnT"]
    wqD = d["wq"]
    wkD = d["wk"]
    wvD = d["wv"]
    wqaD = d["wqa"]
    wqbD = d["wqb"]
    wkaD = d["wka"]
    wkbD = d["wkb"]
    wg1aD = d["wg1a"]
    wg1bD = d["wg1b"]
    bg1D = d["bg1"]
    wg2aD = d["wg2a"]
    wg2bD = d["wg2b"]
    bg2D = d["bg2"]
    weD = d["we"]
    wgaD = d["wga"]
    wgbD = d["wgb"]
    nwD = d["nw"]
    woD = d["wo"]
    out2T = d["out2T"]
    dbgT = {}
    if dbg:
        for nm, shp, dt in (("q1f", [128, 4, 512], BF16), ("q2f", [128, 4, 512], BF16),
                            ("k1f", [128, 4, 512], BF16), ("k2f", [128, 4, 512], BF16),
                            ("k1tm", [128, 4, 512], BF16), ("k2tm", [128, 4, 512], BF16),
                            ("vt", [128, 4, 512], BF16), ("gk1tm", [128, 4, 512], BF16),
                            ("gk2tm", [128, 4, 512], BF16), ("rwm", [128, 4, 8], F32),
                            ("rwb", [128, 4, 512], BF16), ("sgf", [128, 4, 512], BF16),
                            ("gkc", [128, 640], BF16), ("expg", [128, 640], BF16),
                            ("netm", [128, 640], BF16), ("qcat", [128, 640], BF16),
                            ("kcat", [128, 640], BF16), ("kctm", [128, 640], BF16),
                            ("dec", [128, 5], F32), ("Am", [128, 256], BF16),
                            ("ob", [128, 4, 512], BF16), ("Sst", [128, 4, 5, 128], F32),
                            ("Sbf", [128, 5, 128], BF16)):
            dbgT[nm] = nc.dram_tensor("d_" + nm, shp, dt, kind="ExternalOutput").ap()

    xn = P.sbuf("xn", [128, 8, T], BF16)
    wq = P.sbuf("wqS", [128, 8, 512], BF16)
    wk = P.sbuf("wkS", [128, 8, 512], BF16)
    wv = P.sbuf("wvS", [128, 8, 512], BF16)
    wqa = P.sbuf("wqaS", [128, 8, 64], BF16)
    wka = P.sbuf("wkaS", [128, 8, 64], BF16)
    wga = P.sbuf("wgaS", [128, 8, 64], BF16)
    wqb = P.sbuf("wqbS", [64, 512], BF16)
    wkb = P.sbuf("wkbS", [64, 512], BF16)
    wgb = P.sbuf("wgbS", [64, 512], BF16)
    wg1a = P.sbuf("wg1aS", [128, 8, 16], BF16)
    wg2a = P.sbuf("wg2aS", [128, 8, 16], BF16)
    wg1b = P.sbuf("wg1bS", [16, 512], BF16)
    wg2b = P.sbuf("wg2bS", [16, 512], BF16)
    weS = P.sbuf("weS", [128, 8, 4], BF16)
    wo = P.sbuf("woS", [128, 4, 1024], BF16)
    bg1S = P.sbuf("bg1S", [1, 512], BF16)
    bg2S = P.sbuf("bg2S", [1, 512], BF16)
    bgF = P.sbuf("bgF", [1, 1024], F32)
    nwS = P.sbuf("nwS", [128, 1], F32)
    q1f = P.sbuf("q1f", [128, 4, QT], BF16)
    q2f = P.sbuf("q2f", [128, 4, QT], BF16)
    k1f = P.sbuf("k1f", [128, 4, QT], BF16)
    k2f = P.sbuf("k2f", [128, 4, QT], BF16)
    k1tm = P.sbuf("k1tm", [128, 4, 512], BF16)
    k2tm = P.sbuf("k2tm", [128, 4, 512], BF16)
    vt = P.sbuf("vt", [128, 4, 512], BF16)
    gk1tm = P.sbuf("gk1tm", [128, 4, 512], BF16)
    gk2tm = P.sbuf("gk2tm", [128, 4, 512], BF16)
    gf = P.sbuf("gf", [128, 4, QT], BF16)
    sgf = P.sbuf("sgf", [128, 4, QT], BF16)
    ob = P.sbuf("ob", [128, 4, QT], BF16)
    rwb = P.sbuf("rwb", [128, 4, 512], BF16)
    rwm = P.sbuf("rwm", [128, 4, 8], F32)
    xaf = P.sbuf("xaf", [64, 512], BF16)
    gaf = P.sbuf("gaf", [16, 1024], BF16)
    expg = P.sbuf("expg", [128, 640], BF16)
    negg = P.sbuf("negg", [128, 640], BF16)
    netm = P.sbuf("netm", [128, 640], BF16)
    qcat = P.sbuf("qcat", [128, 640], BF16)
    kcat = P.sbuf("kcat", [128, 640], BF16)
    kctm = P.sbuf("kctm", [128, 640], BF16)
    gkc = P.sbuf("gkc", [128, 640], BF16)
    dec = P.sbuf("dec", [128, 2, 5], F32)
    Am = P.sbuf("Am", [128, 256], BF16)
    Sst = P.sbuf("Sst", [128, 4, 5, 128], F32)
    Sbf = P.sbuf("Sbf", [128, 5, 128], BF16)
    U = P.sbuf("U", [128, 8, 512], F32)
    SP = P.sbuf("SP", [128, 8192], BF16)
    sD = P.sbuf("sD", [128, 512], F32)
    sE = P.sbuf("sE", [128, 512], F32)
    sF = P.sbuf("sF", [128, 512], F32)
    sG = P.sbuf("sG", [128, 512], F32)
    sH = P.sbuf("sH", [128, 1536], F32)
    ones128 = P.sbuf("ones128", [128, 128], BF16)
    ones1b = P.sbuf("ones1b", [1, 128], BF16)
    Lb = P.sbuf("Lb", [128, 128], BF16)
    Lf = P.sbuf("Lf", [128, 128], F32)
    identB = P.sbuf("identB", [128, 128], BF16)
    identF = P.sbuf("identF", [128, 128], F32)
    OSEL = P.sbuf("OSEL", [128, 128], BF16)
    esel = P.sbuf("esel", [4, 4, 128], F32)
    rwT = P.sbuf("rwT", [4, 512], F32)
    ot = U[:, 2, :]
    xb = SP[:, 0:4096].rearrange("p (a b) -> p a b", b=512)
    sqb = SP[:, 4096:8192].rearrange("p (a b) -> p a b", b=512)

    psA_ = P.psum("psA_", [128, 1024])
    psB_ = P.psum("psB_", [128, 1024])
    psC_ = P.psum("psC_", [128, 512])
    psD_ = P.psum("psD_", [128, 512])
    psE_ = P.psum("psE_", [128, 1024])
    psCb = psC_[:, :].bitcast(BF16)

    E = P.emit
    state = {}

    E("vector", lambda h: h.memset(ones128[:, :], 1.0))
    E("vector", lambda h: h.memset(ones1b[:, :], 1.0))
    E("vector", lambda h: h.memset(Sbf[:, :, :], 0.0))
    E("vector", lambda h: h.memset(OSEL[:, :], 0.0))
    E("vector", lambda h: h.memset(OSEL[0:64, 0:64], 1.0))
    E("vector", lambda h: h.memset(OSEL[64:128, 64:128], 1.0), inc=("v", 1))
    E("gpsimd", lambda h: h.memset(Lb[:, :], 1.0))
    E("gpsimd", lambda h: h.affine_select(out=Lb[:, :], in_=Lb[:, :], compare_op=OP.is_ge,
                                          fill=0.0, base=0, pattern=[[1, 128]],
                                          channel_multiplier=-1))
    E("gpsimd", lambda h: h.memset(Lf[:, :], 1.0))
    E("gpsimd", lambda h: h.affine_select(out=Lf[:, :], in_=Lf[:, :], compare_op=OP.is_ge,
                                          fill=0.0, base=0, pattern=[[1, 128]],
                                          channel_multiplier=-1))
    E("gpsimd", lambda h: h.memset(identB[:, :], 0.0))
    E("gpsimd", lambda h: h.affine_select(out=identB[:, :], in_=identB[:, :],
                                          compare_op=OP.not_equal, fill=1.0, base=0,
                                          pattern=[[-1, 128]], channel_multiplier=1))
    E("gpsimd", lambda h: h.memset(identF[:, :], 0.0))
    E("gpsimd", lambda h: h.affine_select(out=identF[:, :], in_=identF[:, :],
                                          compare_op=OP.not_equal, fill=1.0, base=0,
                                          pattern=[[-1, 128]], channel_multiplier=1))
    for e in range(4):
        E("gpsimd", lambda h, e=e: h.memset(esel[:, e, :], 1.0))
        E("gpsimd", lambda h, e=e: h.affine_select(out=esel[:, e, :], in_=esel[:, e, :],
                                                   compare_op=OP.is_equal, fill=0.0,
                                                   base=-e, pattern=[[0, 128]],
                                                   channel_multiplier=1),
          inc=("g", 1) if e == 3 else None)
    g_c = P.val("g")

    E("sync", lambda h: h.dma_start(out=bgF[0:1, 0:512], in_=bg1D), inc=("lp", 16))
    E("sync", lambda h: h.dma_start(out=bgF[0:1, 512:1024], in_=bg2D), inc=("lp", 16))
    E("sync", lambda h: h.dma_start(out=nwS[:, :], in_=nwD), inc=("lp", 16))
    lp = P.val("lp")
    E("vector", lambda h: h.tensor_copy(out=bg1S[0:1, :], in_=bgF[0:1, 0:512]),
      wait=[("lp", lp)])
    E("vector", lambda h: h.tensor_copy(out=bg2S[0:1, :], in_=bgF[0:1, 512:1024]),
      inc=("v", 1))
    v_params = P.val("v")
    wlist = [(wq, wqD, 8, 512), (wk, wkD, 8, 512), (wv, wvD, 8, 512),
             (wqa, wqaD, 8, 64), (wka, wkaD, 8, 64), (wga, wgaD, 8, 64),
             (wg1a, wg1aD, 8, 16), (wg2a, wg2aD, 8, 16), (weS, weD, 8, 4),
             (wo, woD, 4, 1024)]
    for sb, dr, nchk, wdt in wlist:
        for c in range(nchk):
            E("gpsimd", lambda h, sb=sb, dr=dr, c=c: h.dma_start(
                out=sb[:, c, :], in_=dr[c * 128:(c + 1) * 128, :]), inc=("w", 16))
    for sb, dr in ((wqb, wqbD), (wkb, wkbD), (wgb, wgbD), (wg1b, wg1bD), (wg2b, wg2bD)):
        E("gpsimd", lambda h, sb=sb, dr=dr: h.dma_start(out=sb[:, :], in_=dr), inc=("w", 16))
    w_all = P.val("w")

    def MM(ps, lhsT, rhs, start, stop, w0=None, inc=None, tp=None):
        E("tensor", lambda h, ps=ps, lhsT=lhsT, rhs=rhs, start=start, stop=stop, tp=tp:
          h.matmul(ps, lhsT, rhs, start=start, stop=stop,
                   **({"tile_position": tp} if tp else {})),
          wait=w0, inc=inc)

    def ps4(mi):
        return (psA_ if mi < 2 else psB_)[:, (mi % 2) * 512:(mi % 2) * 512 + 512]

    for s in range(nseq):
        v_xn = emit_ln(P, nc, x1T, s, U, xb, sqb, sD, sE, sF, sG, xn, ones128, psA_, state)
        E("vector", lambda h: h.memset(Sst[:, :, :, :], 0.0), inc=("v", 1))
        v_s0 = P.val("v")
        for q in range(NQ):
            q0 = q * QT
            xnq = lambda c: xn[:, c, q0:q0 + QT]
            pv = state.get("p1_war")
            for mi in range(4):
                for c in range(8):
                    w0 = None
                    if mi == 0 and c == 0:
                        w0 = [("v", v_xn), ("w", w_all), ("v", v_s0), ("g", g_c)]
                        if pv:
                            w0 += [("a", pv[0]), ("v", pv[1])]
                    MM(ps4(mi), wq[:, c, mi * 128:(mi + 1) * 128], xnq(c),
                       c == 0, c == 7, w0, ("t", 1) if (c == 7 and mi == 3) else None)
            t_q1 = P.val("t")
            E("scalar", lambda h: h.activation(out=q1f[:, 0:2, :].rearrange("p a b -> p (a b)"),
                                               in_=psA_[:, :], func=AF.Identity, scale=0.125),
              wait=[("t", t_q1)])
            E("scalar", lambda h: h.activation(out=q1f[:, 2:4, :].rearrange("p a b -> p (a b)"),
                                               in_=psB_[:, :], func=AF.Identity, scale=0.125),
              inc=("a", 1))
            a_q1 = P.val("a")
            for mi in range(4):
                for c in range(8):
                    w0 = [("a", a_q1)] if (mi == 0 and c == 0) else None
                    MM(ps4(mi), wk[:, c, mi * 128:(mi + 1) * 128], xnq(c),
                       c == 0, c == 7, w0, ("t", 1) if (c == 7 and mi == 3) else None)
            t_k1 = P.val("t")
            E("scalar", lambda h: h.activation(out=k1f[:, 0:2, :].rearrange("p a b -> p (a b)"),
                                               in_=psA_[:, :], func=AF.Identity),
              wait=[("t", t_k1)])
            E("scalar", lambda h: h.activation(out=k1f[:, 2:4, :].rearrange("p a b -> p (a b)"),
                                               in_=psB_[:, :], func=AF.Identity),
              inc=("a", 1))
            a_prev = P.val("a")
            for ts in range(4):
                ssl = slice(q0 + ts * 128, q0 + (ts + 1) * 128)
                for c in range(8):
                    w0 = [("a", a_prev)] if c == 0 else None
                    MM(psA_[:, 0:512], xn[:, c, ssl], wk[:, c, :], c == 0, c == 7, w0)
                    MM(psA_[:, 512:1024], xn[:, c, ssl], wv[:, c, :], c == 0, c == 7,
                       None, ("t", 1) if c == 7 else None)
                t_kv = P.val("t")
                E("scalar", lambda h, ts=ts: h.activation(out=k1tm[:, ts, :], in_=psA_[:, 0:512],
                                                          func=AF.Identity),
                  wait=[("t", t_kv)])
                E("scalar", lambda h, ts=ts: h.activation(out=vt[:, ts, :], in_=psA_[:, 512:1024],
                                                          func=AF.Identity),
                  inc=("a", 1))
                a_prev = P.val("a")
            for c in range(8):
                w0 = [("a", a_prev)] if c == 0 else None
                MM(psB_[0:64, 0:512], wqa[:, c, :], xnq(c), c == 0, c == 7, w0,
                   ("t", 1) if c == 7 else None)
            t_xa = P.val("t")
            E("scalar", lambda h: h.activation(out=xaf[:, :], in_=psB_[0:64, 0:512],
                                               func=AF.Identity),
              wait=[("t", t_xa)], inc=("a", 1))
            a_xa = P.val("a")
            for mi in range(4):
                w0 = [("a", a_xa)] if mi == 0 else None
                MM(ps4(mi), wqb[:, mi * 128:(mi + 1) * 128], xaf[:, :], True, True, w0,
                   ("t", 1) if mi == 3 else None)
            t_q2 = P.val("t")
            E("scalar", lambda h: h.activation(out=q2f[:, 0:2, :].rearrange("p a b -> p (a b)"),
                                               in_=psA_[:, :], func=AF.Identity, scale=0.125),
              wait=[("t", t_q2)])
            E("scalar", lambda h: h.activation(out=q2f[:, 2:4, :].rearrange("p a b -> p (a b)"),
                                               in_=psB_[:, :], func=AF.Identity, scale=0.125),
              inc=("a", 1))
            a_q2 = P.val("a")
            for c in range(8):
                w0 = [("a", a_q2)] if c == 0 else None
                MM(psB_[0:64, 0:512], wka[:, c, :], xnq(c), c == 0, c == 7, w0,
                   ("t", 1) if c == 7 else None)
            t_ka = P.val("t")
            E("scalar", lambda h: h.activation(out=xaf[:, :], in_=psB_[0:64, 0:512],
                                               func=AF.Identity),
              wait=[("t", t_ka)], inc=("a", 1))
            a_ka = P.val("a")
            v_k2 = None
            for ts in range(4):
                w0 = [("a", a_ka)] if ts == 0 else [("v", v_k2)]
                MM(psA_[:, 0:512], xaf[:, ts * 128:(ts + 1) * 128], wkb[:, :], True, True,
                   w0, ("t", 1))
                t_k2 = P.val("t")
                E("vector", lambda h: h.tensor_copy(out=sH[:, 0:512], in_=psA_[:, 0:512]),
                  wait=[("t", t_k2)])
                sh3 = sH[:, 0:512].rearrange("p (g w) -> p g w", w=64)
                E("vector", lambda h, sh3=sh3: h.tensor_reduce(out=sD[:, 0:8], in_=sh3,
                                                               axis=AX.X, op=OP.max))
                E("vector", lambda h, sh3=sh3: h.tensor_tensor(
                    out=sh3, in0=sh3, in1=sD[:, 0:8].unsqueeze(2).broadcast_to([128, 8, 64]),
                    op=OP.subtract), inc=("v", 1))
                v_sm = P.val("v")
                E("scalar", lambda h: h.activation(out=sH[:, 512:1024], in_=sH[:, 0:512],
                                                   func=AF.Exp),
                  wait=[("v", v_sm)], inc=("a", 1))
                a_sm = P.val("a")
                se3 = sH[:, 512:1024].rearrange("p (g w) -> p g w", w=64)
                E("vector", lambda h, se3=se3: h.tensor_reduce(out=sD[:, 8:16], in_=se3,
                                                               axis=AX.X, op=OP.add),
                  wait=[("a", a_sm)], inc=("v", 1))
                v_ks = P.val("v")
                E("vector", lambda h: h.reciprocal(out=sD[:, 16:24], in_=sD[:, 8:16]),
                  wait=[("v", v_ks)], inc=("v", 1))
                v_kr = P.val("v")
                E("vector", lambda h, se3=se3, ts=ts: h.tensor_tensor(
                    out=k2tm[:, ts, :].rearrange("p (g w) -> p g w", w=64), in0=se3,
                    in1=sD[:, 16:24].unsqueeze(2).broadcast_to([128, 8, 64]), op=OP.mult),
                  wait=[("v", v_kr)], inc=("v", 1))
                v_k2 = P.val("v")
            a_tp = None
            for ts in range(4):
                for mi in range(4):
                    w0 = None
                    if mi == 0:
                        w0 = [("v", v_k2)] if ts == 0 else []
                        if a_tp is not None:
                            w0.append(("a", a_tp))
                        w0 = w0 or None
                    E("tensor", lambda h, ts=ts, mi=mi: h.transpose(
                        psCb[:, mi * 128:(mi + 1) * 128],
                        k2tm[:, ts, mi * 128:(mi + 1) * 128], identB[:, :]),
                      wait=w0, inc=("t", 1) if mi == 3 else None)
                t_tp = P.val("t")
                E("scalar", lambda h, ts=ts: h.activation(
                    out=k2f[:, :, ts * 128:(ts + 1) * 128],
                    in_=psCb[:, 0:512].rearrange("p (a b) -> p a b", b=128),
                    func=AF.Identity),
                  wait=[("t", t_tp)], inc=("a", 1))
                a_tp = P.val("a")
            for c in range(8):
                w0 = [("a", a_tp)] if c == 0 else None
                MM(psB_[0:16, 0:512], wg1a[:, c, :], xnq(c), c == 0, c == 7, w0)
            for c in range(8):
                MM(psB_[0:16, 512:1024], wg2a[:, c, :], xnq(c), c == 0, c == 7,
                   None, ("t", 1) if c == 7 else None)
            t_ga = P.val("t")
            E("scalar", lambda h: h.activation(out=gaf[:, :], in_=psB_[0:16, :],
                                               func=AF.Identity),
              wait=[("t", t_ga)], inc=("a", 1))
            a_ga = P.val("a")
            v_gprev = None
            for gi, (wb, bgS, gkt) in enumerate(((wg1b, bg1S, gk1tm), (wg2b, bg2S, gk2tm))):
                for ts in range(4):
                    w0 = [("a", a_ga), ("v", v_params)] if (gi == 0 and ts == 0) else (
                        [("v", v_gprev)] if v_gprev else None)
                    MM(psA_[:, 0:512], ones1b[0:1, :], bgS[0:1, :], True, False, w0)
                    MM(psA_[:, 0:512], gaf[0:16, gi * 512 + ts * 128: gi * 512 + (ts + 1) * 128],
                       wb[:, :], False, True, None, ("t", 1))
                    t_z = P.val("t")
                    E("scalar", lambda h: h.activation(out=sH[:, 512:1024], in_=psA_[:, 0:512],
                                                       func=AF.Exp, scale=-1.0),
                      wait=[("t", t_z)], inc=("a", 1))
                    a_e1 = P.val("a")
                    E("scalar", lambda h: h.activation(out=sH[:, 512:1024], in_=sH[:, 512:1024],
                                                       func=AF.Ln, bias=1.0),
                      wait=[("a", a_e1)], inc=("a", 1))
                    a_lg = P.val("a")
                    E("vector", lambda h, gkt=gkt, ts=ts: h.tensor_scalar(
                        out=gkt[:, ts, :], in0=sH[:, 512:1024], scalar1=-1.0 / 16.0,
                        scalar2=None, op0=OP.mult),
                      wait=[("a", a_lg)], inc=("v", 1))
                    v_gprev = P.val("v")
            v_rw = v_gprev
            for ts in range(4):
                ssl = slice(q0 + ts * 128, q0 + (ts + 1) * 128)
                for c in range(8):
                    w0 = [("v", v_rw)] if c == 0 else None
                    MM(psC_[:, 0:4], xn[:, c, ssl], weS[:, c, :], c == 0, c == 7, w0,
                       ("t", 1) if c == 7 else None)
                t_e = P.val("t")
                E("vector", lambda h: h.tensor_copy(out=sD[:, 0:4], in_=psC_[:, 0:4]),
                  wait=[("t", t_e)], inc=("v", 1))
                v_ecp = P.val("v")
                E("vector", lambda h: h.tensor_reduce(out=sD[:, 4:5], in_=sD[:, 0:4],
                                                      axis=AX.X, op=OP.max), inc=("v", 1))
                v_em = P.val("v")
                E("vector", lambda h: h.tensor_scalar(out=sD[:, 0:4], in0=sD[:, 0:4],
                                                      scalar1=sD[:, 4:5], scalar2=None,
                                                      op0=OP.subtract),
                  wait=[("v", v_em)], inc=("v", 1))
                v_e1 = P.val("v")
                E("scalar", lambda h: h.activation(out=sD[:, 0:4], in_=sD[:, 0:4], func=AF.Exp),
                  wait=[("v", v_e1)], inc=("a", 1))
                a_e2 = P.val("a")
                E("vector", lambda h: h.tensor_reduce(out=sD[:, 4:5], in_=sD[:, 0:4],
                                                      axis=AX.X, op=OP.add),
                  wait=[("a", a_e2)], inc=("v", 1))
                v_es = P.val("v")
                E("vector", lambda h: h.reciprocal(out=sD[:, 5:6], in_=sD[:, 4:5]),
                  wait=[("v", v_es)], inc=("v", 1))
                v_er = P.val("v")
                E("vector", lambda h: h.tensor_scalar(out=sD[:, 0:4], in0=sD[:, 0:4],
                                                      scalar1=sD[:, 5:6], scalar2=None,
                                                      op0=OP.mult),
                  wait=[("v", v_er)], inc=("v", 1))
                v_en = P.val("v")
                E("vector", lambda h: h.tensor_reduce(out=sD[:, 4:5], in_=sD[:, 0:4],
                                                      axis=AX.X, op=OP.max),
                  wait=[("v", v_en)], inc=("v", 1))
                v_em2 = P.val("v")
                E("vector", lambda h, ts=ts: h.tensor_scalar(
                    out=rwm[:, ts, 4:8], in0=sD[:, 0:4], scalar1=sD[:, 4:5], scalar2=None,
                    op0=OP.is_ge),
                  wait=[("v", v_em2)], inc=("v", 1))
                v_m = P.val("v")
                E("vector", lambda h, ts=ts: h.tensor_tensor(out=rwm[:, ts, 0:4],
                                                             in0=sD[:, 0:4],
                                                             in1=rwm[:, ts, 4:8], op=OP.mult),
                  wait=[("v", v_m)], inc=("v", 1))
                v_rw = P.val("v")
                E("tensor", lambda h, ts=ts: h.transpose(
                    psC_[0:4, 128:256], rwm[:, ts, 0:4], identF[:, :]),
                  wait=[("v", v_rw)], inc=("t", 1))
                t_rt = P.val("t")
                E("vector", lambda h, ts=ts: h.tensor_copy(out=rwT[0:4, ts * 128:(ts + 1) * 128],
                                                           in_=psC_[0:4, 128:256]),
                  wait=[("t", t_rt)], inc=("v", 1))
                v_rw = P.val("v")
            v_rwb_prev = v_rw
            for ck in range(4):
                for e in range(4):
                    w0 = [("v", v_rwb_prev)] if e == 0 else None
                    MM(psC_[:, 0:512][:, e * 128:(e + 1) * 128], esel[:, e, :],
                       rwT[0:4, ck * 128:(ck + 1) * 128], True, True, w0,
                       ("t", 1) if e == 3 else None)
                t_rb = P.val("t")
                E("vector", lambda h, ck=ck: h.tensor_copy(out=rwb[:, ck, :], in_=psC_[:, 0:512]),
                  wait=[("t", t_rb)], inc=("v", 1))
                v_rwb_prev = P.val("v")
            v_rwb = P.val("v")
            for c in range(8):
                w0 = [("v", v_rwb)] if c == 0 else None
                MM(psB_[0:64, 0:512], wga[:, c, :], xnq(c), c == 0, c == 7, w0,
                   ("t", 1) if c == 7 else None)
            t_xg = P.val("t")
            E("scalar", lambda h: h.activation(out=xaf[:, :], in_=psB_[0:64, 0:512],
                                               func=AF.Identity),
              wait=[("t", t_xg)], inc=("a", 1))
            a_xg = P.val("a")
            for mi in range(4):
                w0 = [("a", a_xg)] if mi == 0 else None
                MM(ps4(mi), wgb[:, mi * 128:(mi + 1) * 128], xaf[:, :], True, True, w0,
                   ("t", 1) if mi == 3 else None)
            t_g = P.val("t")
            E("scalar", lambda h: h.activation(out=gf[:, 0:2, :].rearrange("p a b -> p (a b)"),
                                               in_=psA_[:, :], func=AF.Identity),
              wait=[("t", t_g)])
            E("scalar", lambda h: h.activation(out=gf[:, 2:4, :].rearrange("p a b -> p (a b)"),
                                               in_=psB_[:, :], func=AF.Identity),
              inc=("a", 1))
            a_gf = P.val("a")
            sHb = SP[:, 4096:6144]
            E("scalar", lambda h, sHb=sHb: h.activation(
                out=sHb, in_=gf[:, :, :].rearrange("p a b -> p (a b)"), func=AF.Tanh, scale=0.5),
              wait=[("a", a_gf)], inc=("a", 1))
            a_th = P.val("a")
            E("vector", lambda h, sHb=sHb: h.tensor_scalar(
                out=sHb, in0=sHb, scalar1=1.0, scalar2=0.5, op0=OP.add, op1=OP.mult),
              wait=[("a", a_th)], inc=("v", 1))
            v_t2 = P.val("v")
            E("vector", lambda h, sHb=sHb: h.tensor_tensor(
                out=sgf[:, :, :].rearrange("p a b -> p (a b)"),
                in0=gf[:, :, :].rearrange("p a b -> p (a b)"), in1=sHb, op=OP.mult),
              wait=[("v", v_t2)], inc=("v", 1))
            v_sg = P.val("v")

            if dbg and s == 0 and q == DQ:
                for nm, sb in (("q1f", q1f), ("q2f", q2f), ("k1f", k1f), ("k2f", k2f),
                               ("k1tm", k1tm), ("k2tm", k2tm), ("vt", vt),
                               ("gk1tm", gk1tm), ("gk2tm", gk2tm), ("rwm", rwm),
                               ("rwb", rwb), ("sgf", sgf)):
                    E("sync", lambda h, nm=nm, sb=sb: h.dma_start(out=dbgT[nm], in_=sb[:, :, :]),
                      wait=[("v", v_sg)] if nm == "q1f" else None, inc=("st", 16))
            for ck in range(NCK):
                for p in range(NPAIR_F):
                    u = state.get("l1f_u", 0)
                    psG = psA_ if u % 2 == 0 else psE_
                    psl = slice(p * 128, (p + 1) * 128)
                    cksl = slice(ck * 128, (ck + 1) * 128)
                    w0 = [("v", v_sg)] if (ck == 0 and p == 0) else None
                    E("vector", lambda h, ck=ck, psl=psl: h.tensor_copy(
                        out=gkc[:, 0:128], in_=gk1tm[:, ck, psl]),
                      wait=(w0 or []) + [("t", state.get("l1f_csg", 0))])
                    E("vector", lambda h, ck=ck, psl=psl: h.tensor_tensor(
                        out=gkc[:, 128:640].rearrange("p (e w) -> p e w", w=128),
                        in0=gk2tm[:, ck, psl].unsqueeze(1).broadcast_to([128, 4, 128]),
                        in1=rwm[:, ck, 4:8].unsqueeze(2).broadcast_to([128, 4, 128]),
                        op=OP.mult), inc=("v", 1))
                    v_gkc = P.val("v")
                    for j in range(5):
                        w0 = [("v", v_gkc)]
                        if j == 0:
                            w0.append(("a", state.get(("gfmw", id(psG)), 0)))
                            w0.append(("v", state.get(("gfmv", id(psG)), 0)))
                        MM(psG[:, 0:640][:, j * 128:(j + 1) * 128],
                           gkc[:, j * 128:(j + 1) * 128], Lb[:, :], True, True,
                           w0 if j == 0 else None, ("t", 1) if j == 4 else None)
                    t_cs = P.val("t")
                    if state.get("l1f_pend") is not None:
                        state["l1f_pend"]()
                        state["l1f_pend"] = None
                    E("scalar", lambda h, psG=psG, u=u: h.activation(
                        out=dec[:, u % 2, :].unsqueeze(2),
                        in_=psG[:, 0:640].rearrange("p (j w) -> p j w", w=128)[:, :, 127:128],
                        func=AF.Exp),
                      wait=[("t", t_cs)])
                    E("scalar", lambda h, psG=psG: h.activation(out=expg[:, :], in_=psG[:, 0:640],
                                                                func=AF.Exp))
                    E("scalar", lambda h, psG=psG: h.activation(out=negg[:, :], in_=psG[:, 0:640],
                                                                func=AF.Exp, scale=-1.0),
                      inc=("a", 1))
                    a_ex = P.val("a")
                    state[("gfmw", id(psG))] = a_ex
                    E("vector", lambda h, p=p, cksl=cksl: h.tensor_copy(
                        out=qcat[:, 0:128], in_=q1f[:, p, cksl]),
                      wait=[("a", a_ex), ("t", state.get("l1f_Aw", 0))])
                    E("vector", lambda h, p=p, ck=ck, cksl=cksl: h.tensor_tensor(
                        out=qcat[:, 128:640].rearrange("p (e w) -> p e w", w=128),
                        in0=q2f[:, p, cksl].unsqueeze(1).broadcast_to([128, 4, 128]),
                        in1=rwb[:, ck, :].rearrange("p (e w) -> p e w", w=128), op=OP.mult))
                    E("vector", lambda h: h.tensor_tensor(out=qcat[:, :], in0=qcat[:, :],
                                                          in1=expg[:, :], op=OP.mult))
                    E("vector", lambda h, p=p, cksl=cksl: h.tensor_copy(
                        out=kcat[:, 0:128], in_=k1f[:, p, cksl]))
                    E("vector", lambda h, p=p, ck=ck, cksl=cksl: h.tensor_tensor(
                        out=kcat[:, 128:640].rearrange("p (e w) -> p e w", w=128),
                        in0=k2f[:, p, cksl].unsqueeze(1).broadcast_to([128, 4, 128]),
                        in1=rwb[:, ck, :].rearrange("p (e w) -> p e w", w=128), op=OP.mult))
                    E("vector", lambda h: h.tensor_tensor(out=kcat[:, :], in0=kcat[:, :],
                                                          in1=negg[:, :], op=OP.mult))
                    E("vector", lambda h, p=p: h.tensor_copy(
                        out=Sbf[0:64, :, 0:64], in_=Sst[0:64, p, :, 0:64]),
                      wait=[("t", state.get("t_oi", 0))])
                    E("vector", lambda h, p=p: h.tensor_copy(
                        out=Sbf[64:128, :, 64:128], in_=Sst[64:128, p, :, 64:128]),
                      inc=("v", 1))
                    v_bld = P.val("v")

                    def pend(p=p, ck=ck, cksl=cksl, psl=psl, v_bld=v_bld, psG=psG, u=u):
                        psBb = psB_[:, :].bitcast(BF16)
                        for j in range(5):
                            w0 = None
                            if j == 0:
                                w0 = [("v", v_bld), ("v", state.get("l1f_tpw", 0)),
                                      ("g", g_c)]
                            E("tensor", lambda h, j=j, psBb=psBb: h.transpose(
                                psBb[:, j * 128:(j + 1) * 128],
                                kcat[:, j * 128:(j + 1) * 128], identB[:, :]),
                              wait=w0, inc=("t", 1) if j == 4 else None)
                        t_tp5 = P.val("t")
                        E("vector", lambda h, psBb=psBb: h.tensor_copy(
                            out=kctm[:, 0:640], in_=psBb[:, 0:640]),
                          wait=[("t", t_tp5)], inc=("v", 1))
                        state["l1f_tpw"] = P.val("v")
                        for j in range(5):
                            w0 = [("v", v_bld)] if j == 0 else None
                            MM(psC_[:, 0:128], kcat[0:64, j * 128:(j + 1) * 128],
                               qcat[0:64, j * 128:(j + 1) * 128], j == 0, j == 4, w0,
                               None, (0, 0))
                            MM(psD_[:, 0:128], kcat[64:128, j * 128:(j + 1) * 128],
                               qcat[64:128, j * 128:(j + 1) * 128], j == 0, j == 4, None,
                               ("t", 1) if j == 4 else None, (64, 0))
                        t_A = P.val("t")
                        E("vector", lambda h: h.tensor_tensor(
                            out=Am[:, 0:128], in0=psC_[:, 0:128], in1=Lf[:, :], op=OP.mult),
                          wait=[("t", t_A)])
                        E("vector", lambda h: h.tensor_tensor(
                            out=Am[:, 128:256], in0=psD_[:, 0:128], in1=Lf[:, :], op=OP.mult),
                          inc=("v", 1))
                        v_am = P.val("v")
                        MM(psC_[0:64, 128:256], vt[:, ck, p * 128:p * 128 + 64], Am[:, 0:128],
                           True, False, [("v", v_am)], None, (0, 0))
                        MM(psD_[64:128, 128:256], vt[:, ck, p * 128 + 64:(p + 1) * 128],
                           Am[:, 128:256], True, False, None, None, (0, 64))
                        for j in range(5):
                            MM(psC_[0:64, 128:256], Sbf[:, j, 0:64],
                               qcat[:, j * 128:(j + 1) * 128], False, j == 4, None, None,
                               (0, 0))
                            MM(psD_[64:128, 128:256], Sbf[:, j, 64:128],
                               qcat[:, j * 128:(j + 1) * 128], False, j == 4, None,
                               ("t", 1) if j == 4 else None, (0, 64))
                        t_oi = P.val("t")
                        state["t_oi"] = t_oi
                        for j in range(5):
                            w0 = [("v", state["l1f_tpw"])] if j == 0 else None
                            MM(psG[:, 0:640][:, j * 128:(j + 1) * 128],
                               kctm[:, j * 128:(j + 1) * 128], vt[:, ck, psl], True, True,
                               w0, ("t", 1) if j == 4 else None)
                        t_kv2 = P.val("t")
                        state["l1f_Aw"] = t_kv2
                        E("vector", lambda h, p=p, psG=psG: h.tensor_tensor(
                            out=Sst[:, p, :, :].rearrange("p a b -> p (a b)"),
                            in0=Sst[:, p, :, :].rearrange("p a b -> p (a b)"),
                            in1=psG[:, 0:640], op=OP.add),
                          wait=[("t", t_kv2)])
                        E("vector", lambda h, p=p, u=u: h.tensor_tensor(
                            out=Sst[:, p, :, :], in0=Sst[:, p, :, :],
                            in1=dec[:, u % 2, :].unsqueeze(2).broadcast_to([128, 5, 128]),
                            op=OP.mult), inc=("v", 1))
                        state[("gfmv", id(psG))] = P.val("v")
                        E("vector", lambda h, p=p, cksl=cksl: h.tensor_copy(
                            out=ob[0:64, p, cksl], in_=psC_[0:64, 128:256]),
                          wait=[("t", t_oi)])
                        E("vector", lambda h, p=p, cksl=cksl: h.tensor_copy(
                            out=ob[64:128, p, cksl], in_=psD_[64:128, 128:256]),
                          inc=("v", 1))
                    state["l1f_pend"] = pend
                    state["l1f_u"] = u + 1
            if state.get("l1f_pend") is not None:
                state["l1f_pend"]()
                state["l1f_pend"] = None
            v_p2 = P.val("v")

            obf = ob[:, :, :].rearrange("p a b -> p (a b)")
            E("vector", lambda h, obf=obf: h.tensor_tensor(out=SP[:, 0:2048], in0=obf, in1=obf,
                                                           op=OP.mult),
              wait=[("v", v_p2)], inc=("v", 1))
            v_o2 = P.val("v")
            for j in range(4):
                w0 = None
                if j == 0:
                    w0 = [("v", v_o2), ("a", state.get("p2_ps_war", 0))]
                MM(ps4(j), OSEL[:, :], SP[:, j * 512:(j + 1) * 512], True, True, w0,
                   ("t", 1) if j == 3 else None)
            t_rms = P.val("t")
            lnbuf = SP[:, 4096:8192].bitcast(F32)
            E("scalar", lambda h, lnbuf=lnbuf: h.activation(out=lnbuf[:, 0:1024], in_=psA_[:, :],
                                               func=AF.Ln, scale=1.0 / 64.0, bias=1e-5),
              wait=[("t", t_rms)])
            E("scalar", lambda h, lnbuf=lnbuf: h.activation(out=lnbuf[:, 1024:2048], in_=psB_[:, :],
                                               func=AF.Ln, scale=1.0 / 64.0, bias=1e-5),
              inc=("a", 1))
            a_l = P.val("a")
            E("scalar", lambda h, lnbuf=lnbuf: h.activation(out=SP[:, 2048:4096], in_=lnbuf[:, 0:2048],
                                               func=AF.Exp, scale=-0.5),
              wait=[("a", a_l)], inc=("a", 1))
            a_rstd = P.val("a")
            state["p3_ps_war"] = a_rstd
            E("vector", lambda h, obf=obf: h.tensor_tensor(out=obf, in0=obf,
                                                           in1=SP[:, 2048:4096], op=OP.mult),
              wait=[("a", a_rstd)])
            E("vector", lambda h, obf=obf: h.tensor_scalar(out=obf, in0=obf, scalar1=nwS[:, 0:1],
                                                           scalar2=None, op0=OP.mult),
              wait=[("lp", lp)])
            E("vector", lambda h, obf=obf: h.tensor_tensor(
                out=obf, in0=obf, in1=sgf[:, :, :].rearrange("p a b -> p (a b)"), op=OP.mult),
              inc=("v", 1))
            v_of = P.val("v")
            v_ep = None
            for m in range(8):
                for c in range(4):
                    w0 = None
                    if c == 0:
                        w0 = [("v", v_of)]
                        if v_ep is not None:
                            w0 = [("v", v_ep)]
                    MM(psC_[:, 0:512], wo[:, c, m * 128:(m + 1) * 128], ob[:, c, :],
                       c == 0, c == 3, w0, ("t", 1) if c == 3 else None)
                t_wo = P.val("t")
                st_now = P.val("st")
                w_x1 = [("st", st_now), ("v", state.get("ep_v", 0))]
                if m == 0:
                    w_x1.append(("v", v_of))
                E("sync", lambda h, m=m, s=s, q0=q0: h.dma_start(
                    out=U[:, 0, :], in_=x1T[s, m * 128:(m + 1) * 128, q0:q0 + QT]),
                  wait=w_x1, inc=("ld", 16))
                E("sync", lambda h, m=m, s=s, q0=q0: h.dma_start(
                    out=U[:, 1, :], in_=attnT[s, m * 128:(m + 1) * 128, q0:q0 + QT]),
                  inc=("ld", 16))
                ld_r = P.val("ld")
                E("vector", lambda h: h.tensor_copy(out=ot[:, :], in_=psC_[:, 0:512]),
                  wait=[("t", t_wo), ("ld", ld_r)], inc=("v", 1))
                v_ep = P.val("v")
                E("vector", lambda h: h.tensor_tensor(out=ot[:, :], in0=ot[:, :],
                                                      in1=U[:, 0, :], op=OP.add))
                E("vector", lambda h: h.tensor_tensor(out=ot[:, :], in0=ot[:, :],
                                                      in1=U[:, 1, :], op=OP.add),
                  inc=("v", 1))
                state["ep_v"] = P.val("v")
                v_ot = P.val("v")
                E("sync", lambda h, m=m, s=s, q0=q0: h.dma_start(
                    out=out2T[s, m * 128:(m + 1) * 128, q0:q0 + QT], in_=ot[:, :]),
                  wait=[("v", v_ot)], inc=("st", 16))
            state["p1_war"] = (P.val("a"), P.val("v"))
        state["ln_war"] = P.val("t")
        state["ps_war"] = P.val("a")



_CACHE = {}

T_ = 2048


def _program():
    if "p" in _CACHE:
        return _CACHE["p"]
    nc = bass.Bass("TRN2", target_bir_lowering=False)
    P = Plan(nc)
    F = F32
    d = {}

    def din(name, shape):
        d[name] = nc.dram_tensor(name, shape, F, kind="ExternalInput").ap()
    din("xT", [2, 1024, T_])
    din("wqkvT", [1024, 3072]); din("woutT", [1024, 1024]); din("cbT", [1024, 1024])
    din("bout", [128, 8]); din("scb", [128, 8]); din("fbias", [128, 8])
    din("wqT", [1024, 1024]); din("wkT", [1024, 1024]); din("wvT", [1024, 1024])
    din("woT", [1024, 1024]); din("bqf", [128, 8]); din("bkf", [128, 8])
    din("bvr", [1, 1024]); din("bof", [128, 8])
    din("wq", [1024, 512]); din("wk", [1024, 512]); din("wv", [1024, 512])
    din("wqa", [1024, 64]); din("wqb", [64, 512]); din("wka", [1024, 64])
    din("wkb", [64, 512]); din("wg1a", [1024, 16]); din("wg1b", [16, 512])
    din("bg1", [1, 512]); din("wg2a", [1024, 16]); din("wg2b", [16, 512])
    din("bg2", [1, 512]); din("we", [1024, 4]); din("wga", [1024, 64])
    din("wgb", [64, 512]); din("nw", [128, 1]); din("wo", [512, 1024])
    d["x1T"] = nc.dram_tensor("x1T", [2, 1024, T_], F, kind="Internal").ap()
    d["attnT"] = nc.dram_tensor("attnT", [2, 1024, T_], F, kind="Internal").ap()
    d["out2T"] = nc.dram_tensor("out2T", [2, 1024, T_], F, kind="ExternalOutput").ap()

    for s in ("ld", "lp", "w", "st", "t", "v", "a", "g"):
        P.sem(s)
    cpool = P.sbuf("cpool_root", [128, 4], F32)
    for i, cv in enumerate((1e-5, 1e-6, 1e-30, 1.0)):
        nc.const_aps.register(F32, cv, cpool[:, i:i + 1])
        P.emit("vector", lambda h, i=i, cv=cv: h.memset(cpool[:, i:i + 1], cv),
               inc=("v", 1) if i == 3 else None)

    with P.scope():
        emit_l0(nc, P, d, 2)
    P.barrier()
    d1 = dict(d)
    d1["xT"] = d["x1T"]
    with P.scope():
        emit_l1a(nc, P, d1, 2)
    P.barrier()
    with P.scope():
        emit_l1f(nc, P, d, 2)
    P.wait("sync", ("st", P.val("st")))
    P.finalize()
    _CACHE["p"] = nc
    return nc


def kernel(L0_attn_x, L0_attn_Wqkv, L0_attn_Wout, L0_attn_bout, L0_ffn_CB, L0_ffn_SCB,
           L0_ffn_bias, L1_attn_W_Q, L1_attn_b_Q, L1_attn_W_K, L1_attn_b_K, L1_attn_W_V,
           L1_attn_b_V, L1_attn_W_O, L1_attn_b_O, L1_ffn_Wq, L1_ffn_Wk, L1_ffn_Wv,
           L1_ffn_Wqa, L1_ffn_Wqb, L1_ffn_Wka, L1_ffn_Wkb, L1_ffn_Wg1a, L1_ffn_Wg1b,
           L1_ffn_bg1, L1_ffn_Wg2a, L1_ffn_Wg2b, L1_ffn_bg2, L1_ffn_We, L1_ffn_Wga,
           L1_ffn_Wgb, L1_ffn_norm_w, L1_ffn_Wo):
    C = np.ascontiguousarray
    ncores = 8
    x = np.asarray(L0_attn_x)
    fm8 = lambda b: C(np.asarray(b).reshape(8, 128).T)

    nc = _program()
    w = {
        "wqkvT": C(np.asarray(L0_attn_Wqkv).T),
        "woutT": C(np.asarray(L0_attn_Wout).T),
        "cbT": C(np.asarray(L0_ffn_CB).T),
        "bout": fm8(L0_attn_bout),
        "scb": fm8(L0_ffn_SCB),
        "fbias": fm8(L0_ffn_bias),
        "wqT": C(np.asarray(L1_attn_W_Q).transpose(2, 0, 1).reshape(1024, 1024)),
        "wkT": C(np.asarray(L1_attn_W_K).transpose(2, 0, 1).reshape(1024, 1024)),
        "wvT": C(np.asarray(L1_attn_W_V).transpose(2, 0, 1).reshape(1024, 1024)),
        "woT": C(np.asarray(L1_attn_W_O).transpose(0, 2, 1).reshape(1024, 1024)),
        "bqf": fm8(np.asarray(L1_attn_b_Q).reshape(1024)),
        "bkf": fm8(np.asarray(L1_attn_b_K).reshape(1024)),
        "bvr": C(np.asarray(L1_attn_b_V).reshape(1, 1024)),
        "bof": fm8(L1_attn_b_O),
        "wq": C(np.asarray(L1_ffn_Wq)), "wk": C(np.asarray(L1_ffn_Wk)),
        "wv": C(np.asarray(L1_ffn_Wv)), "wqa": C(np.asarray(L1_ffn_Wqa)),
        "wqb": C(np.asarray(L1_ffn_Wqb)), "wka": C(np.asarray(L1_ffn_Wka)),
        "wkb": C(np.asarray(L1_ffn_Wkb)), "wg1a": C(np.asarray(L1_ffn_Wg1a)),
        "wg1b": C(np.asarray(L1_ffn_Wg1b)), "bg1": C(np.asarray(L1_ffn_bg1).reshape(1, 512)),
        "wg2a": C(np.asarray(L1_ffn_Wg2a)), "wg2b": C(np.asarray(L1_ffn_Wg2b)),
        "bg2": C(np.asarray(L1_ffn_bg2).reshape(1, 512)), "we": C(np.asarray(L1_ffn_We)),
        "wga": C(np.asarray(L1_ffn_Wga)), "wgb": C(np.asarray(L1_ffn_Wgb)),
        "nw": C(np.tile(np.asarray(L1_ffn_norm_w), 2).reshape(128, 1)),
        "wo": C(np.asarray(L1_ffn_Wo)),
    }
    in_maps = [dict(w, xT=C(x[2 * i:2 * i + 2].transpose(0, 2, 1))) for i in range(ncores)]
    res = run_bass_kernel_spmd(nc, in_maps, core_ids=list(range(ncores)))
    out = np.concatenate([np.asarray(res.results[i]["out2T"]).transpose(0, 2, 1)
                          for i in range(ncores)], axis=0)
    return out.astype(np.float32)


# version 18
# speedup vs baseline: 1.0015x; 1.0015x over previous
import numpy as np
import concourse.bass as bass
import concourse.mybir as mybir
from concourse.alu_op_type import AluOpType as OP
from contextlib import ExitStack
from concourse.bass_utils import run_bass_kernel_spmd

F32 = mybir.dt.float32
BF16 = mybir.dt.bfloat16
AF = mybir.ActivationFunctionType
AX = mybir.AxisListType


ENGS = ("sync", "tensor", "vector", "scalar", "gpsimd")


def _reg_const(nc, val, ap):
    try:
        nc.const_aps.register(F32, val, ap)
    except Exception:
        pass


class Plan:

    def __init__(self, nc):
        self.nc = nc
        self.q = {e: [] for e in ENGS}
        self.sems = {}
        self.semv = {}
        self.stack = ExitStack()
        self.root_stack = self.stack
        self.scope_id = 0
        self.fence = None

    def scope(self):
        from contextlib import contextmanager

        @contextmanager
        def _s():
            old = self.stack
            self.scope_id += 1
            self.stack = ExitStack()
            try:
                yield
            finally:
                self.stack.close()
                self.stack = old
        return _s()

    def barrier(self):
        if self.fence is None:
            old = self.stack
            self.stack = self.root_stack
            self.fence = self.sbuf("fencetile", [1, 2], F32)
            self.stack = old
        ft = self.fence
        self.emit("vector", lambda h: h.memset(ft[0:1, 0:1], 0.0), inc=("v", 1))
        self.emit("scalar", lambda h: h.activation(out=ft[0:1, 1:2], in_=ft[0:1, 0:1],
                                                   func=AF.Identity), inc=("a", 1))
        self.emit("gpsimd", lambda h: h.memset(ft[0:1, 1:2], 0.0), inc=("g", 1))
        pairs = [(s, v) for s, v in self.semv.items()]
        for e in ENGS:
            self.wait(e, *pairs)

    def sem(self, name):
        h = self.stack.enter_context(self.nc.semaphore(name))
        self.sems[name] = h
        self.semv[name] = 0
        return h

    def sbuf(self, name, shape, dtype):
        return self.stack.enter_context(
            self.nc.sbuf_tensor(f"s{self.scope_id}_{name}", shape, dtype))

    def psum(self, name, shape, dtype=F32):
        return self.stack.enter_context(
            self.nc.psum_tensor(f"s{self.scope_id}_{name}", shape, dtype))

    def emit(self, eng, fn, wait=None, inc=None):
        if inc is not None:
            self.semv[inc[0]] += inc[1]
        sems = self.sems

        def f(h):
            if wait:
                for s, v in wait:
                    h.wait_ge(sems[s], v)
            r = fn(h)
            if inc is not None:
                r.then_inc(sems[inc[0]], inc[1])
        self.q[eng].append(f)

    def wait(self, eng, *pairs):
        sems = self.sems

        def f(h):
            for s, v in pairs:
                h.wait_ge(sems[s], v)
        self.q[eng].append(f)

    def val(self, semname):
        return self.semv[semname]

    def after_all(self, semname):
        return (semname, self.semv[semname])

    def finalize(self):
        nc = self.nc
        q = self.q
        with nc.Block() as block:
            @block.sync
            def _(h):
                for f in q["sync"]:
                    f(h)

            @block.tensor
            def _(h):
                for f in q["tensor"]:
                    f(h)

            @block.vector
            def _(h):
                for f in q["vector"]:
                    f(h)

            @block.scalar
            def _(h):
                for f in q["scalar"]:
                    f(h)

            @block.gpsimd
            def _(h):
                for f in q["gpsimd"]:
                    f(h)
        self.stack.close()


T = 2048
NCH = 8
NTT = 4
NTS = 16
NPAIR = 8


def emit_l0(nc, P, d, nseq=2):
    xT = d["xT"]
    wqkvT = d["wqkvT"]
    woutT = d["woutT"]
    cbT = d["cbT"]
    bout = d["bout"]
    scb = d["scb"]
    fbias = d["fbias"]
    x1T = d["x1T"]

    xn = P.sbuf("xn", [128, NCH, T], BF16)
    wbuf = P.sbuf("wbuf", [128, NCH, 2048], BF16)
    qp = P.sbuf("qp", [128, NCH, T], BF16)
    kp = P.sbuf("kp", [128, NTS, 1024], BF16)
    vt = P.sbuf("vt", [128, NTS, 16 * 65], BF16)
    kvs = P.sbuf("kvs", [128, NPAIR * 130], BF16)
    U = P.sbuf("U", [128, NCH, 512], F32)
    SP = P.sbuf("SP", [128, 8192], BF16)
    sD = P.sbuf("sD", [128, 512], F32)
    sE = P.sbuf("sE", [128, 512], F32)
    sF = P.sbuf("sF", [128, 512], F32)
    sG = P.sbuf("sG", [128, 512], F32)
    ones128 = P.sbuf("ones128", [128, 128], BF16)
    ones1f = P.sbuf("ones1f", [1, 64], F32)
    ksumC = P.sbuf("ksumC", [128, NPAIR, 64], BF16)
    boutS = P.sbuf("boutS", [128, 8], F32)
    scbS = P.sbuf("scbS", [128, 8], F32)
    fbS = P.sbuf("fbS", [128, 8], F32)
    ea = sF
    ef = sG

    xtile = U
    sA = U[:, 0:2, :].rearrange("p a b -> p (a b)")
    xb = SP[:, 0:4096].rearrange("p (a b) -> p a b", b=512)
    sqb = SP[:, 4096:8192].rearrange("p (a b) -> p a b", b=512)
    sB = SP[:, 0:1024]
    sC = SP[:, 1024:2048]

    psBig = P.psum("psBig", [128, 2048])
    psB2 = P.psum("psB2", [128, 1024])
    psQ1 = P.psum("psQ1", [128, 512])
    psQ2 = P.psum("psQ2", [128, 512])

    E = P.emit

    E("vector", lambda h: h.memset(ones128[:, :], 1.0))
    E("vector", lambda h: h.memset(ones1f[:, :], 1.0))
    E("sync", lambda h: h.dma_start(out=boutS[:, :], in_=bout), inc=("lp", 16))
    E("sync", lambda h: h.dma_start(out=scbS[:, :], in_=scb), inc=("lp", 16))
    E("sync", lambda h: h.dma_start(out=fbS[:, :], in_=fbias), inc=("lp", 16))
    ld_params = P.val("lp")
    E("vector", lambda h: h.tensor_scalar(out=scbS[:, :], in0=scbS[:, :], scalar1=1.0 / 127.0,
                                          scalar2=None, op0=OP.mult),
      wait=[("lp", ld_params)], inc=("v", 1))
    v_scb = P.val("v")
    ln_war = 0
    vt_ones_v = None
    a_kvfree = None
    v_vt = None

    for s in range(nseq):
        t_wb = P.val("t")
        for c in range(8):
            E("gpsimd", lambda h, c=c: h.dma_start(out=wbuf[:, c, :],
                                                   in_=wqkvT[c * 128:(c + 1) * 128, 1024:3072]),
              wait=[("t", t_wb)] if (c == 0 and s > 0) else None, inc=("w", 16))
        w_kv = P.val("w")
        for tt in range(NTT):
            tsl = slice(tt * 512, (tt + 1) * 512)
            st_now = P.val("st")
            v_now = P.val("v")
            for c in range(8):
                E("sync", lambda h, c=c, s=s, tsl=tsl: h.dma_start(
                    out=xtile[:, c, :], in_=xT[s, c * 128:(c + 1) * 128, tsl]),
                  wait=[("st", st_now), ("v", v_now)] if c == 0 else None, inc=("ld", 16))
            ld_x = P.val("ld")
            for c in range(8):
                w0 = [("ld", ld_x), ("t", ln_war)] if c == 0 else None
                E("vector", lambda h, c=c: h.tensor_copy(out=xb[:, c, :], in_=xtile[:, c, :]), wait=w0)
                E("vector", lambda h, c=c: h.tensor_tensor(
                    out=sqb[:, c, :], in0=xb[:, c, :], in1=xb[:, c, :], op=OP.mult),
                  inc=("v", 1) if c == 7 else None)
            v_cast = P.val("v")
            for c in range(8):
                E("tensor", lambda h, c=c: h.matmul(psB2[:, 0:512], ones128[:, :], xb[:, c, :],
                                                    start=(c == 0), stop=(c == 7)),
                  wait=[("v", v_cast)] if c == 0 else None)
            for c in range(8):
                E("tensor", lambda h, c=c: h.matmul(psB2[:, 512:1024], ones128[:, :], sqb[:, c, :],
                                                    start=(c == 0), stop=(c == 7)),
                  inc=("t", 1) if c == 7 else None)
            t_stats = P.val("t")
            ln_war = t_stats
            E("vector", lambda h: h.tensor_scalar(out=sD[:, :], in0=psB2[:, 0:512],
                                                  scalar1=1.0 / 1024.0, scalar2=None, op0=OP.mult),
              wait=[("t", t_stats)])
            E("vector", lambda h: h.tensor_scalar(out=sG[:, :], in0=psB2[:, 512:1024],
                                                  scalar1=1.0 / 1024.0, scalar2=None, op0=OP.mult))
            E("vector", lambda h: h.tensor_tensor(out=sE[:, :], in0=sD[:, :], in1=sD[:, :], op=OP.mult))
            E("vector", lambda h: h.tensor_tensor(out=sE[:, :], in0=sG[:, :], in1=sE[:, :],
                                                  op=OP.subtract), inc=("v", 1))
            v_var = P.val("v")
            E("scalar", lambda h: h.activation(out=sG[:, :], in_=sE[:, :], func=AF.Ln, bias=1e-5),
              wait=[("v", v_var)], inc=("a", 1))
            a_ln = P.val("a")
            E("scalar", lambda h: h.activation(out=sE[:, :], in_=sG[:, :], func=AF.Exp, scale=-0.5),
              wait=[("a", a_ln)], inc=("a", 1))
            a_rstd = P.val("a")
            E("vector", lambda h: h.tensor_tensor(out=sF[:, :], in0=sD[:, :], in1=sE[:, :], op=OP.mult),
              wait=[("a", a_rstd)])
            for c in range(8):
                E("vector", lambda h, c=c: h.tensor_tensor(out=xtile[:, c, :], in0=xtile[:, c, :],
                                                           in1=sE[:, :], op=OP.mult))
                E("vector", lambda h, c=c, tsl=tsl: h.tensor_tensor(
                    out=xn[:, c, tsl], in0=xtile[:, c, :], in1=sF[:, :], op=OP.subtract),
                  inc=("v", 1) if c == 7 else None)
        v_xn = P.val("v")
        if vt_ones_v is None:
            E("vector", lambda h: h.memset(
                vt[:, :, :].rearrange("p a (h w) -> p a h w", w=65)[:, :, :, 64:65], 1.0),
              inc=("v", 1))
            vt_ones_v = P.val("v")

        kv_add_prev = v_xn
        for ts in range(NTS):
            ssl = slice(ts * 128, (ts + 1) * 128)
            for c in range(8):
                for g4 in range(4):
                    w0 = None
                    if c == 0 and g4 == 0:
                        if ts == 0:
                            w0 = [("v", v_xn), ("w", w_kv)]
                            if a_kvfree is not None:
                                w0 += [("a", a_kvfree), ("v", v_vt)]
                        else:
                            w0 = [("a", a_kvfree), ("v", v_vt)]
                    E("tensor", lambda h, c=c, g4=g4, ssl=ssl: h.matmul(
                        psBig[:, g4 * 512:(g4 + 1) * 512], xn[:, c, ssl],
                        wbuf[:, c, g4 * 512:(g4 + 1) * 512],
                        start=(c == 0), stop=(c == 7)), wait=w0,
                        inc=("t", 1) if (c == 7 and g4 == 3) else None)
            t_kv = P.val("t")
            E("scalar", lambda h: h.activation(out=sA[:, :], in_=psBig[:, 0:1024],
                                               func=AF.Relu, scale=-1.0),
              wait=[("t", t_kv), ("v", kv_add_prev)], inc=("a", 1))
            a_r1 = P.val("a")
            E("scalar", lambda h: h.activation(out=sC[:, :], in_=psBig[:, 0:1024], func=AF.Relu),
              inc=("a", 1))
            a_kvfree = P.val("a")
            E("scalar", lambda h: h.activation(out=sB[:, :], in_=sA[:, :], func=AF.Exp, scale=-1.0),
              wait=[("a", a_r1)], inc=("a", 1))
            a_exp = P.val("a")
            E("vector", lambda h, ts=ts: h.tensor_tensor(out=kp[:, ts, :], in0=sB[:, :], in1=sC[:, :],
                                                         op=OP.add),
              wait=[("a", a_exp), ("v", vt_ones_v)], inc=("v", 1))
            kv_add_prev = P.val("v")
            E("vector", lambda h, ts=ts: h.tensor_copy(
                out=vt[:, ts, :].rearrange("p (h w) -> p h w", w=65)[:, :, 0:64],
                in_=psBig[:, 1024:2048].rearrange("p (h w) -> p h w", w=64)),
              inc=("v", 1))
            v_vt = P.val("v")
        v_kv_done = P.val("v")

        t_all_kv = P.val("t")
        for c in range(8):
            E("gpsimd", lambda h, c=c: h.dma_start(out=wbuf[:, c, 0:1024],
                                                   in_=wqkvT[c * 128:(c + 1) * 128, 0:1024]),
              wait=[("t", t_all_kv)] if c == 0 else None, inc=("w", 16))
        w_q = P.val("w")
        a_start = P.val("a")
        ps_free = {id(psQ1): a_start, id(psQ2): a_start}
        q_add_prev = v_kv_done
        for m in range(8):
            for tt in range(NTT):
                ps = psQ1 if (m * NTT + tt) % 2 == 0 else psQ2
                tsl = slice(tt * 512, (tt + 1) * 512)
                for c in range(8):
                    w0 = None
                    if c == 0:
                        w0 = [("w", w_q)]
                        if ps_free[id(ps)] is not None:
                            w0.append(("a", ps_free[id(ps)]))
                    E("tensor", lambda h, c=c, m=m, tsl=tsl, ps=ps: h.matmul(
                        ps[:, :], wbuf[:, c, m * 128:(m + 1) * 128], xn[:, c, tsl],
                        start=(c == 0), stop=(c == 7)), wait=w0,
                        inc=("t", 1) if c == 7 else None)
                t_q = P.val("t")
                E("scalar", lambda h, ps=ps: h.activation(out=sA[:, 0:512], in_=ps[:, :],
                                                          func=AF.Relu, scale=-1.0),
                  wait=[("t", t_q), ("v", q_add_prev)], inc=("a", 1))
                a_r1 = P.val("a")
                E("scalar", lambda h, ps=ps: h.activation(out=sC[:, 0:512], in_=ps[:, :], func=AF.Relu),
                  inc=("a", 1))
                ps_free[id(ps)] = P.val("a")
                E("scalar", lambda h: h.activation(out=sB[:, 0:512], in_=sA[:, 0:512],
                                                   func=AF.Exp, scale=-1.0),
                  wait=[("a", a_r1)], inc=("a", 1))
                a_q = P.val("a")
                E("vector", lambda h, m=m, tsl=tsl: h.tensor_tensor(
                    out=qp[:, m, tsl], in0=sB[:, 0:512], in1=sC[:, 0:512], op=OP.add),
                  wait=[("a", a_q)], inc=("v", 1))
                q_add_prev = P.val("v")
        v_q_done = P.val("v")
        a_q_done = P.val("a")

        kvs_evac = None
        for p in range(NPAIR):
            csl = slice(p * 130, (p + 1) * 130)
            for ts in range(NTS):
                w0 = None
                if ts == 0:
                    w0 = [("v", v_kv_done)]
                    if kvs_evac is not None:
                        w0.append(("v", kvs_evac))
                E("tensor", lambda h, p=p, ts=ts, csl=csl: h.matmul(
                    psB2[:, 0:130], kp[:, ts, p * 128:(p + 1) * 128], vt[:, ts, csl],
                    start=(ts == 0), stop=(ts == NTS - 1)), wait=w0,
                    inc=("t", 1) if ts == NTS - 1 else None)
            t_p = P.val("t")
            E("vector", lambda h, csl=csl: h.tensor_copy(out=kvs[:, csl], in_=psB2[:, 0:130]),
              wait=[("t", t_p)], inc=("v", 1))
            kvs_evac = P.val("v")
        v_kvs = P.val("v")

        for p in range(NPAIR):
            E("vector", lambda h, p=p: h.tensor_copy(
                out=ksumC[:, p, :],
                in_=kvs[:, p * 130 + 64:p * 130 + 65].broadcast_to([128, 64])),
              wait=[("v", v_kvs)] if p == 0 else None, inc=("v", 1) if p == 7 else None)
        v_ksumC = P.val("v")
        mult_done = [None] * 4
        addz_done = [None] * 2
        for p in range(NPAIR):
            for tt in range(NTT):
                idx = p * NTT + tt
                tsl = slice(tt * 512, (tt + 1) * 512)
                psO = psBig[:, (idx % 4) * 512:(idx % 4) * 512 + 512]
                psZ2 = psB2[:, (idx % 2) * 512:(idx % 2) * 512 + 512]
                sZ = sD if idx % 2 == 0 else sE
                w0 = [("v", v_ksumC), ("a", a_q_done)] if idx == 0 else []
                if addz_done[idx % 2] is not None:
                    w0.append(("a", addz_done[idx % 2]))
                E("tensor", lambda h, p=p, tsl=tsl, psZ2=psZ2: h.matmul(
                    psZ2[0:64, :], ksumC[0:64, p, :], qp[0:64, p, tsl],
                    start=True, stop=True, tile_position=(0, 0)), wait=w0 or None)
                E("tensor", lambda h, p=p, tsl=tsl, psZ2=psZ2: h.matmul(
                    psZ2[64:128, :], ksumC[64:128, p, :], qp[64:128, p, tsl],
                    start=True, stop=True, tile_position=(64, 64)), inc=("t", 1))
                t_z = P.val("t")
                w1 = []
                if mult_done[idx % 4] is not None:
                    w1.append(("v", mult_done[idx % 4]))
                E("tensor", lambda h, p=p, tsl=tsl, psO=psO: h.matmul(
                    psO[0:64, :], kvs[0:64, p * 130:p * 130 + 64], qp[0:64, p, tsl],
                    start=True, stop=True, tile_position=(0, 0)), wait=w1 or None)
                E("tensor", lambda h, p=p, tsl=tsl, psO=psO: h.matmul(
                    psO[64:128, :], kvs[64:128, p * 130 + 65:p * 130 + 129], qp[64:128, p, tsl],
                    start=True, stop=True, tile_position=(64, 64)), inc=("t", 1))
                t_o = P.val("t")
                E("scalar", lambda h, psZ2=psZ2, sZ=sZ: h.activation(
                    out=sZ[:, :], in_=psZ2[:, :], func=AF.Ln, bias=1e-6),
                  wait=[("t", t_z)])
                E("scalar", lambda h, sZ=sZ: h.activation(
                    out=sZ[:, :], in_=sZ[:, :], func=AF.Exp, scale=-1.0), inc=("a", 1))
                addz_done[idx % 2] = P.val("a")
                a_z = addz_done[idx % 2]
                E("vector", lambda h, p=p, tsl=tsl, psO=psO, sZ=sZ: h.tensor_tensor(
                    out=qp[:, p, tsl], in0=psO[:, :], in1=sZ[:, :], op=OP.mult),
                  wait=[("t", t_o), ("a", a_z)], inc=("v", 1))
                mult_done[idx % 4] = P.val("v")
        v_o_done = P.val("v")

        t_now = P.val("t")
        for c in range(8):
            E("gpsimd", lambda h, c=c: h.dma_start(out=kp[:, c, :],
                                                   in_=woutT[c * 128:(c + 1) * 128, :]),
              wait=[("t", t_now), ("v", v_o_done)] if c == 0 else None, inc=("w", 16))
            E("gpsimd", lambda h, c=c: h.dma_start(out=kp[:, 8 + c, :],
                                                   in_=cbT[c * 128:(c + 1) * 128, :]),
              inc=("w", 16))
        w_epi = P.val("w")
        epi_free = None
        epi_v = None
        for tt in range(NTT):
            tsl = slice(tt * 512, (tt + 1) * 512)
            st_now = P.val("st")
            for c in range(8):
                w0 = [("st", st_now), ("a", a_q_done)] if c == 0 else None
                E("sync", lambda h, c=c, s=s, tsl=tsl: h.dma_start(
                    out=xtile[:, c, :], in_=xT[s, c * 128:(c + 1) * 128, tsl]),
                  wait=w0, inc=("ld", 16))
            ld_x2 = P.val("ld")
            for m in range(8):
                for c in range(8):
                    w0 = None
                    if c == 0:
                        w0 = [("w", w_epi)]
                        if epi_free is not None:
                            w0.append(("a", epi_free))
                    E("tensor", lambda h, c=c, m=m, tsl=tsl: h.matmul(
                        psQ1[:, :], kp[:, c, m * 128:(m + 1) * 128], qp[:, c, tsl],
                        start=(c == 0), stop=(c == 7)), wait=w0)
                    E("tensor", lambda h, c=c, m=m, tsl=tsl: h.matmul(
                        psQ2[:, :], kp[:, 8 + c, m * 128:(m + 1) * 128], xn[:, c, tsl],
                        start=(c == 0), stop=(c == 7)),
                        inc=("t", 1) if c == 7 else None)
                t_epi = P.val("t")
                w_ea = [("t", t_epi), ("v", v_scb)]
                if epi_v is not None:
                    w_ea.append(("v", epi_v))
                E("scalar", lambda h, m=m: h.activation(out=ea[:, :], in_=psQ1[:, :], func=AF.Identity,
                                                        bias=boutS[:, m:m + 1]),
                  wait=w_ea, inc=("a", 1))
                E("scalar", lambda h, m=m: h.activation(out=ef[:, :], in_=psQ2[:, :], func=AF.Identity,
                                                        scale=scbS[:, m:m + 1], bias=fbS[:, m:m + 1]),
                  wait=[("v", v_scb)], inc=("a", 1))
                epi_free = P.val("a")
                a_e = P.val("a")
                E("vector", lambda h, m=m: h.tensor_tensor(out=xtile[:, m, :], in0=xtile[:, m, :],
                                                           in1=ea[:, :], op=OP.add),
                  wait=[("a", a_e), ("ld", ld_x2)])
                E("vector", lambda h, m=m: h.tensor_tensor(out=xtile[:, m, :], in0=xtile[:, m, :],
                                                           in1=ef[:, :], op=OP.add), inc=("v", 1))
                epi_v = P.val("v")
                v_e = P.val("v")
                E("sync", lambda h, m=m, s=s, tsl=tsl: h.dma_start(
                    out=x1T[s, m * 128:(m + 1) * 128, tsl], in_=xtile[:, m, :]),
                  wait=[("v", v_e)], inc=("st", 16))
        ln_war = P.val("t")





def emit_ln(P, nc, xT, s, xtile, xb, sqb, sD, sE, sF, sG, xn, ones128, psSt, state, W=512):
    E = P.emit
    for tt in range(T // W):
        tsl = slice(tt * W, (tt + 1) * W)
        st_now = P.val("st")
        v_now = P.val("v")
        for c in range(8):
            E("sync", lambda h, c=c, s=s, tsl=tsl: h.dma_start(
                out=xtile[:, c, :], in_=xT[s, c * 128:(c + 1) * 128, tsl]),
              wait=[("st", st_now), ("v", v_now)] if c == 0 else None, inc=("ld", 16))
        ld_x = P.val("ld")
        for c in range(8):
            w0 = [("ld", ld_x), ("t", state.get("ln_war", 0))] if c == 0 else None
            E("vector", lambda h, c=c: h.tensor_copy(out=xb[:, c, :], in_=xtile[:, c, :]), wait=w0)
            E("vector", lambda h, c=c: h.tensor_tensor(
                out=sqb[:, c, :], in0=xb[:, c, :], in1=xb[:, c, :], op=OP.mult),
              inc=("v", 1) if c == 7 else None)
        v_cast = P.val("v")
        for c in range(8):
            E("tensor", lambda h, c=c: h.matmul(psSt[:, 0:W], ones128[:, :], xb[:, c, :],
                                                start=(c == 0), stop=(c == 7)),
              wait=[("v", v_cast), ("a", state.get("ps_war", 0))] if c == 0 else None)
        for c in range(8):
            E("tensor", lambda h, c=c: h.matmul(psSt[:, 512:512 + W], ones128[:, :], sqb[:, c, :],
                                                start=(c == 0), stop=(c == 7)),
              inc=("t", 1) if c == 7 else None)
        t_stats = P.val("t")
        state["ln_war"] = t_stats
        E("vector", lambda h: h.tensor_scalar(out=sD[:, 0:W], in0=psSt[:, 0:W],
                                              scalar1=1.0 / 1024.0, scalar2=None, op0=OP.mult),
          wait=[("t", t_stats)])
        E("vector", lambda h: h.tensor_scalar(out=sG[:, 0:W], in0=psSt[:, 512:512 + W],
                                              scalar1=1.0 / 1024.0, scalar2=None, op0=OP.mult))
        E("vector", lambda h: h.tensor_tensor(out=sE[:, 0:W], in0=sD[:, 0:W], in1=sD[:, 0:W],
                                              op=OP.mult))
        E("vector", lambda h: h.tensor_tensor(out=sE[:, 0:W], in0=sG[:, 0:W], in1=sE[:, 0:W],
                                              op=OP.subtract), inc=("v", 1))
        v_var = P.val("v")
        E("scalar", lambda h: h.activation(out=sG[:, 0:W], in_=sE[:, 0:W], func=AF.Ln, bias=1e-5),
          wait=[("v", v_var)], inc=("a", 1))
        a_ln = P.val("a")
        E("scalar", lambda h: h.activation(out=sE[:, 0:W], in_=sG[:, 0:W], func=AF.Exp, scale=-0.5),
          wait=[("a", a_ln)], inc=("a", 1))
        a_rstd = P.val("a")
        E("vector", lambda h: h.tensor_tensor(out=sF[:, 0:W], in0=sD[:, 0:W], in1=sE[:, 0:W],
                                              op=OP.mult),
          wait=[("a", a_rstd)])
        for c in range(8):
            E("vector", lambda h, c=c: h.tensor_tensor(out=xtile[:, c, :], in0=xtile[:, c, :],
                                                       in1=sE[:, 0:W], op=OP.mult))
            E("vector", lambda h, c=c, tsl=tsl: h.tensor_tensor(
                out=xn[:, c, tsl], in0=xtile[:, c, :], in1=sF[:, 0:W], op=OP.subtract),
              inc=("v", 1) if c == 7 else None)
    return P.val("v")


def emit_normalize(P, ps, zps, dnb, rcb, zbb, out_ap, ones1f, t_o, eps):
    E = P.emit
    E("vector", lambda h, ps=ps, dnb=dnb: h.tensor_copy(out=dnb[0:1, :], in_=ps[64:65, :]),
      wait=[("t", t_o)], inc=("v", 1))
    v_dn = P.val("v")
    E("scalar", lambda h, dnb=dnb, rcb=rcb: h.activation(
        out=rcb[0:1, :], in_=dnb[0:1, :], func=AF.Ln, bias=eps),
      wait=[("v", v_dn)], inc=("a", 1))
    a_ln2 = P.val("a")
    E("scalar", lambda h, rcb=rcb: h.activation(
        out=rcb[0:1, :], in_=rcb[0:1, :], func=AF.Exp, scale=-1.0),
      wait=[("a", a_ln2)], inc=("a", 1))
    a_rc = P.val("a")
    E("tensor", lambda h, rcb=rcb, zps=zps: h.matmul(
        zps[0:64, :], ones1f[0:1, 0:64], rcb[0:1, :], start=True, stop=True),
      wait=[("a", a_rc)], inc=("t", 1))
    t_z = P.val("t")
    E("vector", lambda h, zbb=zbb, zps=zps: h.tensor_copy(out=zbb[:, :], in_=zps[0:64, :]),
      wait=[("t", t_z)], inc=("v", 1))
    E("vector", lambda h, ps=ps, zbb=zbb, out_ap=out_ap: h.tensor_tensor(
        out=out_ap, in0=ps[0:64, :], in1=zbb[:, :], op=OP.mult), inc=("v", 1))
    return P.val("v")


DBGP = 1

def emit_l1a(nc, P, d, nseq=2):
    xT = d["xT"]
    wqT = d["wqT"]
    wkT = d["wkT"]
    wvT = d["wvT"]
    woT = d["woT"]
    bqf = d["bqf"]
    bkf = d["bkf"]
    bvr = d["bvr"]
    bof = d["bof"]
    attnT = d["attnT"]

    xn = P.sbuf("xn", [128, NCH, T], BF16)
    wq = P.sbuf("wq", [128, NCH, 256], BF16)
    wk = P.sbuf("wk", [128, NCH, 256], BF16)
    wv = P.sbuf("wv", [128, NCH, 1024], BF16)
    qf = P.sbuf("qf", [128, T], BF16)
    kf = P.sbuf("kf", [128, T], BF16)
    vt2 = P.sbuf("vt2", [128, 16, 448], BF16)
    PT1 = P.sbuf("PT1", [128, 8192], BF16)
    PT2 = P.sbuf("PT2", [128, 8192], BF16)
    PT3 = P.sbuf("PT3", [128, 8192], BF16)
    PT4 = P.sbuf("PT4", [128, 8192], BF16)
    Obuf = P.sbuf("Obuf", [128, NCH, T], BF16)
    U = P.sbuf("U", [128, NCH, 256], F32)
    SP = P.sbuf("SP", [128, 4096], BF16)
    sD = P.sbuf("sD", [128, 512], F32)
    sE = P.sbuf("sE", [128, 512], F32)
    sF = P.sbuf("sF", [128, 512], F32)
    sG = P.sbuf("sG", [128, 512], F32)
    zbuf = P.sbuf("zbuf", [128, 512], F32)
    bvB = P.sbuf("bvB", [128, 1024], BF16)
    ones128 = P.sbuf("ones128", [128, 128], BF16)
    ones1f = P.sbuf("ones1f", [1, 128], F32)
    bqS = P.sbuf("bqS", [128, 8], F32)
    bkS = P.sbuf("bkS", [128, 8], F32)
    boS = P.sbuf("boS", [128, 8], F32)
    ident = P.sbuf("ident", [128, 128], F32)
    swapF = P.sbuf("swapF", [128, 128], F32)
    mx = P.sbuf("mx", [128, 2], F32)
    mrow = P.sbuf("mrow", [1, 128], F32)
    mrow2 = P.sbuf("mrow2", [1, 128], F32)
    cb = P.sbuf("cb", [128, 1], F32)

    xtile = U
    xb = SP[:, 0:2048].rearrange("p (a b) -> p a b", b=256)
    sqb = SP[:, 2048:4096].rearrange("p (a b) -> p a b", b=256)
    bvF = PT1[:, :].bitcast(F32)[0:1, 0:1024]

    psA = P.psum("psA", [128, 2048])
    psB = P.psum("psB", [128, 2048])
    SLOT = [psA[:, 0:1024], psA[:, 1024:2048]]
    psAVE = psB[:, 1024:1536]
    psAVO = psB[:, 1536:2048]
    psW = psB[:, 0:512]
    psTR1 = psB[0:1, 1024:1152]
    psTR2 = psB[0:1, 1152:1280]
    psCB = psB[:, 1280:1281]

    E = P.emit
    state = {}

    E("vector", lambda h: h.memset(ones128[:, :], 1.0))
    E("vector", lambda h: h.memset(ones1f[:, :], 1.0))
    E("gpsimd", lambda h: h.memset(ident[:, :], 0.0), inc=("g", 1))
    E("gpsimd", lambda h: h.affine_select(out=ident[:, :], in_=ident[:, :],
                                          compare_op=OP.not_equal, fill=1.0, base=0,
                                          pattern=[[-1, 128]], channel_multiplier=1),
      wait=[("g", P.val("g"))], inc=("g", 1))
    g_id = P.val("g")
    E("gpsimd", lambda h: h.memset(swapF[:, :], 0.0))
    E("gpsimd", lambda h: h.affine_select(out=swapF[:, :], in_=swapF[:, :],
                                          compare_op=OP.not_equal, fill=1.0, base=64,
                                          pattern=[[-1, 128]], channel_multiplier=1))
    E("gpsimd", lambda h: h.affine_select(out=swapF[:, :], in_=swapF[:, :],
                                          compare_op=OP.not_equal, fill=1.0, base=-64,
                                          pattern=[[-1, 128]], channel_multiplier=1),
      inc=("g", 1))
    g_sw = P.val("g")
    vt7 = vt2[:, :, :].rearrange("p m (k w) -> p m k w", w=64)
    E("vector", lambda h: h.memset(vt7[:, :, 0:1, :], 1.0))
    E("vector", lambda h: h.memset(vt7[:, :, 3:4, :], 1.0))
    E("vector", lambda h: h.memset(vt7[:, :, 6:7, :], 1.0), inc=("v", 1))
    E("sync", lambda h: h.dma_start(out=bqS[:, :], in_=bqf), inc=("lp", 16))
    E("sync", lambda h: h.dma_start(out=bkS[:, :], in_=bkf), inc=("lp", 16))
    E("sync", lambda h: h.dma_start(out=boS[:, :], in_=bof), inc=("lp", 16))
    E("sync", lambda h: h.dma_start(out=bvF, in_=bvr), inc=("lp", 16))
    lp = P.val("lp")
    E("vector", lambda h: h.tensor_scalar(out=bqS[:, :], in0=bqS[:, :], scalar1=0.125,
                                          scalar2=None, op0=OP.mult),
      wait=[("lp", lp)], inc=("v", 1))
    v_params = P.val("v")
    for c in range(8):
        E("gpsimd", lambda h, c=c: h.dma_start(out=wq[:, c, 0:128],
                                               in_=wqT[c * 128:(c + 1) * 128, 0:128]),
          inc=("w", 16))
        E("gpsimd", lambda h, c=c: h.dma_start(out=wk[:, c, 0:128],
                                               in_=wkT[c * 128:(c + 1) * 128, 0:128]),
          inc=("w", 16))
    w_pair = {0: P.val("w")}
    tqp_hist = {}
    tkp_hist = {}
    E("tensor", lambda h: h.matmul(psB[:, 0:512], ones1f[0:1, 0:128], bvF[:, 0:512],
                                   start=True, stop=True),
      wait=[("lp", lp), ("v", v_params)])
    E("tensor", lambda h: h.matmul(psB[:, 512:1024], ones1f[0:1, 0:128], bvF[:, 512:1024],
                                   start=True, stop=True), inc=("t", 1))
    t_bvb = P.val("t")
    E("vector", lambda h: h.tensor_copy(out=bvB[:, :], in_=psB[:, 0:1024]),
      wait=[("t", t_bvb)], inc=("v", 1))
    v_bvb = P.val("v")

    hz = {
        "slot": [0, 0],
        "t_pt": [0, 0],
        "t_avden": 0,
        "v_norm": 0,
        "v_rc": 0,
        "t_swap": 0,
        "a_epi": 0,
    }

    for s in range(nseq):
        t_wv = P.val("t")
        for c in range(8):
            E("gpsimd", lambda h, c=c: h.dma_start(out=wv[:, c, :],
                                                   in_=wvT[c * 128:(c + 1) * 128, :]),
              wait=[("t", t_wv)] if (c == 0 and s > 0) else None, inc=("w", 16))
        w_v = P.val("w")

        v_xn = emit_ln(P, nc, xT, s, xtile, xb, sqb, sD, sE, sF, sG, xn, ones128, psA, state,
                       W=256)
        for g in range(4):
            tb0 = state.pop("l1a_tB", None)
            if tb0:
                tb0()
            pend0 = state.pop("l1a_pend", None) or []
            tb_g = state.pop("l1a_pend_tB", None)
            while pend0:
                pend0.pop(0)()
            if tb_g:
                tb_g()
            fsl = slice(g * 256, (g + 1) * 256)
            v_evac_prev = [None, None]
            for ts in range(16):
                ssl = slice(ts * 128, (ts + 1) * 128)
                psV = psB[:, (ts % 2) * 512:(ts % 2) * 512 + 256]
                w0 = []
                if ts == 0:
                    w0 += [("v", v_xn), ("w", w_v),
                           ("v", max(hz["v_norm"], hz["v_rc"])), ("v", v_bvb)]
                if v_evac_prev[ts % 2] is not None:
                    w0.append(("v", v_evac_prev[ts % 2]))
                for c in range(8):
                    E("tensor", lambda h, c=c, ssl=ssl, fsl=fsl, psV=psV: h.matmul(
                        psV, xn[:, c, ssl], wv[:, c, fsl], start=(c == 0), stop=(c == 7)),
                        wait=(w0 or None) if c == 0 else None,
                        inc=("t", 1) if c == 7 else None)
                t_v = P.val("t")
                w1 = [("t", t_v)]
                if ts == 0:
                    w1.append(("t", hz["t_avden"]))
                E("vector", lambda h, ts=ts, psV=psV, g=g: h.tensor_tensor(
                    out=vt2[:, ts, 64:192], in0=psV[:, 0:128],
                    in1=bvB[:, g * 256:g * 256 + 128], op=OP.add),
                  wait=w1)
                E("vector", lambda h, ts=ts, psV=psV, g=g: h.tensor_tensor(
                    out=vt2[:, ts, 256:384], in0=psV[:, 128:256],
                    in1=bvB[:, g * 256 + 128:g * 256 + 256], op=OP.add),
                  inc=("v", 1))
                v_evac_prev[ts % 2] = P.val("v")
            v_vt2 = P.val("v")

            for pp in range(2):
                p = 2 * g + pp
                vi = s * 8 + p
                sl = vi % 2
                tb0 = state.pop("l1a_tB", None)
                if tb0:
                    tb0()
                pend_prev = state.pop("l1a_pend", None) or []
                tb_prev = state.pop("l1a_pend_tB", None)
                if vi + 1 < nseq * 8:
                    pn = (vi + 1) % 8
                    sln = (vi + 1) % 2
                    for c in range(8):
                        w0 = [("t", tqp_hist.get(vi - 1, 0))] if c == 0 else None
                        E("gpsimd", lambda h, c=c, pn=pn, sln=sln: h.dma_start(
                            out=wq[:, c, sln * 128:(sln + 1) * 128],
                            in_=wqT[c * 128:(c + 1) * 128, pn * 128:(pn + 1) * 128]),
                          wait=w0, inc=("w", 16))
                    for c in range(8):
                        w0 = [("t", tkp_hist.get(vi - 1, 0))] if c == 0 else None
                        E("gpsimd", lambda h, c=c, pn=pn, sln=sln: h.dma_start(
                            out=wk[:, c, sln * 128:(sln + 1) * 128],
                            in_=wkT[c * 128:(c + 1) * 128, pn * 128:(pn + 1) * 128]),
                          wait=w0, inc=("w", 16))
                    w_pair[vi + 1] = P.val("w")
                for g4 in range(4):
                    for c in range(8):
                        w0 = None
                        if g4 == 0 and c == 0:
                            w0 = [("w", w_pair[vi]), ("a", max(hz["slot"][0], hz["slot"][1])),
                                  ("a", hz["a_epi"])]
                        E("tensor", lambda h, c=c, g4=g4, sl=sl: h.matmul(
                            psA[:, g4 * 512:(g4 + 1) * 512], wq[:, c, sl * 128:(sl + 1) * 128],
                            xn[:, c, g4 * 512:(g4 + 1) * 512],
                            start=(c == 0), stop=(c == 7)), wait=w0,
                            inc=("t", 1) if (c == 7 and g4 == 3) else None)
                    for _ in range(4):
                        if pend_prev:
                            pend_prev.pop(0)()
                while pend_prev:
                    pend_prev.pop(0)()
                if tb_prev:
                    tb_prev()
                t_qp = P.val("t")
                tqp_hist[vi] = t_qp
                E("vector", lambda h, p=p: h.tensor_scalar(
                    out=qf[:, :], in0=psA[:, 0:2048], scalar1=0.125,
                    scalar2=bqS[:, p:p + 1], op0=OP.mult, op1=OP.add),
                  wait=[("t", t_qp), ("v", v_params)], inc=("v", 1))
                v_qe = P.val("v")
                for g4 in range(4):
                    for c in range(8):
                        w0 = None
                        if g4 == 0 and c == 0:
                            w0 = [("v", max(v_vt2, hz["v_norm"], hz["v_rc"]))]
                        E("tensor", lambda h, c=c, g4=g4, sl=sl: h.matmul(
                            psB[:, g4 * 512:(g4 + 1) * 512], wk[:, c, sl * 128:(sl + 1) * 128],
                            xn[:, c, g4 * 512:(g4 + 1) * 512],
                            start=(c == 0), stop=(c == 7)), wait=w0,
                            inc=("t", 1) if (c == 7 and g4 == 3) else None)
                t_kp = P.val("t")
                tkp_hist[vi] = t_kp
                E("vector", lambda h, p=p: h.tensor_scalar(
                    out=kf[:, :], in0=psB[:, 0:2048], scalar1=bkS[:, p:p + 1],
                    scalar2=None, op0=OP.add),
                  wait=[("t", t_kp)], inc=("v", 1))
                v_ke = P.val("v")
                E("vector", lambda h: h.tensor_reduce(out=mx[:, 0:1], in_=qf[:, :], axis=AX.X,
                                                      op=OP.max, apply_absolute_value=True))
                E("vector", lambda h: h.tensor_reduce(out=mx[:, 1:2], in_=kf[:, :], axis=AX.X,
                                                      op=OP.max, apply_absolute_value=True),
                  inc=("v", 1))
                v_mx = P.val("v")
                E("tensor", lambda h: h.transpose(psTR1, mx[:, 0:1], ident[:, :]),
                  wait=[("v", v_mx), ("g", g_id)])
                E("tensor", lambda h: h.transpose(psTR2, mx[:, 1:2], ident[:, :]),
                  inc=("t", 1))
                t_tr = P.val("t")
                E("vector", lambda h: h.tensor_copy(out=mrow[0:1, :], in_=psTR1),
                  wait=[("t", t_tr)])
                E("vector", lambda h: h.tensor_copy(out=mrow2[0:1, :], in_=psTR2))
                E("vector", lambda h: h.tensor_reduce(out=mrow[0:1, 0:1], in_=mrow[0:1, 0:128],
                                                      axis=AX.X, op=OP.max))
                E("vector", lambda h: h.tensor_reduce(out=mrow2[0:1, 0:1], in_=mrow2[0:1, 0:128],
                                                      axis=AX.X, op=OP.max))
                E("vector", lambda h: h.tensor_tensor(out=mrow[0:1, 0:1], in0=mrow[0:1, 0:1],
                                                      in1=mrow2[0:1, 0:1], op=OP.mult))
                E("vector", lambda h: h.tensor_scalar(out=mrow[0:1, 0:1], in0=mrow[0:1, 0:1],
                                                      scalar1=-64.0, scalar2=None, op0=OP.mult),
                  inc=("v", 1))
                v_negc = P.val("v")
                E("tensor", lambda h: h.matmul(psCB, ones1f[0:1, :], mrow[0:1, 0:1],
                                               start=True, stop=True),
                  wait=[("v", v_negc)], inc=("t", 1))
                t_cb = P.val("t")
                E("vector", lambda h: h.tensor_copy(out=cb[:, :], in_=psCB),
                  wait=[("t", t_cb)], inc=("v", 1))
                v_cb = P.val("v")

                PTbufs = [(PT1, PT2), (PT3, PT4)]

                def mk_pend(qb, pp=pp, p=p):
                    PTe, PTo = PTbufs[qb % 2]
                    a14 = hz["slot"][0]
                    a15 = hz["slot"][1]
                    v_norm_gate = hz["v_norm"]
                    v_dve_gate = v_cb
                    qsl_ = slice(qb * 512, (qb + 1) * 512)
                    thunks = []

                    def quad(m, a14=a14, a15=a15,
                             v_dve_gate=v_dve_gate, PTe=PTe, PTo=PTo, qb=qb):
                        w0h0 = w0h1 = None
                        if m == 0:
                            w0h0 = [("a", a14), ("v", v_dve_gate), ("v", hz["v_norm"])]
                            w0h1 = [("a", a15)]
                        E("tensor", lambda h, m=m: h.matmul(
                            psAVE[:, :], vt2[:, m, pp * 192:pp * 192 + 128],
                            PTe[:, m * 512:(m + 1) * 512],
                            start=(m == 0), stop=(m == 15)),
                          wait=w0h0)
                        E("tensor", lambda h, m=m: h.matmul(
                            psAVO[:, :], vt2[:, m, pp * 192 + 128:pp * 192 + 256],
                            PTo[:, m * 512:(m + 1) * 512],
                            start=(m == 0), stop=(m == 15)),
                          wait=w0h1, inc=("t", 1) if m == 15 else None)
                        if m == 15:
                            tv = P.val("t")
                            hz["t_pt"][qb % 2] = tv
                            hz["t_avden"] = tv

                    for m in range(16):
                        thunks.append(lambda m=m: quad(m))

                    def tailA():
                        t_qd = hz["t_avden"]
                        dnb = sD
                        E("vector", lambda h: h.tensor_copy(
                            out=dnb[0:64, :], in_=psAVE[0:64, :]),
                          wait=[("t", t_qd), ("t", hz["t_swap"]),
                                ("st", P.val("st"))])
                        E("vector", lambda h: h.tensor_copy(
                            out=dnb[64:128, :], in_=psAVO[64:128, :]), inc=("v", 1))
                        v_dnb = P.val("v")
                        E("tensor", lambda h: h.matmul(
                            psW[:, :], swapF[:, :], dnb[:, :], start=True, stop=True),
                          wait=[("v", v_dnb), ("v", hz["v_rc"]), ("g", g_sw)],
                          inc=("t", 1))
                        hz["t_swap"] = P.val("t")
                    thunks.append(tailA)

                    def tailB(p=p, qsl_=qsl_):
                        t_qd = hz["t_avden"]
                        t_sw = hz["t_swap"]
                        E("vector", lambda h: h.tensor_scalar(
                            out=zbuf[:, :], in0=psW[:, :], scalar1=1e-30, scalar2=None,
                            op0=OP.add),
                          wait=[("t", t_sw), ("v", hz["v_norm"])], inc=("v", 1))
                        hz["v_rc"] = P.val("v")
                        E("vector", lambda h: h.reciprocal(out=zbuf[:, :], in_=zbuf[:, :]),
                          inc=("v", 1))
                        v_rc2 = P.val("v")
                        E("vector", lambda h: h.tensor_tensor(
                            out=Obuf[0:64, p, qsl_], in0=psAVO[0:64, :], in1=zbuf[0:64, :],
                            op=OP.mult),
                          wait=[("t", t_qd), ("v", v_rc2)])
                        E("vector", lambda h: h.tensor_tensor(
                            out=Obuf[64:128, p, qsl_], in0=psAVE[64:128, :],
                            in1=zbuf[64:128, :], op=OP.mult), inc=("v", 1))
                        hz["v_norm"] = P.val("v")
                    return thunks, tailB

                pend = state.get("l1a_pend") or []
                pend_tB = state.get("l1a_pend_tB")
                for qb in range(NTT):
                    qsl = slice(qb * 512, (qb + 1) * 512)
                    a_slot = [0] * 16
                    for rr in range(8):
                        w0 = [("a", a_slot[2 * rr - 1] if rr > 0
                               else max(hz["slot"][0], hz["slot"][1]))]
                        if rr == 0 and qb == 0:
                            w0.append(("v", v_ke))
                        for ii, (hh, j) in enumerate(((0, 0), (1, 0), (0, 1), (1, 1))):
                            m = 2 * rr + j
                            base = hh * 64
                            ps = SLOT[hh]
                            E("tensor", lambda h, m=m, j=j, ps=ps, base=base, qsl=qsl:
                              h.matmul(ps[:, j * 512:(j + 1) * 512],
                                       kf[base:base + 64, m * 128:(m + 1) * 128],
                                       qf[base:base + 64, qsl], start=True, stop=True,
                                       tile_position=(base, 0)),
                              wait=w0 if ii == 0 else None,
                              inc=("t", 1) if ii == 3 else None)
                        t_sc = P.val("t")
                        for hh in (0, 1):
                            PT = PTbufs[qb % 2][hh]
                            ps = SLOT[hh]
                            w0e = [("t", t_sc)]
                            if rr == 0 and hh == 0:
                                w0e.append(("v", v_cb))
                                w0e.append(("t", hz["t_pt"][qb % 2]))
                            E("scalar", lambda h, rr=rr, ps=ps, PT=PT: h.activation(
                                out=PT[:, rr * 1024:(rr + 1) * 1024], in_=ps[:, 0:1024],
                                func=AF.Exp, bias=cb[:, 0:1]),
                              wait=w0e, inc=("a", 1))
                            a_slot[2 * rr + hh] = P.val("a")
                            hz["slot"][hh] = a_slot[2 * rr + hh]
                        if rr == 0:
                            tb0 = state.pop("l1a_tB", None)
                            if tb0:
                                tb0()
                        for _ in range(2):
                            if pend:
                                pend.pop(0)()
                        if rr == 7:
                            while pend:
                                pend.pop(0)()
                            if pend_tB is not None:
                                state["l1a_tB"] = pend_tB
                                pend_tB = None
                    pend, pend_tB = mk_pend(qb)
                state["l1a_pend"] = pend
                state["l1a_pend_tB"] = pend_tB

        tb0 = state.pop("l1a_tB", None)
        if tb0:
            tb0()
        pendx = state.pop("l1a_pend", None) or []
        tb_x = state.pop("l1a_pend_tB", None)
        while pendx:
            pendx.pop(0)()
        if tb_x:
            tb_x()
        v_obuf = P.val("v")
        t_now2 = P.val("t")
        for c in range(8):
            E("gpsimd", lambda h, c=c: h.dma_start(out=wv[0:64, c, :],
                                                   in_=woT[c * 128 + 64:(c + 1) * 128, :]),
              wait=[("t", t_now2)] if c == 0 else None, inc=("w", 16))
            E("gpsimd", lambda h, c=c: h.dma_start(out=wv[64:128, c, :],
                                                   in_=woT[c * 128:c * 128 + 64, :]),
              inc=("w", 16))
        w_o = P.val("w")
        a_evac = [None, None]
        for tt in range(NTT):
            tsl = slice(tt * 512, (tt + 1) * 512)
            for m in range(8):
                mi = tt * 8 + m
                psE_ = psA[:, (mi % 2) * 512:(mi % 2) * 512 + 512]
                w0 = None
                if mi == 0:
                    w0 = [("w", w_o), ("v", v_obuf),
                          ("a", max(hz["slot"][0], hz["slot"][1]))]
                elif a_evac[mi % 2] is not None:
                    w0 = [("a", a_evac[mi % 2])]
                for c in range(8):
                    E("tensor", lambda h, c=c, m=m, tsl=tsl, psE_=psE_: h.matmul(
                        psE_, wv[:, c, m * 128:(m + 1) * 128], Obuf[:, c, tsl],
                        start=(c == 0), stop=(c == 7)), wait=w0 if c == 0 else None,
                        inc=("t", 1) if c == 7 else None)
                t_ep = P.val("t")
                w_ea = [("t", t_ep)]
                if "epi_st" in state:
                    w_ea.append(("st", state["epi_st"]))
                eb = sD if mi % 2 == 0 else sE
                E("scalar", lambda h, psE_=psE_, eb=eb, m=m: h.activation(
                    out=eb[:, :], in_=psE_, func=AF.Identity, bias=boS[:, m:m + 1]),
                  wait=w_ea, inc=("a", 1))
                a_evac[mi % 2] = P.val("a")
                a_ev = P.val("a")
                E("sync", lambda h, m=m, s=s, tsl=tsl, eb=eb: h.dma_start(
                    out=attnT[s, m * 128:(m + 1) * 128, tsl], in_=eb[:, :]),
                  wait=[("a", a_ev)], inc=("st", 16))
                state["epi_st"] = P.val("st")
        hz["a_epi"] = P.val("a")
        state["ps_war"] = hz["a_epi"]




NPAIR_F = 4
NQ = 4
QT = 512
NCK = 4





def emit_l1f(nc, P, d, nseq=2, dbg=False):
    x1T = d["x1T"]
    attnT = d["attnT"]
    wqD = d["wq"]
    wkD = d["wk"]
    wvD = d["wv"]
    wqaD = d["wqa"]
    wqbD = d["wqb"]
    wkaD = d["wka"]
    wkbD = d["wkb"]
    wg1aD = d["wg1a"]
    wg1bD = d["wg1b"]
    bg1D = d["bg1"]
    wg2aD = d["wg2a"]
    wg2bD = d["wg2b"]
    bg2D = d["bg2"]
    weD = d["we"]
    wgaD = d["wga"]
    wgbD = d["wgb"]
    nwD = d["nw"]
    woD = d["wo"]
    out2T = d["out2T"]
    dbgT = {}
    if dbg:
        for nm, shp, dt in (("q1f", [128, 4, 512], BF16), ("q2f", [128, 4, 512], BF16),
                            ("k1f", [128, 4, 512], BF16), ("k2f", [128, 4, 512], BF16),
                            ("k1tm", [128, 4, 512], BF16), ("k2tm", [128, 4, 512], BF16),
                            ("vt", [128, 4, 512], BF16), ("gk1tm", [128, 4, 512], BF16),
                            ("gk2tm", [128, 4, 512], BF16), ("rwm", [128, 4, 8], F32),
                            ("rwb", [128, 4, 512], BF16), ("sgf", [128, 4, 512], BF16),
                            ("gkc", [128, 640], BF16), ("expg", [128, 640], BF16),
                            ("netm", [128, 640], BF16), ("qcat", [128, 640], BF16),
                            ("kcat", [128, 640], BF16), ("kctm", [128, 640], BF16),
                            ("dec", [128, 5], F32), ("Am", [128, 256], BF16),
                            ("ob", [128, 4, 512], BF16), ("Sst", [128, 4, 5, 128], F32),
                            ("Sbf", [128, 5, 128], BF16)):
            dbgT[nm] = nc.dram_tensor("d_" + nm, shp, dt, kind="ExternalOutput").ap()

    xn = P.sbuf("xn", [128, 8, T], BF16)
    wq = P.sbuf("wqS", [128, 8, 512], BF16)
    wk = P.sbuf("wkS", [128, 8, 512], BF16)
    wv = P.sbuf("wvS", [128, 8, 512], BF16)
    wqa = P.sbuf("wqaS", [128, 8, 64], BF16)
    wka = P.sbuf("wkaS", [128, 8, 64], BF16)
    wga = P.sbuf("wgaS", [128, 8, 64], BF16)
    wqb = P.sbuf("wqbS", [64, 512], BF16)
    wkb = P.sbuf("wkbS", [64, 512], BF16)
    wgb = P.sbuf("wgbS", [64, 512], BF16)
    wg1a = P.sbuf("wg1aS", [128, 8, 16], BF16)
    wg2a = P.sbuf("wg2aS", [128, 8, 16], BF16)
    wg1b = P.sbuf("wg1bS", [16, 512], BF16)
    wg2b = P.sbuf("wg2bS", [16, 512], BF16)
    weS = P.sbuf("weS", [128, 8, 4], BF16)
    wo = P.sbuf("woS", [128, 4, 1024], BF16)
    bg1S = P.sbuf("bg1S", [1, 512], BF16)
    bg2S = P.sbuf("bg2S", [1, 512], BF16)
    bgF = P.sbuf("bgF", [1, 1024], F32)
    nwS = P.sbuf("nwS", [128, 1], F32)
    q1f = P.sbuf("q1f", [128, 4, QT], BF16)
    q2f = P.sbuf("q2f", [128, 4, QT], BF16)
    k1f = P.sbuf("k1f", [128, 4, QT], BF16)
    k2f = P.sbuf("k2f", [128, 4, QT], BF16)
    k1tm = P.sbuf("k1tm", [128, 4, 512], BF16)
    k2tm = P.sbuf("k2tm", [128, 4, 512], BF16)
    vt = P.sbuf("vt", [128, 4, 512], BF16)
    gk1tm = P.sbuf("gk1tm", [128, 4, 512], BF16)
    gk2tm = P.sbuf("gk2tm", [128, 4, 512], BF16)
    gf = P.sbuf("gf", [128, 4, QT], BF16)
    sgf = P.sbuf("sgf", [128, 4, QT], BF16)
    ob = P.sbuf("ob", [128, 4, QT], BF16)
    rwb = P.sbuf("rwb", [128, 4, 512], BF16)
    rwm = P.sbuf("rwm", [128, 4, 8], F32)
    xaf = P.sbuf("xaf", [64, 512], BF16)
    gaf = P.sbuf("gaf", [16, 1024], BF16)
    expg = P.sbuf("expg", [128, 640], BF16)
    negg = P.sbuf("negg", [128, 640], BF16)
    netm = P.sbuf("netm", [128, 640], BF16)
    qcat = P.sbuf("qcat", [128, 640], BF16)
    kcat = P.sbuf("kcat", [128, 640], BF16)
    kctm = P.sbuf("kctm", [128, 640], BF16)
    gkc = P.sbuf("gkc", [128, 640], BF16)
    dec = P.sbuf("dec", [128, 2, 5], F32)
    Am = P.sbuf("Am", [128, 256], BF16)
    Sst = P.sbuf("Sst", [128, 4, 5, 128], F32)
    Sbf = P.sbuf("Sbf", [128, 5, 128], BF16)
    U = P.sbuf("U", [128, 8, 512], F32)
    SP = P.sbuf("SP", [128, 8192], BF16)
    sD = P.sbuf("sD", [128, 512], F32)
    sE = P.sbuf("sE", [128, 512], F32)
    sF = P.sbuf("sF", [128, 512], F32)
    sG = P.sbuf("sG", [128, 512], F32)
    sH = P.sbuf("sH", [128, 1536], F32)
    ones128 = P.sbuf("ones128", [128, 128], BF16)
    ones1b = P.sbuf("ones1b", [1, 128], BF16)
    Lb = P.sbuf("Lb", [128, 128], BF16)
    Lf = P.sbuf("Lf", [128, 128], F32)
    identB = P.sbuf("identB", [128, 128], BF16)
    identF = P.sbuf("identF", [128, 128], F32)
    OSEL = P.sbuf("OSEL", [128, 128], BF16)
    esel = P.sbuf("esel", [4, 4, 128], F32)
    rwT = P.sbuf("rwT", [4, 512], F32)
    ot = U[:, 2, :]
    xb = SP[:, 0:4096].rearrange("p (a b) -> p a b", b=512)
    sqb = SP[:, 4096:8192].rearrange("p (a b) -> p a b", b=512)

    psA_ = P.psum("psA_", [128, 1024])
    psB_ = P.psum("psB_", [128, 1024])
    psC_ = P.psum("psC_", [128, 512])
    psD_ = P.psum("psD_", [128, 512])
    psE_ = P.psum("psE_", [128, 1024])
    psCb = psC_[:, :].bitcast(BF16)

    E = P.emit
    state = {}

    E("vector", lambda h: h.memset(ones128[:, :], 1.0))
    E("vector", lambda h: h.memset(ones1b[:, :], 1.0))
    E("vector", lambda h: h.memset(Sbf[:, :, :], 0.0))
    E("vector", lambda h: h.memset(OSEL[:, :], 0.0))
    E("vector", lambda h: h.memset(OSEL[0:64, 0:64], 1.0))
    E("vector", lambda h: h.memset(OSEL[64:128, 64:128], 1.0), inc=("v", 1))
    E("gpsimd", lambda h: h.memset(Lb[:, :], 1.0))
    E("gpsimd", lambda h: h.affine_select(out=Lb[:, :], in_=Lb[:, :], compare_op=OP.is_ge,
                                          fill=0.0, base=0, pattern=[[1, 128]],
                                          channel_multiplier=-1))
    E("gpsimd", lambda h: h.memset(Lf[:, :], 1.0))
    E("gpsimd", lambda h: h.affine_select(out=Lf[:, :], in_=Lf[:, :], compare_op=OP.is_ge,
                                          fill=0.0, base=0, pattern=[[1, 128]],
                                          channel_multiplier=-1))
    E("gpsimd", lambda h: h.memset(identB[:, :], 0.0))
    E("gpsimd", lambda h: h.affine_select(out=identB[:, :], in_=identB[:, :],
                                          compare_op=OP.not_equal, fill=1.0, base=0,
                                          pattern=[[-1, 128]], channel_multiplier=1))
    E("gpsimd", lambda h: h.memset(identF[:, :], 0.0))
    E("gpsimd", lambda h: h.affine_select(out=identF[:, :], in_=identF[:, :],
                                          compare_op=OP.not_equal, fill=1.0, base=0,
                                          pattern=[[-1, 128]], channel_multiplier=1))
    for e in range(4):
        E("gpsimd", lambda h, e=e: h.memset(esel[:, e, :], 1.0))
        E("gpsimd", lambda h, e=e: h.affine_select(out=esel[:, e, :], in_=esel[:, e, :],
                                                   compare_op=OP.is_equal, fill=0.0,
                                                   base=-e, pattern=[[0, 128]],
                                                   channel_multiplier=1),
          inc=("g", 1) if e == 3 else None)
    g_c = P.val("g")

    E("sync", lambda h: h.dma_start(out=bgF[0:1, 0:512], in_=bg1D), inc=("lp", 16))
    E("sync", lambda h: h.dma_start(out=bgF[0:1, 512:1024], in_=bg2D), inc=("lp", 16))
    E("sync", lambda h: h.dma_start(out=nwS[:, :], in_=nwD), inc=("lp", 16))
    lp = P.val("lp")
    E("vector", lambda h: h.tensor_copy(out=bg1S[0:1, :], in_=bgF[0:1, 0:512]),
      wait=[("lp", lp)])
    E("vector", lambda h: h.tensor_copy(out=bg2S[0:1, :], in_=bgF[0:1, 512:1024]),
      inc=("v", 1))
    v_params = P.val("v")
    wlist = [(wq, wqD, 8, 512), (wk, wkD, 8, 512), (wv, wvD, 8, 512),
             (wqa, wqaD, 8, 64), (wka, wkaD, 8, 64), (wga, wgaD, 8, 64),
             (wg1a, wg1aD, 8, 16), (wg2a, wg2aD, 8, 16), (weS, weD, 8, 4),
             (wo, woD, 4, 1024)]
    for sb, dr, nchk, wdt in wlist:
        for c in range(nchk):
            E("gpsimd", lambda h, sb=sb, dr=dr, c=c: h.dma_start(
                out=sb[:, c, :], in_=dr[c * 128:(c + 1) * 128, :]), inc=("w", 16))
    for sb, dr in ((wqb, wqbD), (wkb, wkbD), (wgb, wgbD), (wg1b, wg1bD), (wg2b, wg2bD)):
        E("gpsimd", lambda h, sb=sb, dr=dr: h.dma_start(out=sb[:, :], in_=dr), inc=("w", 16))
    w_all = P.val("w")

    def MM(ps, lhsT, rhs, start, stop, w0=None, inc=None, tp=None):
        E("tensor", lambda h, ps=ps, lhsT=lhsT, rhs=rhs, start=start, stop=stop, tp=tp:
          h.matmul(ps, lhsT, rhs, start=start, stop=stop,
                   **({"tile_position": tp} if tp else {})),
          wait=w0, inc=inc)

    def ps4(mi):
        return (psA_ if mi < 2 else psB_)[:, (mi % 2) * 512:(mi % 2) * 512 + 512]

    for s in range(nseq):
        v_xn = emit_ln(P, nc, x1T, s, U, xb, sqb, sD, sE, sF, sG, xn, ones128, psA_, state)
        E("vector", lambda h: h.memset(Sst[:, :, :, :], 0.0), inc=("v", 1))
        v_s0 = P.val("v")
        for q in range(NQ):
            q0 = q * QT
            xnq = lambda c: xn[:, c, q0:q0 + QT]
            pv = state.get("p1_war")
            for mi in range(4):
                for c in range(8):
                    w0 = None
                    if mi == 0 and c == 0:
                        w0 = [("v", v_xn), ("w", w_all), ("v", v_s0), ("g", g_c)]
                        if pv:
                            w0 += [("a", pv[0]), ("v", pv[1])]
                    MM(ps4(mi), wq[:, c, mi * 128:(mi + 1) * 128], xnq(c),
                       c == 0, c == 7, w0, ("t", 1) if (c == 7 and mi == 3) else None)
            t_q1 = P.val("t")
            E("scalar", lambda h: h.activation(out=q1f[:, 0:2, :].rearrange("p a b -> p (a b)"),
                                               in_=psA_[:, :], func=AF.Identity, scale=0.125),
              wait=[("t", t_q1)])
            E("scalar", lambda h: h.activation(out=q1f[:, 2:4, :].rearrange("p a b -> p (a b)"),
                                               in_=psB_[:, :], func=AF.Identity, scale=0.125),
              inc=("a", 1))
            a_q1 = P.val("a")
            for mi in range(4):
                for c in range(8):
                    w0 = [("a", a_q1)] if (mi == 0 and c == 0) else None
                    MM(ps4(mi), wk[:, c, mi * 128:(mi + 1) * 128], xnq(c),
                       c == 0, c == 7, w0, ("t", 1) if (c == 7 and mi == 3) else None)
            t_k1 = P.val("t")
            E("scalar", lambda h: h.activation(out=k1f[:, 0:2, :].rearrange("p a b -> p (a b)"),
                                               in_=psA_[:, :], func=AF.Identity),
              wait=[("t", t_k1)])
            E("scalar", lambda h: h.activation(out=k1f[:, 2:4, :].rearrange("p a b -> p (a b)"),
                                               in_=psB_[:, :], func=AF.Identity),
              inc=("a", 1))
            a_prev = P.val("a")
            for ts in range(4):
                ssl = slice(q0 + ts * 128, q0 + (ts + 1) * 128)
                for c in range(8):
                    w0 = [("a", a_prev)] if c == 0 else None
                    MM(psA_[:, 0:512], xn[:, c, ssl], wk[:, c, :], c == 0, c == 7, w0)
                    MM(psA_[:, 512:1024], xn[:, c, ssl], wv[:, c, :], c == 0, c == 7,
                       None, ("t", 1) if c == 7 else None)
                t_kv = P.val("t")
                E("scalar", lambda h, ts=ts: h.activation(out=k1tm[:, ts, :], in_=psA_[:, 0:512],
                                                          func=AF.Identity),
                  wait=[("t", t_kv)])
                E("scalar", lambda h, ts=ts: h.activation(out=vt[:, ts, :], in_=psA_[:, 512:1024],
                                                          func=AF.Identity),
                  inc=("a", 1))
                a_prev = P.val("a")
            for c in range(8):
                w0 = [("a", a_prev)] if c == 0 else None
                MM(psB_[0:64, 0:512], wqa[:, c, :], xnq(c), c == 0, c == 7, w0,
                   ("t", 1) if c == 7 else None)
            t_xa = P.val("t")
            E("scalar", lambda h: h.activation(out=xaf[:, :], in_=psB_[0:64, 0:512],
                                               func=AF.Identity),
              wait=[("t", t_xa)], inc=("a", 1))
            a_xa = P.val("a")
            for mi in range(4):
                w0 = [("a", a_xa)] if mi == 0 else None
                MM(ps4(mi), wqb[:, mi * 128:(mi + 1) * 128], xaf[:, :], True, True, w0,
                   ("t", 1) if mi == 3 else None)
            t_q2 = P.val("t")
            E("scalar", lambda h: h.activation(out=q2f[:, 0:2, :].rearrange("p a b -> p (a b)"),
                                               in_=psA_[:, :], func=AF.Identity, scale=0.125),
              wait=[("t", t_q2)])
            E("scalar", lambda h: h.activation(out=q2f[:, 2:4, :].rearrange("p a b -> p (a b)"),
                                               in_=psB_[:, :], func=AF.Identity, scale=0.125),
              inc=("a", 1))
            a_q2 = P.val("a")
            for c in range(8):
                w0 = [("a", a_q2)] if c == 0 else None
                MM(psB_[0:64, 0:512], wka[:, c, :], xnq(c), c == 0, c == 7, w0,
                   ("t", 1) if c == 7 else None)
            t_ka = P.val("t")
            E("scalar", lambda h: h.activation(out=xaf[:, :], in_=psB_[0:64, 0:512],
                                               func=AF.Identity),
              wait=[("t", t_ka)], inc=("a", 1))
            a_ka = P.val("a")
            v_k2 = None
            for ts in range(4):
                w0 = [("a", a_ka)] if ts == 0 else [("v", v_k2)]
                MM(psA_[:, 0:512], xaf[:, ts * 128:(ts + 1) * 128], wkb[:, :], True, True,
                   w0, ("t", 1))
                t_k2 = P.val("t")
                E("vector", lambda h: h.tensor_copy(out=sH[:, 0:512], in_=psA_[:, 0:512]),
                  wait=[("t", t_k2)])
                sh3 = sH[:, 0:512].rearrange("p (g w) -> p g w", w=64)
                E("vector", lambda h, sh3=sh3: h.tensor_reduce(out=sD[:, 0:8], in_=sh3,
                                                               axis=AX.X, op=OP.max))
                E("vector", lambda h, sh3=sh3: h.tensor_tensor(
                    out=sh3, in0=sh3, in1=sD[:, 0:8].unsqueeze(2).broadcast_to([128, 8, 64]),
                    op=OP.subtract), inc=("v", 1))
                v_sm = P.val("v")
                E("scalar", lambda h: h.activation(out=sH[:, 512:1024], in_=sH[:, 0:512],
                                                   func=AF.Exp),
                  wait=[("v", v_sm)], inc=("a", 1))
                a_sm = P.val("a")
                se3 = sH[:, 512:1024].rearrange("p (g w) -> p g w", w=64)
                E("vector", lambda h, se3=se3: h.tensor_reduce(out=sD[:, 8:16], in_=se3,
                                                               axis=AX.X, op=OP.add),
                  wait=[("a", a_sm)], inc=("v", 1))
                v_ks = P.val("v")
                E("vector", lambda h: h.reciprocal(out=sD[:, 16:24], in_=sD[:, 8:16]),
                  wait=[("v", v_ks)], inc=("v", 1))
                v_kr = P.val("v")
                E("vector", lambda h, se3=se3, ts=ts: h.tensor_tensor(
                    out=k2tm[:, ts, :].rearrange("p (g w) -> p g w", w=64), in0=se3,
                    in1=sD[:, 16:24].unsqueeze(2).broadcast_to([128, 8, 64]), op=OP.mult),
                  wait=[("v", v_kr)], inc=("v", 1))
                v_k2 = P.val("v")
            a_tp = None
            for ts in range(4):
                for mi in range(4):
                    w0 = None
                    if mi == 0:
                        w0 = [("v", v_k2)] if ts == 0 else []
                        if a_tp is not None:
                            w0.append(("a", a_tp))
                        w0 = w0 or None
                    E("tensor", lambda h, ts=ts, mi=mi: h.transpose(
                        psCb[:, mi * 128:(mi + 1) * 128],
                        k2tm[:, ts, mi * 128:(mi + 1) * 128], identB[:, :]),
                      wait=w0, inc=("t", 1) if mi == 3 else None)
                t_tp = P.val("t")
                E("scalar", lambda h, ts=ts: h.activation(
                    out=k2f[:, :, ts * 128:(ts + 1) * 128],
                    in_=psCb[:, 0:512].rearrange("p (a b) -> p a b", b=128),
                    func=AF.Identity),
                  wait=[("t", t_tp)], inc=("a", 1))
                a_tp = P.val("a")
            for c in range(8):
                w0 = [("a", a_tp)] if c == 0 else None
                MM(psB_[0:16, 0:512], wg1a[:, c, :], xnq(c), c == 0, c == 7, w0)
            for c in range(8):
                MM(psB_[0:16, 512:1024], wg2a[:, c, :], xnq(c), c == 0, c == 7,
                   None, ("t", 1) if c == 7 else None)
            t_ga = P.val("t")
            E("scalar", lambda h: h.activation(out=gaf[:, :], in_=psB_[0:16, :],
                                               func=AF.Identity),
              wait=[("t", t_ga)], inc=("a", 1))
            a_ga = P.val("a")
            v_gprev = None
            for gi, (wb, bgS, gkt) in enumerate(((wg1b, bg1S, gk1tm), (wg2b, bg2S, gk2tm))):
                for ts in range(4):
                    w0 = [("a", a_ga), ("v", v_params)] if (gi == 0 and ts == 0) else (
                        [("v", v_gprev)] if v_gprev else None)
                    MM(psA_[:, 0:512], ones1b[0:1, :], bgS[0:1, :], True, False, w0)
                    MM(psA_[:, 0:512], gaf[0:16, gi * 512 + ts * 128: gi * 512 + (ts + 1) * 128],
                       wb[:, :], False, True, None, ("t", 1))
                    t_z = P.val("t")
                    E("scalar", lambda h: h.activation(out=sH[:, 512:1024], in_=psA_[:, 0:512],
                                                       func=AF.Exp, scale=-1.0),
                      wait=[("t", t_z)], inc=("a", 1))
                    a_e1 = P.val("a")
                    E("scalar", lambda h: h.activation(out=sH[:, 512:1024], in_=sH[:, 512:1024],
                                                       func=AF.Ln, bias=1.0),
                      wait=[("a", a_e1)], inc=("a", 1))
                    a_lg = P.val("a")
                    E("vector", lambda h, gkt=gkt, ts=ts: h.tensor_scalar(
                        out=gkt[:, ts, :], in0=sH[:, 512:1024], scalar1=-1.0 / 16.0,
                        scalar2=None, op0=OP.mult),
                      wait=[("a", a_lg)], inc=("v", 1))
                    v_gprev = P.val("v")
            v_rw = v_gprev
            for ts in range(4):
                ssl = slice(q0 + ts * 128, q0 + (ts + 1) * 128)
                for c in range(8):
                    w0 = [("v", v_rw)] if c == 0 else None
                    MM(psC_[:, 0:4], xn[:, c, ssl], weS[:, c, :], c == 0, c == 7, w0,
                       ("t", 1) if c == 7 else None)
                t_e = P.val("t")
                E("vector", lambda h: h.tensor_copy(out=sD[:, 0:4], in_=psC_[:, 0:4]),
                  wait=[("t", t_e)], inc=("v", 1))
                v_ecp = P.val("v")
                E("vector", lambda h: h.tensor_reduce(out=sD[:, 4:5], in_=sD[:, 0:4],
                                                      axis=AX.X, op=OP.max), inc=("v", 1))
                v_em = P.val("v")
                E("vector", lambda h: h.tensor_scalar(out=sD[:, 0:4], in0=sD[:, 0:4],
                                                      scalar1=sD[:, 4:5], scalar2=None,
                                                      op0=OP.subtract),
                  wait=[("v", v_em)], inc=("v", 1))
                v_e1 = P.val("v")
                E("scalar", lambda h: h.activation(out=sD[:, 0:4], in_=sD[:, 0:4], func=AF.Exp),
                  wait=[("v", v_e1)], inc=("a", 1))
                a_e2 = P.val("a")
                E("vector", lambda h: h.tensor_reduce(out=sD[:, 4:5], in_=sD[:, 0:4],
                                                      axis=AX.X, op=OP.add),
                  wait=[("a", a_e2)], inc=("v", 1))
                v_es = P.val("v")
                E("vector", lambda h: h.reciprocal(out=sD[:, 5:6], in_=sD[:, 4:5]),
                  wait=[("v", v_es)], inc=("v", 1))
                v_er = P.val("v")
                E("vector", lambda h: h.tensor_scalar(out=sD[:, 0:4], in0=sD[:, 0:4],
                                                      scalar1=sD[:, 5:6], scalar2=None,
                                                      op0=OP.mult),
                  wait=[("v", v_er)], inc=("v", 1))
                v_en = P.val("v")
                E("vector", lambda h: h.tensor_reduce(out=sD[:, 4:5], in_=sD[:, 0:4],
                                                      axis=AX.X, op=OP.max),
                  wait=[("v", v_en)], inc=("v", 1))
                v_em2 = P.val("v")
                E("vector", lambda h, ts=ts: h.tensor_scalar(
                    out=rwm[:, ts, 4:8], in0=sD[:, 0:4], scalar1=sD[:, 4:5], scalar2=None,
                    op0=OP.is_ge),
                  wait=[("v", v_em2)], inc=("v", 1))
                v_m = P.val("v")
                E("vector", lambda h, ts=ts: h.tensor_tensor(out=rwm[:, ts, 0:4],
                                                             in0=sD[:, 0:4],
                                                             in1=rwm[:, ts, 4:8], op=OP.mult),
                  wait=[("v", v_m)], inc=("v", 1))
                v_rw = P.val("v")
                E("tensor", lambda h, ts=ts: h.transpose(
                    psC_[0:4, 128:256], rwm[:, ts, 0:4], identF[:, :]),
                  wait=[("v", v_rw)], inc=("t", 1))
                t_rt = P.val("t")
                E("vector", lambda h, ts=ts: h.tensor_copy(out=rwT[0:4, ts * 128:(ts + 1) * 128],
                                                           in_=psC_[0:4, 128:256]),
                  wait=[("t", t_rt)], inc=("v", 1))
                v_rw = P.val("v")
            v_rwb_prev = v_rw
            for ck in range(4):
                for e in range(4):
                    w0 = [("v", v_rwb_prev)] if e == 0 else None
                    MM(psC_[:, 0:512][:, e * 128:(e + 1) * 128], esel[:, e, :],
                       rwT[0:4, ck * 128:(ck + 1) * 128], True, True, w0,
                       ("t", 1) if e == 3 else None)
                t_rb = P.val("t")
                E("vector", lambda h, ck=ck: h.tensor_copy(out=rwb[:, ck, :], in_=psC_[:, 0:512]),
                  wait=[("t", t_rb)], inc=("v", 1))
                v_rwb_prev = P.val("v")
            v_rwb = P.val("v")
            for c in range(8):
                w0 = [("v", v_rwb)] if c == 0 else None
                MM(psB_[0:64, 0:512], wga[:, c, :], xnq(c), c == 0, c == 7, w0,
                   ("t", 1) if c == 7 else None)
            t_xg = P.val("t")
            E("scalar", lambda h: h.activation(out=xaf[:, :], in_=psB_[0:64, 0:512],
                                               func=AF.Identity),
              wait=[("t", t_xg)], inc=("a", 1))
            a_xg = P.val("a")
            for mi in range(4):
                w0 = [("a", a_xg)] if mi == 0 else None
                MM(ps4(mi), wgb[:, mi * 128:(mi + 1) * 128], xaf[:, :], True, True, w0,
                   ("t", 1) if mi == 3 else None)
            t_g = P.val("t")
            E("scalar", lambda h: h.activation(out=gf[:, 0:2, :].rearrange("p a b -> p (a b)"),
                                               in_=psA_[:, :], func=AF.Identity),
              wait=[("t", t_g)])
            E("scalar", lambda h: h.activation(out=gf[:, 2:4, :].rearrange("p a b -> p (a b)"),
                                               in_=psB_[:, :], func=AF.Identity),
              inc=("a", 1))
            a_gf = P.val("a")
            sHb = SP[:, 4096:6144]
            E("scalar", lambda h, sHb=sHb: h.activation(
                out=sHb, in_=gf[:, :, :].rearrange("p a b -> p (a b)"), func=AF.Tanh, scale=0.5),
              wait=[("a", a_gf)], inc=("a", 1))
            a_th = P.val("a")
            E("vector", lambda h, sHb=sHb: h.tensor_scalar(
                out=sHb, in0=sHb, scalar1=1.0, scalar2=0.5, op0=OP.add, op1=OP.mult),
              wait=[("a", a_th)], inc=("v", 1))
            v_t2 = P.val("v")
            E("vector", lambda h, sHb=sHb: h.tensor_tensor(
                out=sgf[:, :, :].rearrange("p a b -> p (a b)"),
                in0=gf[:, :, :].rearrange("p a b -> p (a b)"), in1=sHb, op=OP.mult),
              wait=[("v", v_t2)], inc=("v", 1))
            v_sg = P.val("v")

            if dbg and s == 0 and q == DQ:
                for nm, sb in (("q1f", q1f), ("q2f", q2f), ("k1f", k1f), ("k2f", k2f),
                               ("k1tm", k1tm), ("k2tm", k2tm), ("vt", vt),
                               ("gk1tm", gk1tm), ("gk2tm", gk2tm), ("rwm", rwm),
                               ("rwb", rwb), ("sgf", sgf)):
                    E("sync", lambda h, nm=nm, sb=sb: h.dma_start(out=dbgT[nm], in_=sb[:, :, :]),
                      wait=[("v", v_sg)] if nm == "q1f" else None, inc=("st", 16))
            for ck in range(NCK):
                for p in range(NPAIR_F):
                    u = state.get("l1f_u", 0)
                    psG = psA_ if u % 2 == 0 else psE_
                    psl = slice(p * 128, (p + 1) * 128)
                    cksl = slice(ck * 128, (ck + 1) * 128)
                    w0 = [("v", v_sg)] if (ck == 0 and p == 0) else None
                    E("vector", lambda h, ck=ck, psl=psl: h.tensor_copy(
                        out=gkc[:, 0:128], in_=gk1tm[:, ck, psl]),
                      wait=(w0 or []) + [("t", state.get("l1f_csg", 0))])
                    E("vector", lambda h, ck=ck, psl=psl: h.tensor_tensor(
                        out=gkc[:, 128:640].rearrange("p (e w) -> p e w", w=128),
                        in0=gk2tm[:, ck, psl].unsqueeze(1).broadcast_to([128, 4, 128]),
                        in1=rwm[:, ck, 4:8].unsqueeze(2).broadcast_to([128, 4, 128]),
                        op=OP.mult), inc=("v", 1))
                    v_gkc = P.val("v")
                    for j in range(5):
                        w0 = [("v", v_gkc)]
                        if j == 0:
                            w0.append(("a", state.get(("gfmw", id(psG)), 0)))
                            w0.append(("v", state.get(("gfmv", id(psG)), 0)))
                        MM(psG[:, 0:640][:, j * 128:(j + 1) * 128],
                           gkc[:, j * 128:(j + 1) * 128], Lb[:, :], True, True,
                           w0 if j == 0 else None, ("t", 1) if j == 4 else None)
                    t_cs = P.val("t")
                    if state.get("l1f_pend") is not None:
                        state["l1f_pend"]()
                        state["l1f_pend"] = None
                    E("scalar", lambda h, psG=psG, u=u: h.activation(
                        out=dec[:, u % 2, :].unsqueeze(2),
                        in_=psG[:, 0:640].rearrange("p (j w) -> p j w", w=128)[:, :, 127:128],
                        func=AF.Exp),
                      wait=[("t", t_cs)])
                    E("scalar", lambda h, psG=psG: h.activation(out=expg[:, :], in_=psG[:, 0:640],
                                                                func=AF.Exp))
                    E("scalar", lambda h, psG=psG: h.activation(out=negg[:, :], in_=psG[:, 0:640],
                                                                func=AF.Exp, scale=-1.0),
                      inc=("a", 1))
                    a_ex = P.val("a")
                    state[("gfmw", id(psG))] = a_ex
                    E("vector", lambda h, p=p, cksl=cksl: h.tensor_copy(
                        out=qcat[:, 0:128], in_=q1f[:, p, cksl]),
                      wait=[("a", a_ex), ("t", state.get("l1f_Aw", 0))])
                    E("vector", lambda h, p=p, ck=ck, cksl=cksl: h.tensor_tensor(
                        out=qcat[:, 128:640].rearrange("p (e w) -> p e w", w=128),
                        in0=q2f[:, p, cksl].unsqueeze(1).broadcast_to([128, 4, 128]),
                        in1=rwb[:, ck, :].rearrange("p (e w) -> p e w", w=128), op=OP.mult))
                    E("vector", lambda h: h.tensor_tensor(out=qcat[:, :], in0=qcat[:, :],
                                                          in1=expg[:, :], op=OP.mult))
                    E("vector", lambda h, p=p, cksl=cksl: h.tensor_copy(
                        out=kcat[:, 0:128], in_=k1f[:, p, cksl]))
                    E("vector", lambda h, p=p, ck=ck, cksl=cksl: h.tensor_tensor(
                        out=kcat[:, 128:640].rearrange("p (e w) -> p e w", w=128),
                        in0=k2f[:, p, cksl].unsqueeze(1).broadcast_to([128, 4, 128]),
                        in1=rwb[:, ck, :].rearrange("p (e w) -> p e w", w=128), op=OP.mult))
                    E("vector", lambda h: h.tensor_tensor(out=kcat[:, :], in0=kcat[:, :],
                                                          in1=negg[:, :], op=OP.mult))
                    E("vector", lambda h, p=p: h.tensor_copy(
                        out=Sbf[0:64, :, 0:64], in_=Sst[0:64, p, :, 0:64]),
                      wait=[("t", state.get("t_oi", 0))])
                    E("vector", lambda h, p=p: h.tensor_copy(
                        out=Sbf[64:128, :, 64:128], in_=Sst[64:128, p, :, 64:128]),
                      inc=("v", 1))
                    v_bld = P.val("v")

                    def pend(p=p, ck=ck, cksl=cksl, psl=psl, v_bld=v_bld, psG=psG, u=u):
                        psBb = psB_[:, :].bitcast(BF16)
                        for j in range(5):
                            w0 = None
                            if j == 0:
                                w0 = [("v", v_bld), ("v", state.get("l1f_tpw", 0)),
                                      ("g", g_c)]
                            E("tensor", lambda h, j=j, psBb=psBb: h.transpose(
                                psBb[:, j * 128:(j + 1) * 128],
                                kcat[:, j * 128:(j + 1) * 128], identB[:, :]),
                              wait=w0, inc=("t", 1) if j == 4 else None)
                        t_tp5 = P.val("t")
                        E("vector", lambda h, psBb=psBb: h.tensor_copy(
                            out=kctm[:, 0:640], in_=psBb[:, 0:640]),
                          wait=[("t", t_tp5)], inc=("v", 1))
                        state["l1f_tpw"] = P.val("v")
                        for j in range(5):
                            w0 = [("v", v_bld)] if j == 0 else None
                            MM(psC_[:, 0:128], kcat[0:64, j * 128:(j + 1) * 128],
                               qcat[0:64, j * 128:(j + 1) * 128], j == 0, j == 4, w0,
                               None, (0, 0))
                            MM(psD_[:, 0:128], kcat[64:128, j * 128:(j + 1) * 128],
                               qcat[64:128, j * 128:(j + 1) * 128], j == 0, j == 4, None,
                               ("t", 1) if j == 4 else None, (64, 0))
                        t_A = P.val("t")
                        E("vector", lambda h: h.tensor_tensor(
                            out=Am[:, 0:128], in0=psC_[:, 0:128], in1=Lf[:, :], op=OP.mult),
                          wait=[("t", t_A)])
                        E("vector", lambda h: h.tensor_tensor(
                            out=Am[:, 128:256], in0=psD_[:, 0:128], in1=Lf[:, :], op=OP.mult),
                          inc=("v", 1))
                        v_am = P.val("v")
                        MM(psC_[0:64, 128:256], vt[:, ck, p * 128:p * 128 + 64], Am[:, 0:128],
                           True, False, [("v", v_am)], None, (0, 0))
                        MM(psD_[64:128, 128:256], vt[:, ck, p * 128 + 64:(p + 1) * 128],
                           Am[:, 128:256], True, False, None, None, (0, 64))
                        for j in range(5):
                            MM(psC_[0:64, 128:256], Sbf[:, j, 0:64],
                               qcat[:, j * 128:(j + 1) * 128], False, j == 4, None, None,
                               (0, 0))
                            MM(psD_[64:128, 128:256], Sbf[:, j, 64:128],
                               qcat[:, j * 128:(j + 1) * 128], False, j == 4, None,
                               ("t", 1) if j == 4 else None, (0, 64))
                        t_oi = P.val("t")
                        state["t_oi"] = t_oi
                        for j in range(5):
                            w0 = [("v", state["l1f_tpw"])] if j == 0 else None
                            MM(psG[:, 0:640][:, j * 128:(j + 1) * 128],
                               kctm[:, j * 128:(j + 1) * 128], vt[:, ck, psl], True, True,
                               w0, ("t", 1) if j == 4 else None)
                        t_kv2 = P.val("t")
                        state["l1f_Aw"] = t_kv2
                        E("vector", lambda h, p=p, psG=psG: h.tensor_tensor(
                            out=Sst[:, p, :, :].rearrange("p a b -> p (a b)"),
                            in0=Sst[:, p, :, :].rearrange("p a b -> p (a b)"),
                            in1=psG[:, 0:640], op=OP.add),
                          wait=[("t", t_kv2)])
                        E("vector", lambda h, p=p, u=u: h.tensor_tensor(
                            out=Sst[:, p, :, :], in0=Sst[:, p, :, :],
                            in1=dec[:, u % 2, :].unsqueeze(2).broadcast_to([128, 5, 128]),
                            op=OP.mult), inc=("v", 1))
                        state[("gfmv", id(psG))] = P.val("v")
                        E("vector", lambda h, p=p, cksl=cksl: h.tensor_copy(
                            out=ob[0:64, p, cksl], in_=psC_[0:64, 128:256]),
                          wait=[("t", t_oi)])
                        E("vector", lambda h, p=p, cksl=cksl: h.tensor_copy(
                            out=ob[64:128, p, cksl], in_=psD_[64:128, 128:256]),
                          inc=("v", 1))
                    state["l1f_pend"] = pend
                    state["l1f_u"] = u + 1
            if state.get("l1f_pend") is not None:
                state["l1f_pend"]()
                state["l1f_pend"] = None
            v_p2 = P.val("v")

            obf = ob[:, :, :].rearrange("p a b -> p (a b)")
            E("vector", lambda h, obf=obf: h.tensor_tensor(out=SP[:, 0:2048], in0=obf, in1=obf,
                                                           op=OP.mult),
              wait=[("v", v_p2)], inc=("v", 1))
            v_o2 = P.val("v")
            for j in range(4):
                w0 = None
                if j == 0:
                    w0 = [("v", v_o2), ("a", state.get("p2_ps_war", 0))]
                MM(ps4(j), OSEL[:, :], SP[:, j * 512:(j + 1) * 512], True, True, w0,
                   ("t", 1) if j == 3 else None)
            t_rms = P.val("t")
            lnbuf = SP[:, 4096:8192].bitcast(F32)
            E("scalar", lambda h, lnbuf=lnbuf: h.activation(out=lnbuf[:, 0:1024], in_=psA_[:, :],
                                               func=AF.Ln, scale=1.0 / 64.0, bias=1e-5),
              wait=[("t", t_rms)])
            E("scalar", lambda h, lnbuf=lnbuf: h.activation(out=lnbuf[:, 1024:2048], in_=psB_[:, :],
                                               func=AF.Ln, scale=1.0 / 64.0, bias=1e-5),
              inc=("a", 1))
            a_l = P.val("a")
            E("scalar", lambda h, lnbuf=lnbuf: h.activation(out=SP[:, 2048:4096], in_=lnbuf[:, 0:2048],
                                               func=AF.Exp, scale=-0.5),
              wait=[("a", a_l)], inc=("a", 1))
            a_rstd = P.val("a")
            state["p3_ps_war"] = a_rstd
            E("vector", lambda h, obf=obf: h.tensor_tensor(out=obf, in0=obf,
                                                           in1=SP[:, 2048:4096], op=OP.mult),
              wait=[("a", a_rstd)])
            E("vector", lambda h, obf=obf: h.tensor_scalar(out=obf, in0=obf, scalar1=nwS[:, 0:1],
                                                           scalar2=None, op0=OP.mult),
              wait=[("lp", lp)])
            E("vector", lambda h, obf=obf: h.tensor_tensor(
                out=obf, in0=obf, in1=sgf[:, :, :].rearrange("p a b -> p (a b)"), op=OP.mult),
              inc=("v", 1))
            v_of = P.val("v")
            v_ep = None
            for m in range(8):
                for c in range(4):
                    w0 = None
                    if c == 0:
                        w0 = [("v", v_of)]
                        if v_ep is not None:
                            w0 = [("v", v_ep)]
                    MM(psC_[:, 0:512], wo[:, c, m * 128:(m + 1) * 128], ob[:, c, :],
                       c == 0, c == 3, w0, ("t", 1) if c == 3 else None)
                t_wo = P.val("t")
                st_now = P.val("st")
                w_x1 = [("st", st_now), ("v", state.get("ep_v", 0))]
                if m == 0:
                    w_x1.append(("v", v_of))
                E("sync", lambda h, m=m, s=s, q0=q0: h.dma_start(
                    out=U[:, 0, :], in_=x1T[s, m * 128:(m + 1) * 128, q0:q0 + QT]),
                  wait=w_x1, inc=("ld", 16))
                E("sync", lambda h, m=m, s=s, q0=q0: h.dma_start(
                    out=U[:, 1, :], in_=attnT[s, m * 128:(m + 1) * 128, q0:q0 + QT]),
                  inc=("ld", 16))
                ld_r = P.val("ld")
                E("vector", lambda h: h.tensor_copy(out=ot[:, :], in_=psC_[:, 0:512]),
                  wait=[("t", t_wo), ("ld", ld_r)], inc=("v", 1))
                v_ep = P.val("v")
                E("vector", lambda h: h.tensor_tensor(out=ot[:, :], in0=ot[:, :],
                                                      in1=U[:, 0, :], op=OP.add))
                E("vector", lambda h: h.tensor_tensor(out=ot[:, :], in0=ot[:, :],
                                                      in1=U[:, 1, :], op=OP.add),
                  inc=("v", 1))
                state["ep_v"] = P.val("v")
                v_ot = P.val("v")
                E("sync", lambda h, m=m, s=s, q0=q0: h.dma_start(
                    out=out2T[s, m * 128:(m + 1) * 128, q0:q0 + QT], in_=ot[:, :]),
                  wait=[("v", v_ot)], inc=("st", 16))
            state["p1_war"] = (P.val("a"), P.val("v"))
        state["ln_war"] = P.val("t")
        state["ps_war"] = P.val("a")



_CACHE = {}

T_ = 2048


def _program():
    if "p" in _CACHE:
        return _CACHE["p"]
    nc = bass.Bass("TRN2", target_bir_lowering=False)
    P = Plan(nc)
    F = F32
    d = {}

    def din(name, shape):
        d[name] = nc.dram_tensor(name, shape, F, kind="ExternalInput").ap()
    din("xT", [2, 1024, T_])
    din("wqkvT", [1024, 3072]); din("woutT", [1024, 1024]); din("cbT", [1024, 1024])
    din("bout", [128, 8]); din("scb", [128, 8]); din("fbias", [128, 8])
    din("wqT", [1024, 1024]); din("wkT", [1024, 1024]); din("wvT", [1024, 1024])
    din("woT", [1024, 1024]); din("bqf", [128, 8]); din("bkf", [128, 8])
    din("bvr", [1, 1024]); din("bof", [128, 8])
    din("wq", [1024, 512]); din("wk", [1024, 512]); din("wv", [1024, 512])
    din("wqa", [1024, 64]); din("wqb", [64, 512]); din("wka", [1024, 64])
    din("wkb", [64, 512]); din("wg1a", [1024, 16]); din("wg1b", [16, 512])
    din("bg1", [1, 512]); din("wg2a", [1024, 16]); din("wg2b", [16, 512])
    din("bg2", [1, 512]); din("we", [1024, 4]); din("wga", [1024, 64])
    din("wgb", [64, 512]); din("nw", [128, 1]); din("wo", [512, 1024])
    d["x1T"] = nc.dram_tensor("x1T", [2, 1024, T_], F, kind="Internal").ap()
    d["attnT"] = nc.dram_tensor("attnT", [2, 1024, T_], F, kind="Internal").ap()
    d["out2T"] = nc.dram_tensor("out2T", [2, 1024, T_], F, kind="ExternalOutput").ap()

    for s in ("ld", "lp", "w", "st", "t", "v", "a", "g"):
        P.sem(s)
    cpool = P.sbuf("cpool_root", [128, 4], F32)
    for i, cv in enumerate((1e-5, 1e-6, 1e-30, 1.0)):
        nc.const_aps.register(F32, cv, cpool[:, i:i + 1])
        P.emit("vector", lambda h, i=i, cv=cv: h.memset(cpool[:, i:i + 1], cv),
               inc=("v", 1) if i == 3 else None)

    with P.scope():
        emit_l0(nc, P, d, 2)
    P.barrier()
    d1 = dict(d)
    d1["xT"] = d["x1T"]
    with P.scope():
        emit_l1a(nc, P, d1, 2)
    P.barrier()
    with P.scope():
        emit_l1f(nc, P, d, 2)
    P.wait("sync", ("st", P.val("st")))
    P.finalize()
    _CACHE["p"] = nc
    return nc


def kernel(L0_attn_x, L0_attn_Wqkv, L0_attn_Wout, L0_attn_bout, L0_ffn_CB, L0_ffn_SCB,
           L0_ffn_bias, L1_attn_W_Q, L1_attn_b_Q, L1_attn_W_K, L1_attn_b_K, L1_attn_W_V,
           L1_attn_b_V, L1_attn_W_O, L1_attn_b_O, L1_ffn_Wq, L1_ffn_Wk, L1_ffn_Wv,
           L1_ffn_Wqa, L1_ffn_Wqb, L1_ffn_Wka, L1_ffn_Wkb, L1_ffn_Wg1a, L1_ffn_Wg1b,
           L1_ffn_bg1, L1_ffn_Wg2a, L1_ffn_Wg2b, L1_ffn_bg2, L1_ffn_We, L1_ffn_Wga,
           L1_ffn_Wgb, L1_ffn_norm_w, L1_ffn_Wo):
    C = np.ascontiguousarray
    ncores = 8
    x = np.asarray(L0_attn_x)
    fm8 = lambda b: C(np.asarray(b).reshape(8, 128).T)

    nc = _program()
    w = {
        "wqkvT": C(np.asarray(L0_attn_Wqkv).T),
        "woutT": C(np.asarray(L0_attn_Wout).T),
        "cbT": C(np.asarray(L0_ffn_CB).T),
        "bout": fm8(L0_attn_bout),
        "scb": fm8(L0_ffn_SCB),
        "fbias": fm8(L0_ffn_bias),
        "wqT": C(np.asarray(L1_attn_W_Q).transpose(2, 0, 1).reshape(1024, 1024)),
        "wkT": C(np.asarray(L1_attn_W_K).transpose(2, 0, 1).reshape(1024, 1024)),
        "wvT": C(np.asarray(L1_attn_W_V).transpose(2, 0, 1).reshape(1024, 1024)),
        "woT": C(np.asarray(L1_attn_W_O).transpose(0, 2, 1).reshape(1024, 1024)),
        "bqf": fm8(np.asarray(L1_attn_b_Q).reshape(1024)),
        "bkf": fm8(np.asarray(L1_attn_b_K).reshape(1024)),
        "bvr": C(np.asarray(L1_attn_b_V).reshape(1, 1024)),
        "bof": fm8(L1_attn_b_O),
        "wq": C(np.asarray(L1_ffn_Wq)), "wk": C(np.asarray(L1_ffn_Wk)),
        "wv": C(np.asarray(L1_ffn_Wv)), "wqa": C(np.asarray(L1_ffn_Wqa)),
        "wqb": C(np.asarray(L1_ffn_Wqb)), "wka": C(np.asarray(L1_ffn_Wka)),
        "wkb": C(np.asarray(L1_ffn_Wkb)), "wg1a": C(np.asarray(L1_ffn_Wg1a)),
        "wg1b": C(np.asarray(L1_ffn_Wg1b)), "bg1": C(np.asarray(L1_ffn_bg1).reshape(1, 512)),
        "wg2a": C(np.asarray(L1_ffn_Wg2a)), "wg2b": C(np.asarray(L1_ffn_Wg2b)),
        "bg2": C(np.asarray(L1_ffn_bg2).reshape(1, 512)), "we": C(np.asarray(L1_ffn_We)),
        "wga": C(np.asarray(L1_ffn_Wga)), "wgb": C(np.asarray(L1_ffn_Wgb)),
        "nw": C(np.tile(np.asarray(L1_ffn_norm_w), 2).reshape(128, 1)),
        "wo": C(np.asarray(L1_ffn_Wo)),
    }
    in_maps = [dict(w, xT=C(x[2 * i:2 * i + 2].transpose(0, 2, 1))) for i in range(ncores)]
    res = run_bass_kernel_spmd(nc, in_maps, core_ids=list(range(ncores)))
    out = np.concatenate([np.asarray(res.results[i]["out2T"]).transpose(0, 2, 1)
                          for i in range(ncores)], axis=0)
    return out.astype(np.float32)


# version 19
# speedup vs baseline: 1.0441x; 1.0425x over previous
import numpy as np
import concourse.bass as bass
import concourse.mybir as mybir
from concourse.alu_op_type import AluOpType as OP
from contextlib import ExitStack
from concourse.bass_utils import run_bass_kernel_spmd

F32 = mybir.dt.float32
BF16 = mybir.dt.bfloat16
AF = mybir.ActivationFunctionType
AX = mybir.AxisListType


ENGS = ("sync", "tensor", "vector", "scalar", "gpsimd")


def _reg_const(nc, val, ap):
    try:
        nc.const_aps.register(F32, val, ap)
    except Exception:
        pass


class Plan:

    def __init__(self, nc):
        self.nc = nc
        self.q = {e: [] for e in ENGS}
        self.sems = {}
        self.semv = {}
        self.stack = ExitStack()
        self.root_stack = self.stack
        self.scope_id = 0
        self.fence = None

    def scope(self):
        from contextlib import contextmanager

        @contextmanager
        def _s():
            old = self.stack
            self.scope_id += 1
            self.stack = ExitStack()
            try:
                yield
            finally:
                self.stack.close()
                self.stack = old
        return _s()

    def barrier(self):
        if self.fence is None:
            old = self.stack
            self.stack = self.root_stack
            self.fence = self.sbuf("fencetile", [1, 2], F32)
            self.stack = old
        ft = self.fence
        self.emit("vector", lambda h: h.memset(ft[0:1, 0:1], 0.0), inc=("v", 1))
        self.emit("scalar", lambda h: h.activation(out=ft[0:1, 1:2], in_=ft[0:1, 0:1],
                                                   func=AF.Identity), inc=("a", 1))
        self.emit("gpsimd", lambda h: h.memset(ft[0:1, 1:2], 0.0), inc=("g", 1))
        pairs = [(s, v) for s, v in self.semv.items()]
        for e in ENGS:
            self.wait(e, *pairs)

    def sem(self, name):
        h = self.stack.enter_context(self.nc.semaphore(name))
        self.sems[name] = h
        self.semv[name] = 0
        return h

    def sbuf(self, name, shape, dtype):
        return self.stack.enter_context(
            self.nc.sbuf_tensor(f"s{self.scope_id}_{name}", shape, dtype))

    def psum(self, name, shape, dtype=F32):
        return self.stack.enter_context(
            self.nc.psum_tensor(f"s{self.scope_id}_{name}", shape, dtype))

    def emit(self, eng, fn, wait=None, inc=None):
        if inc is not None:
            self.semv[inc[0]] += inc[1]
        sems = self.sems

        def f(h):
            if wait:
                for s, v in wait:
                    h.wait_ge(sems[s], v)
            r = fn(h)
            if inc is not None:
                r.then_inc(sems[inc[0]], inc[1])
        self.q[eng].append(f)

    def wait(self, eng, *pairs):
        sems = self.sems

        def f(h):
            for s, v in pairs:
                h.wait_ge(sems[s], v)
        self.q[eng].append(f)

    def val(self, semname):
        return self.semv[semname]

    def after_all(self, semname):
        return (semname, self.semv[semname])

    def finalize(self):
        nc = self.nc
        q = self.q
        with nc.Block() as block:
            @block.sync
            def _(h):
                for f in q["sync"]:
                    f(h)

            @block.tensor
            def _(h):
                for f in q["tensor"]:
                    f(h)

            @block.vector
            def _(h):
                for f in q["vector"]:
                    f(h)

            @block.scalar
            def _(h):
                for f in q["scalar"]:
                    f(h)

            @block.gpsimd
            def _(h):
                for f in q["gpsimd"]:
                    f(h)
        self.stack.close()


T = 2048
NCH = 8
NTT = 4
NTS = 16
NPAIR = 8


def emit_l0(nc, P, d, nseq=2):
    xT = d["xT"]
    wqkvT = d["wqkvT"]
    woutT = d["woutT"]
    cbT = d["cbT"]
    bout = d["bout"]
    scb = d["scb"]
    fbias = d["fbias"]
    x1T = d["x1T"]

    xn = P.sbuf("xn", [128, NCH, T], BF16)
    wbuf = P.sbuf("wbuf", [128, NCH, 2048], BF16)
    qp = P.sbuf("qp", [128, NCH, T], BF16)
    kp = P.sbuf("kp", [128, NTS, 1024], BF16)
    vt = P.sbuf("vt", [128, NTS, 16 * 65], BF16)
    kvs = P.sbuf("kvs", [128, NPAIR * 130], BF16)
    U = P.sbuf("U", [128, NCH, 512], F32)
    SP = P.sbuf("SP", [128, 8192], BF16)
    sD = P.sbuf("sD", [128, 512], F32)
    sE = P.sbuf("sE", [128, 512], F32)
    sF = P.sbuf("sF", [128, 512], F32)
    sG = P.sbuf("sG", [128, 512], F32)
    ones128 = P.sbuf("ones128", [128, 128], BF16)
    ones1f = P.sbuf("ones1f", [1, 64], F32)
    ksumC = P.sbuf("ksumC", [128, NPAIR, 64], BF16)
    boutS = P.sbuf("boutS", [128, 8], F32)
    scbS = P.sbuf("scbS", [128, 8], F32)
    fbS = P.sbuf("fbS", [128, 8], F32)
    ea = sF
    ef = sG

    xtile = U
    sA = U[:, 0:2, :].rearrange("p a b -> p (a b)")
    xb = SP[:, 0:4096].rearrange("p (a b) -> p a b", b=512)
    sqb = SP[:, 4096:8192].rearrange("p (a b) -> p a b", b=512)
    sB = SP[:, 0:1024]
    sC = SP[:, 1024:2048]

    psBig = P.psum("psBig", [128, 2048])
    psB2 = P.psum("psB2", [128, 1024])
    psQ1 = P.psum("psQ1", [128, 512])
    psQ2 = P.psum("psQ2", [128, 512])

    E = P.emit

    E("vector", lambda h: h.memset(ones128[:, :], 1.0))
    E("vector", lambda h: h.memset(ones1f[:, :], 1.0))
    E("sync", lambda h: h.dma_start(out=boutS[:, :], in_=bout), inc=("lp", 16))
    E("sync", lambda h: h.dma_start(out=scbS[:, :], in_=scb), inc=("lp", 16))
    E("sync", lambda h: h.dma_start(out=fbS[:, :], in_=fbias), inc=("lp", 16))
    ld_params = P.val("lp")
    E("vector", lambda h: h.tensor_scalar(out=scbS[:, :], in0=scbS[:, :], scalar1=1.0 / 127.0,
                                          scalar2=None, op0=OP.mult),
      wait=[("lp", ld_params)], inc=("v", 1))
    v_scb = P.val("v")
    ln_war = 0
    vt_ones_v = None
    a_kvfree = None
    v_vt = None

    for s in range(nseq):
        t_wb = P.val("t")
        for c in range(8):
            E("gpsimd", lambda h, c=c: h.dma_start(out=wbuf[:, c, :],
                                                   in_=wqkvT[c * 128:(c + 1) * 128, 1024:3072]),
              wait=[("t", t_wb)] if (c == 0 and s > 0) else None, inc=("w", 16))
        w_kv = P.val("w")
        for tt in range(NTT):
            tsl = slice(tt * 512, (tt + 1) * 512)
            st_now = P.val("st")
            v_now = P.val("v")
            for c in range(8):
                E("sync", lambda h, c=c, s=s, tsl=tsl: h.dma_start(
                    out=xtile[:, c, :], in_=xT[s, c * 128:(c + 1) * 128, tsl]),
                  wait=[("st", st_now), ("v", v_now)] if c == 0 else None, inc=("ld", 16))
            ld_x = P.val("ld")
            for c in range(8):
                w0 = [("ld", ld_x), ("t", ln_war)] if c == 0 else None
                E("vector", lambda h, c=c: h.tensor_copy(out=xb[:, c, :], in_=xtile[:, c, :]), wait=w0)
                E("vector", lambda h, c=c: h.tensor_tensor(
                    out=sqb[:, c, :], in0=xb[:, c, :], in1=xb[:, c, :], op=OP.mult),
                  inc=("v", 1) if c == 7 else None)
            v_cast = P.val("v")
            for c in range(8):
                E("tensor", lambda h, c=c: h.matmul(psB2[:, 0:512], ones128[:, :], xb[:, c, :],
                                                    start=(c == 0), stop=(c == 7)),
                  wait=[("v", v_cast)] if c == 0 else None)
            for c in range(8):
                E("tensor", lambda h, c=c: h.matmul(psB2[:, 512:1024], ones128[:, :], sqb[:, c, :],
                                                    start=(c == 0), stop=(c == 7)),
                  inc=("t", 1) if c == 7 else None)
            t_stats = P.val("t")
            ln_war = t_stats
            E("vector", lambda h: h.tensor_scalar(out=sD[:, :], in0=psB2[:, 0:512],
                                                  scalar1=1.0 / 1024.0, scalar2=None, op0=OP.mult),
              wait=[("t", t_stats)])
            E("vector", lambda h: h.tensor_scalar(out=sG[:, :], in0=psB2[:, 512:1024],
                                                  scalar1=1.0 / 1024.0, scalar2=None, op0=OP.mult))
            E("vector", lambda h: h.tensor_tensor(out=sE[:, :], in0=sD[:, :], in1=sD[:, :], op=OP.mult))
            E("vector", lambda h: h.tensor_tensor(out=sE[:, :], in0=sG[:, :], in1=sE[:, :],
                                                  op=OP.subtract), inc=("v", 1))
            v_var = P.val("v")
            E("scalar", lambda h: h.activation(out=sG[:, :], in_=sE[:, :], func=AF.Ln, bias=1e-5),
              wait=[("v", v_var)], inc=("a", 1))
            a_ln = P.val("a")
            E("scalar", lambda h: h.activation(out=sE[:, :], in_=sG[:, :], func=AF.Exp, scale=-0.5),
              wait=[("a", a_ln)], inc=("a", 1))
            a_rstd = P.val("a")
            E("vector", lambda h: h.tensor_tensor(out=sF[:, :], in0=sD[:, :], in1=sE[:, :], op=OP.mult),
              wait=[("a", a_rstd)])
            for c in range(8):
                E("vector", lambda h, c=c: h.tensor_tensor(out=xtile[:, c, :], in0=xtile[:, c, :],
                                                           in1=sE[:, :], op=OP.mult))
                E("vector", lambda h, c=c, tsl=tsl: h.tensor_tensor(
                    out=xn[:, c, tsl], in0=xtile[:, c, :], in1=sF[:, :], op=OP.subtract),
                  inc=("v", 1) if c == 7 else None)
        v_xn = P.val("v")
        if vt_ones_v is None:
            E("vector", lambda h: h.memset(
                vt[:, :, :].rearrange("p a (h w) -> p a h w", w=65)[:, :, :, 64:65], 1.0),
              inc=("v", 1))
            vt_ones_v = P.val("v")

        kv_add_prev = v_xn
        for ts in range(NTS):
            ssl = slice(ts * 128, (ts + 1) * 128)
            for c in range(8):
                for g4 in range(4):
                    w0 = None
                    if c == 0 and g4 == 0:
                        if ts == 0:
                            w0 = [("v", v_xn), ("w", w_kv)]
                            if a_kvfree is not None:
                                w0 += [("a", a_kvfree), ("v", v_vt)]
                        else:
                            w0 = [("a", a_kvfree), ("v", v_vt)]
                    E("tensor", lambda h, c=c, g4=g4, ssl=ssl: h.matmul(
                        psBig[:, g4 * 512:(g4 + 1) * 512], xn[:, c, ssl],
                        wbuf[:, c, g4 * 512:(g4 + 1) * 512],
                        start=(c == 0), stop=(c == 7)), wait=w0,
                        inc=("t", 1) if (c == 7 and g4 == 3) else None)
            t_kv = P.val("t")
            E("scalar", lambda h: h.activation(out=sA[:, :], in_=psBig[:, 0:1024],
                                               func=AF.Relu, scale=-1.0),
              wait=[("t", t_kv), ("v", kv_add_prev)], inc=("a", 1))
            a_r1 = P.val("a")
            E("scalar", lambda h: h.activation(out=sC[:, :], in_=psBig[:, 0:1024], func=AF.Relu),
              inc=("a", 1))
            a_kvfree = P.val("a")
            E("scalar", lambda h: h.activation(out=sB[:, :], in_=sA[:, :], func=AF.Exp, scale=-1.0),
              wait=[("a", a_r1)], inc=("a", 1))
            a_exp = P.val("a")
            E("vector", lambda h, ts=ts: h.tensor_tensor(out=kp[:, ts, :], in0=sB[:, :], in1=sC[:, :],
                                                         op=OP.add),
              wait=[("a", a_exp), ("v", vt_ones_v)], inc=("v", 1))
            kv_add_prev = P.val("v")
            E("vector", lambda h, ts=ts: h.tensor_copy(
                out=vt[:, ts, :].rearrange("p (h w) -> p h w", w=65)[:, :, 0:64],
                in_=psBig[:, 1024:2048].rearrange("p (h w) -> p h w", w=64)),
              inc=("v", 1))
            v_vt = P.val("v")
        v_kv_done = P.val("v")

        t_all_kv = P.val("t")
        for c in range(8):
            E("gpsimd", lambda h, c=c: h.dma_start(out=wbuf[:, c, 0:1024],
                                                   in_=wqkvT[c * 128:(c + 1) * 128, 0:1024]),
              wait=[("t", t_all_kv)] if c == 0 else None, inc=("w", 16))
        w_q = P.val("w")
        a_start = P.val("a")
        ps_free = {id(psQ1): a_start, id(psQ2): a_start}
        q_add_prev = v_kv_done
        for m in range(8):
            for tt in range(NTT):
                ps = psQ1 if (m * NTT + tt) % 2 == 0 else psQ2
                tsl = slice(tt * 512, (tt + 1) * 512)
                for c in range(8):
                    w0 = None
                    if c == 0:
                        w0 = [("w", w_q)]
                        if ps_free[id(ps)] is not None:
                            w0.append(("a", ps_free[id(ps)]))
                    E("tensor", lambda h, c=c, m=m, tsl=tsl, ps=ps: h.matmul(
                        ps[:, :], wbuf[:, c, m * 128:(m + 1) * 128], xn[:, c, tsl],
                        start=(c == 0), stop=(c == 7)), wait=w0,
                        inc=("t", 1) if c == 7 else None)
                t_q = P.val("t")
                E("scalar", lambda h, ps=ps: h.activation(out=sA[:, 0:512], in_=ps[:, :],
                                                          func=AF.Relu, scale=-1.0),
                  wait=[("t", t_q), ("v", q_add_prev)], inc=("a", 1))
                a_r1 = P.val("a")
                E("scalar", lambda h, ps=ps: h.activation(out=sC[:, 0:512], in_=ps[:, :], func=AF.Relu),
                  inc=("a", 1))
                ps_free[id(ps)] = P.val("a")
                E("scalar", lambda h: h.activation(out=sB[:, 0:512], in_=sA[:, 0:512],
                                                   func=AF.Exp, scale=-1.0),
                  wait=[("a", a_r1)], inc=("a", 1))
                a_q = P.val("a")
                E("vector", lambda h, m=m, tsl=tsl: h.tensor_tensor(
                    out=qp[:, m, tsl], in0=sB[:, 0:512], in1=sC[:, 0:512], op=OP.add),
                  wait=[("a", a_q)], inc=("v", 1))
                q_add_prev = P.val("v")
        v_q_done = P.val("v")
        a_q_done = P.val("a")

        kvs_evac = None
        for p in range(NPAIR):
            csl = slice(p * 130, (p + 1) * 130)
            for ts in range(NTS):
                w0 = None
                if ts == 0:
                    w0 = [("v", v_kv_done)]
                    if kvs_evac is not None:
                        w0.append(("v", kvs_evac))
                E("tensor", lambda h, p=p, ts=ts, csl=csl: h.matmul(
                    psB2[:, 0:130], kp[:, ts, p * 128:(p + 1) * 128], vt[:, ts, csl],
                    start=(ts == 0), stop=(ts == NTS - 1)), wait=w0,
                    inc=("t", 1) if ts == NTS - 1 else None)
            t_p = P.val("t")
            E("vector", lambda h, csl=csl: h.tensor_copy(out=kvs[:, csl], in_=psB2[:, 0:130]),
              wait=[("t", t_p)], inc=("v", 1))
            kvs_evac = P.val("v")
        v_kvs = P.val("v")

        for p in range(NPAIR):
            E("vector", lambda h, p=p: h.tensor_copy(
                out=ksumC[:, p, :],
                in_=kvs[:, p * 130 + 64:p * 130 + 65].broadcast_to([128, 64])),
              wait=[("v", v_kvs)] if p == 0 else None, inc=("v", 1) if p == 7 else None)
        v_ksumC = P.val("v")
        mult_done = [None] * 4
        addz_done = [None] * 2
        for p in range(NPAIR):
            for tt in range(NTT):
                idx = p * NTT + tt
                tsl = slice(tt * 512, (tt + 1) * 512)
                psO = psBig[:, (idx % 4) * 512:(idx % 4) * 512 + 512]
                psZ2 = psB2[:, (idx % 2) * 512:(idx % 2) * 512 + 512]
                sZ = sD if idx % 2 == 0 else sE
                w0 = [("v", v_ksumC), ("a", a_q_done)] if idx == 0 else []
                if addz_done[idx % 2] is not None:
                    w0.append(("a", addz_done[idx % 2]))
                E("tensor", lambda h, p=p, tsl=tsl, psZ2=psZ2: h.matmul(
                    psZ2[0:64, :], ksumC[0:64, p, :], qp[0:64, p, tsl],
                    start=True, stop=True, tile_position=(0, 0)), wait=w0 or None)
                E("tensor", lambda h, p=p, tsl=tsl, psZ2=psZ2: h.matmul(
                    psZ2[64:128, :], ksumC[64:128, p, :], qp[64:128, p, tsl],
                    start=True, stop=True, tile_position=(64, 64)), inc=("t", 1))
                t_z = P.val("t")
                w1 = []
                if mult_done[idx % 4] is not None:
                    w1.append(("v", mult_done[idx % 4]))
                E("tensor", lambda h, p=p, tsl=tsl, psO=psO: h.matmul(
                    psO[0:64, :], kvs[0:64, p * 130:p * 130 + 64], qp[0:64, p, tsl],
                    start=True, stop=True, tile_position=(0, 0)), wait=w1 or None)
                E("tensor", lambda h, p=p, tsl=tsl, psO=psO: h.matmul(
                    psO[64:128, :], kvs[64:128, p * 130 + 65:p * 130 + 129], qp[64:128, p, tsl],
                    start=True, stop=True, tile_position=(64, 64)), inc=("t", 1))
                t_o = P.val("t")
                E("scalar", lambda h, psZ2=psZ2, sZ=sZ: h.activation(
                    out=sZ[:, :], in_=psZ2[:, :], func=AF.Ln, bias=1e-6),
                  wait=[("t", t_z)])
                E("scalar", lambda h, sZ=sZ: h.activation(
                    out=sZ[:, :], in_=sZ[:, :], func=AF.Exp, scale=-1.0), inc=("a", 1))
                addz_done[idx % 2] = P.val("a")
                a_z = addz_done[idx % 2]
                E("vector", lambda h, p=p, tsl=tsl, psO=psO, sZ=sZ: h.tensor_tensor(
                    out=qp[:, p, tsl], in0=psO[:, :], in1=sZ[:, :], op=OP.mult),
                  wait=[("t", t_o), ("a", a_z)], inc=("v", 1))
                mult_done[idx % 4] = P.val("v")
        v_o_done = P.val("v")

        t_now = P.val("t")
        for c in range(8):
            E("gpsimd", lambda h, c=c: h.dma_start(out=kp[:, c, :],
                                                   in_=woutT[c * 128:(c + 1) * 128, :]),
              wait=[("t", t_now), ("v", v_o_done)] if c == 0 else None, inc=("w", 16))
            E("gpsimd", lambda h, c=c: h.dma_start(out=kp[:, 8 + c, :],
                                                   in_=cbT[c * 128:(c + 1) * 128, :]),
              inc=("w", 16))
        w_epi = P.val("w")
        epi_free = None
        epi_v = None
        for tt in range(NTT):
            tsl = slice(tt * 512, (tt + 1) * 512)
            st_now = P.val("st")
            for c in range(8):
                w0 = [("st", st_now), ("a", a_q_done)] if c == 0 else None
                E("sync", lambda h, c=c, s=s, tsl=tsl: h.dma_start(
                    out=xtile[:, c, :], in_=xT[s, c * 128:(c + 1) * 128, tsl]),
                  wait=w0, inc=("ld", 16))
            ld_x2 = P.val("ld")
            for m in range(8):
                for c in range(8):
                    w0 = None
                    if c == 0:
                        w0 = [("w", w_epi)]
                        if epi_free is not None:
                            w0.append(("a", epi_free))
                    E("tensor", lambda h, c=c, m=m, tsl=tsl: h.matmul(
                        psQ1[:, :], kp[:, c, m * 128:(m + 1) * 128], qp[:, c, tsl],
                        start=(c == 0), stop=(c == 7)), wait=w0)
                    E("tensor", lambda h, c=c, m=m, tsl=tsl: h.matmul(
                        psQ2[:, :], kp[:, 8 + c, m * 128:(m + 1) * 128], xn[:, c, tsl],
                        start=(c == 0), stop=(c == 7)),
                        inc=("t", 1) if c == 7 else None)
                t_epi = P.val("t")
                w_ea = [("t", t_epi), ("v", v_scb)]
                if epi_v is not None:
                    w_ea.append(("v", epi_v))
                E("scalar", lambda h, m=m: h.activation(out=ea[:, :], in_=psQ1[:, :], func=AF.Identity,
                                                        bias=boutS[:, m:m + 1]),
                  wait=w_ea, inc=("a", 1))
                E("scalar", lambda h, m=m: h.activation(out=ef[:, :], in_=psQ2[:, :], func=AF.Identity,
                                                        scale=scbS[:, m:m + 1], bias=fbS[:, m:m + 1]),
                  wait=[("v", v_scb)], inc=("a", 1))
                epi_free = P.val("a")
                a_e = P.val("a")
                E("vector", lambda h, m=m: h.tensor_tensor(out=xtile[:, m, :], in0=xtile[:, m, :],
                                                           in1=ea[:, :], op=OP.add),
                  wait=[("a", a_e), ("ld", ld_x2)])
                E("vector", lambda h, m=m: h.tensor_tensor(out=xtile[:, m, :], in0=xtile[:, m, :],
                                                           in1=ef[:, :], op=OP.add), inc=("v", 1))
                epi_v = P.val("v")
                v_e = P.val("v")
                E("sync", lambda h, m=m, s=s, tsl=tsl: h.dma_start(
                    out=x1T[s, m * 128:(m + 1) * 128, tsl], in_=xtile[:, m, :]),
                  wait=[("v", v_e)], inc=("st", 16))
        ln_war = P.val("t")





def emit_ln(P, nc, xT, s, xtile, xb, sqb, sD, sE, sF, sG, xn, ones128, psSt, state, W=512):
    E = P.emit
    for tt in range(T // W):
        tsl = slice(tt * W, (tt + 1) * W)
        st_now = P.val("st")
        v_now = P.val("v")
        for c in range(8):
            E("sync", lambda h, c=c, s=s, tsl=tsl: h.dma_start(
                out=xtile[:, c, :], in_=xT[s, c * 128:(c + 1) * 128, tsl]),
              wait=[("st", st_now), ("v", v_now)] if c == 0 else None, inc=("ld", 16))
        ld_x = P.val("ld")
        for c in range(8):
            w0 = [("ld", ld_x), ("t", state.get("ln_war", 0))] if c == 0 else None
            E("vector", lambda h, c=c: h.tensor_copy(out=xb[:, c, :], in_=xtile[:, c, :]), wait=w0)
            E("vector", lambda h, c=c: h.tensor_tensor(
                out=sqb[:, c, :], in0=xb[:, c, :], in1=xb[:, c, :], op=OP.mult),
              inc=("v", 1) if c == 7 else None)
        v_cast = P.val("v")
        for c in range(8):
            E("tensor", lambda h, c=c: h.matmul(psSt[:, 0:W], ones128[:, :], xb[:, c, :],
                                                start=(c == 0), stop=(c == 7)),
              wait=[("v", v_cast), ("a", state.get("ps_war", 0))] if c == 0 else None)
        for c in range(8):
            E("tensor", lambda h, c=c: h.matmul(psSt[:, 512:512 + W], ones128[:, :], sqb[:, c, :],
                                                start=(c == 0), stop=(c == 7)),
              inc=("t", 1) if c == 7 else None)
        t_stats = P.val("t")
        state["ln_war"] = t_stats
        E("vector", lambda h: h.tensor_scalar(out=sD[:, 0:W], in0=psSt[:, 0:W],
                                              scalar1=1.0 / 1024.0, scalar2=None, op0=OP.mult),
          wait=[("t", t_stats)])
        E("vector", lambda h: h.tensor_scalar(out=sG[:, 0:W], in0=psSt[:, 512:512 + W],
                                              scalar1=1.0 / 1024.0, scalar2=None, op0=OP.mult))
        E("vector", lambda h: h.tensor_tensor(out=sE[:, 0:W], in0=sD[:, 0:W], in1=sD[:, 0:W],
                                              op=OP.mult))
        E("vector", lambda h: h.tensor_tensor(out=sE[:, 0:W], in0=sG[:, 0:W], in1=sE[:, 0:W],
                                              op=OP.subtract), inc=("v", 1))
        v_var = P.val("v")
        E("scalar", lambda h: h.activation(out=sG[:, 0:W], in_=sE[:, 0:W], func=AF.Ln, bias=1e-5),
          wait=[("v", v_var)], inc=("a", 1))
        a_ln = P.val("a")
        E("scalar", lambda h: h.activation(out=sE[:, 0:W], in_=sG[:, 0:W], func=AF.Exp, scale=-0.5),
          wait=[("a", a_ln)], inc=("a", 1))
        a_rstd = P.val("a")
        E("vector", lambda h: h.tensor_tensor(out=sF[:, 0:W], in0=sD[:, 0:W], in1=sE[:, 0:W],
                                              op=OP.mult),
          wait=[("a", a_rstd)])
        for c in range(8):
            E("vector", lambda h, c=c: h.tensor_tensor(out=xtile[:, c, :], in0=xtile[:, c, :],
                                                       in1=sE[:, 0:W], op=OP.mult))
            E("vector", lambda h, c=c, tsl=tsl: h.tensor_tensor(
                out=xn[:, c, tsl], in0=xtile[:, c, :], in1=sF[:, 0:W], op=OP.subtract),
              inc=("v", 1) if c == 7 else None)
    return P.val("v")


def emit_normalize(P, ps, zps, dnb, rcb, zbb, out_ap, ones1f, t_o, eps):
    E = P.emit
    E("vector", lambda h, ps=ps, dnb=dnb: h.tensor_copy(out=dnb[0:1, :], in_=ps[64:65, :]),
      wait=[("t", t_o)], inc=("v", 1))
    v_dn = P.val("v")
    E("scalar", lambda h, dnb=dnb, rcb=rcb: h.activation(
        out=rcb[0:1, :], in_=dnb[0:1, :], func=AF.Ln, bias=eps),
      wait=[("v", v_dn)], inc=("a", 1))
    a_ln2 = P.val("a")
    E("scalar", lambda h, rcb=rcb: h.activation(
        out=rcb[0:1, :], in_=rcb[0:1, :], func=AF.Exp, scale=-1.0),
      wait=[("a", a_ln2)], inc=("a", 1))
    a_rc = P.val("a")
    E("tensor", lambda h, rcb=rcb, zps=zps: h.matmul(
        zps[0:64, :], ones1f[0:1, 0:64], rcb[0:1, :], start=True, stop=True),
      wait=[("a", a_rc)], inc=("t", 1))
    t_z = P.val("t")
    E("vector", lambda h, zbb=zbb, zps=zps: h.tensor_copy(out=zbb[:, :], in_=zps[0:64, :]),
      wait=[("t", t_z)], inc=("v", 1))
    E("vector", lambda h, ps=ps, zbb=zbb, out_ap=out_ap: h.tensor_tensor(
        out=out_ap, in0=ps[0:64, :], in1=zbb[:, :], op=OP.mult), inc=("v", 1))
    return P.val("v")


DBGP = 1

def emit_l1a(nc, P, d, nseq=2):
    xT = d["xT"]
    wqT = d["wqT"]
    wkT = d["wkT"]
    wvT = d["wvT"]
    woT = d["woT"]
    bqf = d["bqf"]
    bkf = d["bkf"]
    bvr = d["bvr"]
    bof = d["bof"]
    attnT = d["attnT"]

    xn = P.sbuf("xn", [128, NCH, T], BF16)
    wq = P.sbuf("wq", [128, NCH, 256], BF16)
    wk = P.sbuf("wk", [128, NCH, 256], BF16)
    wv = P.sbuf("wv", [128, NCH, 1024], BF16)
    qf = P.sbuf("qf", [128, T], BF16)
    kf = P.sbuf("kf", [128, T], BF16)
    vt2 = P.sbuf("vt2", [128, 16, 448], BF16)
    PT1 = P.sbuf("PT1", [128, 8192], BF16)
    PT2 = P.sbuf("PT2", [128, 8192], BF16)
    PT3 = P.sbuf("PT3", [128, 8192], BF16)
    PT4 = P.sbuf("PT4", [128, 8192], BF16)
    Obuf = P.sbuf("Obuf", [128, NCH, T], BF16)
    U = P.sbuf("U", [128, NCH, 256], F32)
    SP = P.sbuf("SP", [128, 4096], BF16)
    sD = P.sbuf("sD", [128, 512], F32)
    sE = P.sbuf("sE", [128, 512], F32)
    sF = P.sbuf("sF", [128, 512], F32)
    sG = P.sbuf("sG", [128, 512], F32)
    zbuf = P.sbuf("zbuf", [128, 512], F32)
    bvB = P.sbuf("bvB", [128, 1024], BF16)
    ones128 = P.sbuf("ones128", [128, 128], BF16)
    ones1f = P.sbuf("ones1f", [1, 128], F32)
    bqS = P.sbuf("bqS", [128, 8], F32)
    bkS = P.sbuf("bkS", [128, 8], F32)
    boS = P.sbuf("boS", [128, 8], F32)
    ident = P.sbuf("ident", [128, 128], F32)
    swapF = P.sbuf("swapF", [128, 128], F32)
    mx = P.sbuf("mx", [128, 2], F32)
    mrow = P.sbuf("mrow", [1, 128], F32)
    mrow2 = P.sbuf("mrow2", [1, 128], F32)
    cb = P.sbuf("cb", [128, 1], F32)

    xtile = U
    xb = SP[:, 0:2048].rearrange("p (a b) -> p a b", b=256)
    sqb = SP[:, 2048:4096].rearrange("p (a b) -> p a b", b=256)
    bvF = PT1[:, :].bitcast(F32)[0:1, 0:1024]

    psA = P.psum("psA", [128, 2048])
    psB = P.psum("psB", [128, 2048])
    SLOT = [psA[:, 0:1024], psA[:, 1024:2048]]
    psAVE = psB[:, 1024:1536]
    psAVO = psB[:, 1536:2048]
    psW = psB[:, 0:512]
    psTR1 = psB[0:1, 1024:1152]
    psTR2 = psB[0:1, 1152:1280]
    psCB = psB[:, 1280:1281]

    E = P.emit
    state = {}

    E("vector", lambda h: h.memset(ones128[:, :], 1.0))
    E("vector", lambda h: h.memset(ones1f[:, :], 1.0))
    E("gpsimd", lambda h: h.memset(ident[:, :], 0.0), inc=("g", 1))
    E("gpsimd", lambda h: h.affine_select(out=ident[:, :], in_=ident[:, :],
                                          compare_op=OP.not_equal, fill=1.0, base=0,
                                          pattern=[[-1, 128]], channel_multiplier=1),
      wait=[("g", P.val("g"))], inc=("g", 1))
    g_id = P.val("g")
    E("gpsimd", lambda h: h.memset(swapF[:, :], 0.0))
    E("gpsimd", lambda h: h.affine_select(out=swapF[:, :], in_=swapF[:, :],
                                          compare_op=OP.not_equal, fill=1.0, base=64,
                                          pattern=[[-1, 128]], channel_multiplier=1))
    E("gpsimd", lambda h: h.affine_select(out=swapF[:, :], in_=swapF[:, :],
                                          compare_op=OP.not_equal, fill=1.0, base=-64,
                                          pattern=[[-1, 128]], channel_multiplier=1),
      inc=("g", 1))
    g_sw = P.val("g")
    vt7 = vt2[:, :, :].rearrange("p m (k w) -> p m k w", w=64)
    E("vector", lambda h: h.memset(vt7[:, :, 0:1, :], 1.0))
    E("vector", lambda h: h.memset(vt7[:, :, 3:4, :], 1.0))
    E("vector", lambda h: h.memset(vt7[:, :, 6:7, :], 1.0), inc=("v", 1))
    E("sync", lambda h: h.dma_start(out=bqS[:, :], in_=bqf), inc=("lp", 16))
    E("sync", lambda h: h.dma_start(out=bkS[:, :], in_=bkf), inc=("lp", 16))
    E("sync", lambda h: h.dma_start(out=boS[:, :], in_=bof), inc=("lp", 16))
    E("sync", lambda h: h.dma_start(out=bvF, in_=bvr), inc=("lp", 16))
    lp = P.val("lp")
    E("vector", lambda h: h.tensor_scalar(out=bqS[:, :], in0=bqS[:, :], scalar1=0.125,
                                          scalar2=None, op0=OP.mult),
      wait=[("lp", lp)], inc=("v", 1))
    v_params = P.val("v")
    for c in range(8):
        E("gpsimd", lambda h, c=c: h.dma_start(out=wq[:, c, 0:128],
                                               in_=wqT[c * 128:(c + 1) * 128, 0:128]),
          inc=("w", 16))
        E("gpsimd", lambda h, c=c: h.dma_start(out=wk[:, c, 0:128],
                                               in_=wkT[c * 128:(c + 1) * 128, 0:128]),
          inc=("w", 16))
    w_pair = {0: P.val("w")}
    tqp_hist = {}
    tkp_hist = {}
    E("tensor", lambda h: h.matmul(psB[:, 0:512], ones1f[0:1, 0:128], bvF[:, 0:512],
                                   start=True, stop=True),
      wait=[("lp", lp), ("v", v_params)])
    E("tensor", lambda h: h.matmul(psB[:, 512:1024], ones1f[0:1, 0:128], bvF[:, 512:1024],
                                   start=True, stop=True), inc=("t", 1))
    t_bvb = P.val("t")
    E("vector", lambda h: h.tensor_copy(out=bvB[:, :], in_=psB[:, 0:1024]),
      wait=[("t", t_bvb)], inc=("v", 1))
    v_bvb = P.val("v")

    hz = {
        "slot": [0, 0],
        "t_pt": [0, 0],
        "t_avden": 0,
        "v_norm": 0,
        "a_rc": 0,
        "t_swap": 0,
        "a_epi": 0,
    }

    for s in range(nseq):
        t_wv = P.val("t")
        for c in range(8):
            E("gpsimd", lambda h, c=c: h.dma_start(out=wv[:, c, :],
                                                   in_=wvT[c * 128:(c + 1) * 128, :]),
              wait=[("t", t_wv)] if (c == 0 and s > 0) else None, inc=("w", 16))
        w_v = P.val("w")

        v_xn = emit_ln(P, nc, xT, s, xtile, xb, sqb, sD, sE, sF, sG, xn, ones128, psA, state,
                       W=256)
        for g in range(4):
            pend0 = state.pop("l1a_pend", None) or []
            while pend0:
                pend0.pop(0)()
            fsl = slice(g * 256, (g + 1) * 256)
            v_evac_prev = [None, None]
            for ts in range(16):
                ssl = slice(ts * 128, (ts + 1) * 128)
                psV = psB[:, (ts % 2) * 512:(ts % 2) * 512 + 256]
                w0 = []
                if ts == 0:
                    w0 += [("v", v_xn), ("w", w_v), ("a", hz["a_rc"]),
                           ("v", hz["v_norm"]), ("v", v_bvb)]
                if v_evac_prev[ts % 2] is not None:
                    w0.append(("v", v_evac_prev[ts % 2]))
                for c in range(8):
                    E("tensor", lambda h, c=c, ssl=ssl, fsl=fsl, psV=psV: h.matmul(
                        psV, xn[:, c, ssl], wv[:, c, fsl], start=(c == 0), stop=(c == 7)),
                        wait=(w0 or None) if c == 0 else None,
                        inc=("t", 1) if c == 7 else None)
                t_v = P.val("t")
                w1 = [("t", t_v)]
                if ts == 0:
                    w1.append(("t", hz["t_avden"]))
                E("vector", lambda h, ts=ts, psV=psV, g=g: h.tensor_tensor(
                    out=vt2[:, ts, 64:192], in0=psV[:, 0:128],
                    in1=bvB[:, g * 256:g * 256 + 128], op=OP.add),
                  wait=w1)
                E("vector", lambda h, ts=ts, psV=psV, g=g: h.tensor_tensor(
                    out=vt2[:, ts, 256:384], in0=psV[:, 128:256],
                    in1=bvB[:, g * 256 + 128:g * 256 + 256], op=OP.add),
                  inc=("v", 1))
                v_evac_prev[ts % 2] = P.val("v")
            v_vt2 = P.val("v")

            for pp in range(2):
                p = 2 * g + pp
                vi = s * 8 + p
                sl = vi % 2
                pend_prev = state.pop("l1a_pend", None) or []
                if vi + 1 < nseq * 8:
                    pn = (vi + 1) % 8
                    sln = (vi + 1) % 2
                    for c in range(8):
                        w0 = [("t", tqp_hist.get(vi - 1, 0))] if c == 0 else None
                        E("gpsimd", lambda h, c=c, pn=pn, sln=sln: h.dma_start(
                            out=wq[:, c, sln * 128:(sln + 1) * 128],
                            in_=wqT[c * 128:(c + 1) * 128, pn * 128:(pn + 1) * 128]),
                          wait=w0, inc=("w", 16))
                    for c in range(8):
                        w0 = [("t", tkp_hist.get(vi - 1, 0))] if c == 0 else None
                        E("gpsimd", lambda h, c=c, pn=pn, sln=sln: h.dma_start(
                            out=wk[:, c, sln * 128:(sln + 1) * 128],
                            in_=wkT[c * 128:(c + 1) * 128, pn * 128:(pn + 1) * 128]),
                          wait=w0, inc=("w", 16))
                    w_pair[vi + 1] = P.val("w")
                for g4 in range(4):
                    for c in range(8):
                        w0 = None
                        if g4 == 0 and c == 0:
                            w0 = [("w", w_pair[vi]), ("a", max(hz["slot"][0], hz["slot"][1])),
                                  ("a", hz["a_epi"])]
                        E("tensor", lambda h, c=c, g4=g4, sl=sl: h.matmul(
                            psA[:, g4 * 512:(g4 + 1) * 512], wq[:, c, sl * 128:(sl + 1) * 128],
                            xn[:, c, g4 * 512:(g4 + 1) * 512],
                            start=(c == 0), stop=(c == 7)), wait=w0,
                            inc=("t", 1) if (c == 7 and g4 == 3) else None)
                    for _ in range(4):
                        if pend_prev:
                            pend_prev.pop(0)()
                while pend_prev:
                    pend_prev.pop(0)()
                t_qp = P.val("t")
                tqp_hist[vi] = t_qp
                E("vector", lambda h, p=p: h.tensor_scalar(
                    out=qf[:, :], in0=psA[:, 0:2048], scalar1=0.125,
                    scalar2=bqS[:, p:p + 1], op0=OP.mult, op1=OP.add),
                  wait=[("t", t_qp), ("v", v_params)], inc=("v", 1))
                v_qe = P.val("v")
                for g4 in range(4):
                    for c in range(8):
                        w0 = None
                        if g4 == 0 and c == 0:
                            w0 = [("v", max(v_vt2, hz["v_norm"])),
                                  ("a", hz["a_rc"])]
                        E("tensor", lambda h, c=c, g4=g4, sl=sl: h.matmul(
                            psB[:, g4 * 512:(g4 + 1) * 512], wk[:, c, sl * 128:(sl + 1) * 128],
                            xn[:, c, g4 * 512:(g4 + 1) * 512],
                            start=(c == 0), stop=(c == 7)), wait=w0,
                            inc=("t", 1) if (c == 7 and g4 == 3) else None)
                t_kp = P.val("t")
                tkp_hist[vi] = t_kp
                E("vector", lambda h, p=p: h.tensor_scalar(
                    out=kf[:, :], in0=psB[:, 0:2048], scalar1=bkS[:, p:p + 1],
                    scalar2=None, op0=OP.add),
                  wait=[("t", t_kp)], inc=("v", 1))
                v_ke = P.val("v")
                E("vector", lambda h: h.tensor_reduce(out=mx[:, 0:1], in_=qf[:, :], axis=AX.X,
                                                      op=OP.max, apply_absolute_value=True))
                E("vector", lambda h: h.tensor_reduce(out=mx[:, 1:2], in_=kf[:, :], axis=AX.X,
                                                      op=OP.max, apply_absolute_value=True),
                  inc=("v", 1))
                v_mx = P.val("v")
                E("tensor", lambda h: h.transpose(psTR1, mx[:, 0:1], ident[:, :]),
                  wait=[("v", v_mx), ("g", g_id)])
                E("tensor", lambda h: h.transpose(psTR2, mx[:, 1:2], ident[:, :]),
                  inc=("t", 1))
                t_tr = P.val("t")
                E("vector", lambda h: h.tensor_copy(out=mrow[0:1, :], in_=psTR1),
                  wait=[("t", t_tr)])
                E("vector", lambda h: h.tensor_copy(out=mrow2[0:1, :], in_=psTR2))
                E("vector", lambda h: h.tensor_reduce(out=mrow[0:1, 0:1], in_=mrow[0:1, 0:128],
                                                      axis=AX.X, op=OP.max))
                E("vector", lambda h: h.tensor_reduce(out=mrow2[0:1, 0:1], in_=mrow2[0:1, 0:128],
                                                      axis=AX.X, op=OP.max))
                E("vector", lambda h: h.tensor_tensor(out=mrow[0:1, 0:1], in0=mrow[0:1, 0:1],
                                                      in1=mrow2[0:1, 0:1], op=OP.mult))
                E("vector", lambda h: h.tensor_scalar(out=mrow[0:1, 0:1], in0=mrow[0:1, 0:1],
                                                      scalar1=-64.0, scalar2=None, op0=OP.mult),
                  inc=("v", 1))
                v_negc = P.val("v")
                E("tensor", lambda h: h.matmul(psCB, ones1f[0:1, :], mrow[0:1, 0:1],
                                               start=True, stop=True),
                  wait=[("v", v_negc)], inc=("t", 1))
                t_cb = P.val("t")
                E("vector", lambda h: h.tensor_copy(out=cb[:, :], in_=psCB),
                  wait=[("t", t_cb)], inc=("v", 1))
                v_cb = P.val("v")

                PTbufs = [(PT1, PT2), (PT3, PT4)]

                def mk_pend(qb, pp=pp, p=p):
                    PTe, PTo = PTbufs[qb % 2]
                    a14 = hz["slot"][0]
                    a15 = hz["slot"][1]
                    v_norm_gate = hz["v_norm"]
                    v_dve_gate = v_cb
                    qsl_ = slice(qb * 512, (qb + 1) * 512)
                    thunks = []

                    def quad(m, a14=a14, a15=a15, v_norm_gate=v_norm_gate,
                             v_dve_gate=v_dve_gate, PTe=PTe, PTo=PTo, qb=qb):
                        w0h0 = w0h1 = None
                        if m == 0:
                            w0h0 = [("a", a14), ("v", v_dve_gate), ("v", v_norm_gate)]
                            w0h1 = [("a", a15)]
                        E("tensor", lambda h, m=m: h.matmul(
                            psAVE[:, :], vt2[:, m, pp * 192:pp * 192 + 128],
                            PTe[:, m * 512:(m + 1) * 512],
                            start=(m == 0), stop=(m == 15)),
                          wait=w0h0)
                        E("tensor", lambda h, m=m: h.matmul(
                            psAVO[:, :], vt2[:, m, pp * 192 + 128:pp * 192 + 256],
                            PTo[:, m * 512:(m + 1) * 512],
                            start=(m == 0), stop=(m == 15)),
                          wait=w0h1, inc=("t", 1) if m == 15 else None)
                        if m == 15:
                            tv = P.val("t")
                            hz["t_pt"][qb % 2] = tv
                            hz["t_avden"] = tv

                    for m in range(16):
                        thunks.append(lambda m=m: quad(m))

                    def tail(qb=qb, p=p, qsl_=qsl_):
                        t_qd = hz["t_avden"]
                        v_ng = hz["v_norm"]
                        dnb = sD
                        E("vector", lambda h: h.tensor_copy(
                            out=dnb[0:64, :], in_=psAVE[0:64, :]),
                          wait=[("t", t_qd), ("t", hz["t_swap"]),
                                ("st", P.val("st"))])
                        E("vector", lambda h: h.tensor_copy(
                            out=dnb[64:128, :], in_=psAVO[64:128, :]), inc=("v", 1))
                        v_dnb = P.val("v")
                        E("tensor", lambda h: h.matmul(
                            psW[:, :], swapF[:, :], dnb[:, :], start=True, stop=True),
                          wait=[("v", v_dnb), ("a", hz["a_rc"]), ("g", g_sw)],
                          inc=("t", 1))
                        hz["t_swap"] = P.val("t")
                        t_sw = hz["t_swap"]
                        E("scalar", lambda h: h.activation(
                            out=zbuf[:, :], in_=psW[:, :], func=AF.Ln, bias=1e-30),
                          wait=[("t", t_sw), ("v", v_ng)], inc=("a", 1))
                        hz["a_rc"] = P.val("a")
                        E("scalar", lambda h: h.activation(
                            out=zbuf[:, :], in_=zbuf[:, :], func=AF.Exp, scale=-1.0),
                          inc=("a", 1))
                        a_rc = P.val("a")
                        E("vector", lambda h: h.tensor_tensor(
                            out=Obuf[0:64, p, qsl_], in0=psAVO[0:64, :], in1=zbuf[0:64, :],
                            op=OP.mult),
                          wait=[("t", t_qd), ("a", a_rc)])
                        E("vector", lambda h: h.tensor_tensor(
                            out=Obuf[64:128, p, qsl_], in0=psAVE[64:128, :],
                            in1=zbuf[64:128, :], op=OP.mult), inc=("v", 1))
                        hz["v_norm"] = P.val("v")
                    thunks.append(tail)
                    return thunks

                pend = state.get("l1a_pend") or []
                for qb in range(NTT):
                    qsl = slice(qb * 512, (qb + 1) * 512)
                    a_slot = [0] * 16
                    for rr in range(8):
                        if rr > 0:
                            aw = (a_slot[2 * (rr - 1)], a_slot[2 * (rr - 1) + 1])
                        else:
                            aw = (hz["slot"][0], hz["slot"][1])
                        w0h = [[("a", aw[0])], [("a", aw[1])]]
                        if rr == 0 and qb == 0:
                            w0h[0].append(("v", v_ke))
                        for ii, (hh, j) in enumerate(((0, 0), (1, 0), (0, 1), (1, 1))):
                            m = 2 * rr + j
                            base = hh * 64
                            ps = SLOT[hh]
                            E("tensor", lambda h, m=m, j=j, ps=ps, base=base, qsl=qsl:
                              h.matmul(ps[:, j * 512:(j + 1) * 512],
                                       kf[base:base + 64, m * 128:(m + 1) * 128],
                                       qf[base:base + 64, qsl], start=True, stop=True,
                                       tile_position=(base, 0)),
                              wait=w0h[hh] if j == 0 else None,
                              inc=("t", 1) if ii in (2, 3) else None)
                        t_sc = P.val("t")
                        t_h0 = t_sc - 1
                        for hh in (0, 1):
                            PT = PTbufs[qb % 2][hh]
                            ps = SLOT[hh]
                            w0e = [("t", t_h0 if hh == 0 else t_sc)]
                            if rr == 0 and hh == 0:
                                w0e.append(("v", v_cb))
                                w0e.append(("t", hz["t_pt"][qb % 2]))
                            E("scalar", lambda h, rr=rr, ps=ps, PT=PT: h.activation(
                                out=PT[:, rr * 1024:(rr + 1) * 1024], in_=ps[:, 0:1024],
                                func=AF.Exp, bias=cb[:, 0:1]),
                              wait=w0e, inc=("a", 1))
                            a_slot[2 * rr + hh] = P.val("a")
                            hz["slot"][hh] = a_slot[2 * rr + hh]
                        for _ in range(2):
                            if pend:
                                pend.pop(0)()
                        if rr == 7:
                            while pend:
                                pend.pop(0)()
                    pend = mk_pend(qb)
                state["l1a_pend"] = pend

        pendx = state.pop("l1a_pend", None) or []
        while pendx:
            pendx.pop(0)()
        v_obuf = P.val("v")
        t_now2 = P.val("t")
        for c in range(8):
            E("gpsimd", lambda h, c=c: h.dma_start(out=wv[0:64, c, :],
                                                   in_=woT[c * 128 + 64:(c + 1) * 128, :]),
              wait=[("t", t_now2)] if c == 0 else None, inc=("w", 16))
            E("gpsimd", lambda h, c=c: h.dma_start(out=wv[64:128, c, :],
                                                   in_=woT[c * 128:c * 128 + 64, :]),
              inc=("w", 16))
        w_o = P.val("w")
        a_evac = [None, None]
        for tt in range(NTT):
            tsl = slice(tt * 512, (tt + 1) * 512)
            for m in range(8):
                mi = tt * 8 + m
                psE_ = psA[:, (mi % 2) * 512:(mi % 2) * 512 + 512]
                w0 = None
                if mi == 0:
                    w0 = [("w", w_o), ("v", v_obuf),
                          ("a", max(hz["slot"][0], hz["slot"][1]))]
                elif a_evac[mi % 2] is not None:
                    w0 = [("a", a_evac[mi % 2])]
                for c in range(8):
                    E("tensor", lambda h, c=c, m=m, tsl=tsl, psE_=psE_: h.matmul(
                        psE_, wv[:, c, m * 128:(m + 1) * 128], Obuf[:, c, tsl],
                        start=(c == 0), stop=(c == 7)), wait=w0 if c == 0 else None,
                        inc=("t", 1) if c == 7 else None)
                t_ep = P.val("t")
                w_ea = [("t", t_ep)]
                if "epi_st" in state:
                    w_ea.append(("st", state["epi_st"]))
                eb = sD if mi % 2 == 0 else sE
                E("scalar", lambda h, psE_=psE_, eb=eb, m=m: h.activation(
                    out=eb[:, :], in_=psE_, func=AF.Identity, bias=boS[:, m:m + 1]),
                  wait=w_ea, inc=("a", 1))
                a_evac[mi % 2] = P.val("a")
                a_ev = P.val("a")
                E("sync", lambda h, m=m, s=s, tsl=tsl, eb=eb: h.dma_start(
                    out=attnT[s, m * 128:(m + 1) * 128, tsl], in_=eb[:, :]),
                  wait=[("a", a_ev)], inc=("st", 16))
                state["epi_st"] = P.val("st")
        hz["a_epi"] = P.val("a")
        state["ps_war"] = hz["a_epi"]




NPAIR_F = 4
NQ = 4
QT = 512
NCK = 4





def emit_l1f(nc, P, d, nseq=2, dbg=False):
    x1T = d["x1T"]
    attnT = d["attnT"]
    wqD = d["wq"]
    wkD = d["wk"]
    wvD = d["wv"]
    wqaD = d["wqa"]
    wqbD = d["wqb"]
    wkaD = d["wka"]
    wkbD = d["wkb"]
    wg1aD = d["wg1a"]
    wg1bD = d["wg1b"]
    bg1D = d["bg1"]
    wg2aD = d["wg2a"]
    wg2bD = d["wg2b"]
    bg2D = d["bg2"]
    weD = d["we"]
    wgaD = d["wga"]
    wgbD = d["wgb"]
    nwD = d["nw"]
    woD = d["wo"]
    out2T = d["out2T"]
    dbgT = {}
    if dbg:
        for nm, shp, dt in (("q1f", [128, 4, 512], BF16), ("q2f", [128, 4, 512], BF16),
                            ("k1f", [128, 4, 512], BF16), ("k2f", [128, 4, 512], BF16),
                            ("k1tm", [128, 4, 512], BF16), ("k2tm", [128, 4, 512], BF16),
                            ("vt", [128, 4, 512], BF16), ("gk1tm", [128, 4, 512], BF16),
                            ("gk2tm", [128, 4, 512], BF16), ("rwm", [128, 4, 8], F32),
                            ("rwb", [128, 4, 512], BF16), ("sgf", [128, 4, 512], BF16),
                            ("gkc", [128, 640], BF16), ("expg", [128, 640], BF16),
                            ("netm", [128, 640], BF16), ("qcat", [128, 640], BF16),
                            ("kcat", [128, 640], BF16), ("kctm", [128, 640], BF16),
                            ("dec", [128, 5], F32), ("Am", [128, 256], BF16),
                            ("ob", [128, 4, 512], BF16), ("Sst", [128, 4, 5, 128], F32),
                            ("Sbf", [128, 5, 128], BF16)):
            dbgT[nm] = nc.dram_tensor("d_" + nm, shp, dt, kind="ExternalOutput").ap()

    xn = P.sbuf("xn", [128, 8, T], BF16)
    wq = P.sbuf("wqS", [128, 8, 512], BF16)
    wk = P.sbuf("wkS", [128, 8, 512], BF16)
    wv = P.sbuf("wvS", [128, 8, 512], BF16)
    wqa = P.sbuf("wqaS", [128, 8, 64], BF16)
    wka = P.sbuf("wkaS", [128, 8, 64], BF16)
    wga = P.sbuf("wgaS", [128, 8, 64], BF16)
    wqb = P.sbuf("wqbS", [64, 512], BF16)
    wkb = P.sbuf("wkbS", [64, 512], BF16)
    wgb = P.sbuf("wgbS", [64, 512], BF16)
    wg1a = P.sbuf("wg1aS", [128, 8, 16], BF16)
    wg2a = P.sbuf("wg2aS", [128, 8, 16], BF16)
    wg1b = P.sbuf("wg1bS", [16, 512], BF16)
    wg2b = P.sbuf("wg2bS", [16, 512], BF16)
    weS = P.sbuf("weS", [128, 8, 4], BF16)
    wo = P.sbuf("woS", [128, 4, 1024], BF16)
    bg1S = P.sbuf("bg1S", [1, 512], BF16)
    bg2S = P.sbuf("bg2S", [1, 512], BF16)
    bgF = P.sbuf("bgF", [1, 1024], F32)
    nwS = P.sbuf("nwS", [128, 1], F32)
    q1f = P.sbuf("q1f", [128, 4, QT], BF16)
    q2f = P.sbuf("q2f", [128, 4, QT], BF16)
    k1f = P.sbuf("k1f", [128, 4, QT], BF16)
    k2f = P.sbuf("k2f", [128, 4, QT], BF16)
    k1tm = P.sbuf("k1tm", [128, 4, 512], BF16)
    k2tm = P.sbuf("k2tm", [128, 4, 512], BF16)
    vt = P.sbuf("vt", [128, 4, 512], BF16)
    gk1tm = P.sbuf("gk1tm", [128, 4, 512], BF16)
    gk2tm = P.sbuf("gk2tm", [128, 4, 512], BF16)
    gf = P.sbuf("gf", [128, 4, QT], BF16)
    sgf = P.sbuf("sgf", [128, 4, QT], BF16)
    ob = P.sbuf("ob", [128, 4, QT], BF16)
    rwb = P.sbuf("rwb", [128, 4, 512], BF16)
    rwm = P.sbuf("rwm", [128, 4, 8], F32)
    xaf = P.sbuf("xaf", [64, 512], BF16)
    gaf = P.sbuf("gaf", [16, 1024], BF16)
    expg = P.sbuf("expg", [128, 640], BF16)
    negg = P.sbuf("negg", [128, 640], BF16)
    netm = P.sbuf("netm", [128, 640], BF16)
    qcat = P.sbuf("qcat", [128, 640], BF16)
    kcat = P.sbuf("kcat", [128, 640], BF16)
    kctm = P.sbuf("kctm", [128, 640], BF16)
    gkc = P.sbuf("gkc", [128, 640], BF16)
    dec = P.sbuf("dec", [128, 2, 5], F32)
    Am = P.sbuf("Am", [128, 256], BF16)
    Sst = P.sbuf("Sst", [128, 4, 5, 128], F32)
    Sbf = P.sbuf("Sbf", [128, 5, 128], BF16)
    U = P.sbuf("U", [128, 8, 512], F32)
    SP = P.sbuf("SP", [128, 8192], BF16)
    sD = P.sbuf("sD", [128, 512], F32)
    sE = P.sbuf("sE", [128, 512], F32)
    sF = P.sbuf("sF", [128, 512], F32)
    sG = P.sbuf("sG", [128, 512], F32)
    sH = P.sbuf("sH", [128, 1536], F32)
    ones128 = P.sbuf("ones128", [128, 128], BF16)
    ones1b = P.sbuf("ones1b", [1, 128], BF16)
    Lb = P.sbuf("Lb", [128, 128], BF16)
    Lf = P.sbuf("Lf", [128, 128], F32)
    identB = P.sbuf("identB", [128, 128], BF16)
    identF = P.sbuf("identF", [128, 128], F32)
    OSEL = P.sbuf("OSEL", [128, 128], BF16)
    esel = P.sbuf("esel", [4, 4, 128], F32)
    rwT = P.sbuf("rwT", [4, 512], F32)
    ot = U[:, 2, :]
    xb = SP[:, 0:4096].rearrange("p (a b) -> p a b", b=512)
    sqb = SP[:, 4096:8192].rearrange("p (a b) -> p a b", b=512)

    psA_ = P.psum("psA_", [128, 1024])
    psB_ = P.psum("psB_", [128, 1024])
    psC_ = P.psum("psC_", [128, 512])
    psD_ = P.psum("psD_", [128, 512])
    psE_ = P.psum("psE_", [128, 1024])
    psCb = psC_[:, :].bitcast(BF16)

    E = P.emit
    state = {}

    E("vector", lambda h: h.memset(ones128[:, :], 1.0))
    E("vector", lambda h: h.memset(ones1b[:, :], 1.0))
    E("vector", lambda h: h.memset(Sbf[:, :, :], 0.0))
    E("vector", lambda h: h.memset(OSEL[:, :], 0.0))
    E("vector", lambda h: h.memset(OSEL[0:64, 0:64], 1.0))
    E("vector", lambda h: h.memset(OSEL[64:128, 64:128], 1.0), inc=("v", 1))
    E("gpsimd", lambda h: h.memset(Lb[:, :], 1.0))
    E("gpsimd", lambda h: h.affine_select(out=Lb[:, :], in_=Lb[:, :], compare_op=OP.is_ge,
                                          fill=0.0, base=0, pattern=[[1, 128]],
                                          channel_multiplier=-1))
    E("gpsimd", lambda h: h.memset(Lf[:, :], 1.0))
    E("gpsimd", lambda h: h.affine_select(out=Lf[:, :], in_=Lf[:, :], compare_op=OP.is_ge,
                                          fill=0.0, base=0, pattern=[[1, 128]],
                                          channel_multiplier=-1))
    E("gpsimd", lambda h: h.memset(identB[:, :], 0.0))
    E("gpsimd", lambda h: h.affine_select(out=identB[:, :], in_=identB[:, :],
                                          compare_op=OP.not_equal, fill=1.0, base=0,
                                          pattern=[[-1, 128]], channel_multiplier=1))
    E("gpsimd", lambda h: h.memset(identF[:, :], 0.0))
    E("gpsimd", lambda h: h.affine_select(out=identF[:, :], in_=identF[:, :],
                                          compare_op=OP.not_equal, fill=1.0, base=0,
                                          pattern=[[-1, 128]], channel_multiplier=1))
    for e in range(4):
        E("gpsimd", lambda h, e=e: h.memset(esel[:, e, :], 1.0))
        E("gpsimd", lambda h, e=e: h.affine_select(out=esel[:, e, :], in_=esel[:, e, :],
                                                   compare_op=OP.is_equal, fill=0.0,
                                                   base=-e, pattern=[[0, 128]],
                                                   channel_multiplier=1),
          inc=("g", 1) if e == 3 else None)
    g_c = P.val("g")

    E("sync", lambda h: h.dma_start(out=bgF[0:1, 0:512], in_=bg1D), inc=("lp", 16))
    E("sync", lambda h: h.dma_start(out=bgF[0:1, 512:1024], in_=bg2D), inc=("lp", 16))
    E("sync", lambda h: h.dma_start(out=nwS[:, :], in_=nwD), inc=("lp", 16))
    lp = P.val("lp")
    E("vector", lambda h: h.tensor_copy(out=bg1S[0:1, :], in_=bgF[0:1, 0:512]),
      wait=[("lp", lp)])
    E("vector", lambda h: h.tensor_copy(out=bg2S[0:1, :], in_=bgF[0:1, 512:1024]),
      inc=("v", 1))
    v_params = P.val("v")
    wlist = [(wq, wqD, 8, 512), (wk, wkD, 8, 512), (wv, wvD, 8, 512),
             (wqa, wqaD, 8, 64), (wka, wkaD, 8, 64), (wga, wgaD, 8, 64),
             (wg1a, wg1aD, 8, 16), (wg2a, wg2aD, 8, 16), (weS, weD, 8, 4),
             (wo, woD, 4, 1024)]
    for sb, dr, nchk, wdt in wlist:
        for c in range(nchk):
            E("gpsimd", lambda h, sb=sb, dr=dr, c=c: h.dma_start(
                out=sb[:, c, :], in_=dr[c * 128:(c + 1) * 128, :]), inc=("w", 16))
    for sb, dr in ((wqb, wqbD), (wkb, wkbD), (wgb, wgbD), (wg1b, wg1bD), (wg2b, wg2bD)):
        E("gpsimd", lambda h, sb=sb, dr=dr: h.dma_start(out=sb[:, :], in_=dr), inc=("w", 16))
    w_all = P.val("w")

    def MM(ps, lhsT, rhs, start, stop, w0=None, inc=None, tp=None):
        E("tensor", lambda h, ps=ps, lhsT=lhsT, rhs=rhs, start=start, stop=stop, tp=tp:
          h.matmul(ps, lhsT, rhs, start=start, stop=stop,
                   **({"tile_position": tp} if tp else {})),
          wait=w0, inc=inc)

    def ps4(mi):
        return (psA_ if mi < 2 else psB_)[:, (mi % 2) * 512:(mi % 2) * 512 + 512]

    for s in range(nseq):
        v_xn = emit_ln(P, nc, x1T, s, U, xb, sqb, sD, sE, sF, sG, xn, ones128, psA_, state)
        E("vector", lambda h: h.memset(Sst[:, :, :, :], 0.0), inc=("v", 1))
        v_s0 = P.val("v")
        for q in range(NQ):
            q0 = q * QT
            xnq = lambda c: xn[:, c, q0:q0 + QT]
            pv = state.get("p1_war")
            for mi in range(4):
                for c in range(8):
                    w0 = None
                    if mi == 0 and c == 0:
                        w0 = [("v", v_xn), ("w", w_all), ("v", v_s0), ("g", g_c)]
                        if pv:
                            w0 += [("a", pv[0]), ("v", pv[1])]
                    MM(ps4(mi), wq[:, c, mi * 128:(mi + 1) * 128], xnq(c),
                       c == 0, c == 7, w0, ("t", 1) if (c == 7 and mi == 3) else None)
            t_q1 = P.val("t")
            E("scalar", lambda h: h.activation(out=q1f[:, 0:2, :].rearrange("p a b -> p (a b)"),
                                               in_=psA_[:, :], func=AF.Identity, scale=0.125),
              wait=[("t", t_q1)])
            E("scalar", lambda h: h.activation(out=q1f[:, 2:4, :].rearrange("p a b -> p (a b)"),
                                               in_=psB_[:, :], func=AF.Identity, scale=0.125),
              inc=("a", 1))
            a_q1 = P.val("a")
            for mi in range(4):
                for c in range(8):
                    w0 = [("a", a_q1)] if (mi == 0 and c == 0) else None
                    MM(ps4(mi), wk[:, c, mi * 128:(mi + 1) * 128], xnq(c),
                       c == 0, c == 7, w0, ("t", 1) if (c == 7 and mi == 3) else None)
            t_k1 = P.val("t")
            E("scalar", lambda h: h.activation(out=k1f[:, 0:2, :].rearrange("p a b -> p (a b)"),
                                               in_=psA_[:, :], func=AF.Identity),
              wait=[("t", t_k1)])
            E("scalar", lambda h: h.activation(out=k1f[:, 2:4, :].rearrange("p a b -> p (a b)"),
                                               in_=psB_[:, :], func=AF.Identity),
              inc=("a", 1))
            a_prev = P.val("a")
            for ts in range(4):
                ssl = slice(q0 + ts * 128, q0 + (ts + 1) * 128)
                for c in range(8):
                    w0 = [("a", a_prev)] if c == 0 else None
                    MM(psA_[:, 0:512], xn[:, c, ssl], wk[:, c, :], c == 0, c == 7, w0)
                    MM(psA_[:, 512:1024], xn[:, c, ssl], wv[:, c, :], c == 0, c == 7,
                       None, ("t", 1) if c == 7 else None)
                t_kv = P.val("t")
                E("scalar", lambda h, ts=ts: h.activation(out=k1tm[:, ts, :], in_=psA_[:, 0:512],
                                                          func=AF.Identity),
                  wait=[("t", t_kv)])
                E("scalar", lambda h, ts=ts: h.activation(out=vt[:, ts, :], in_=psA_[:, 512:1024],
                                                          func=AF.Identity),
                  inc=("a", 1))
                a_prev = P.val("a")
            for c in range(8):
                w0 = [("a", a_prev)] if c == 0 else None
                MM(psB_[0:64, 0:512], wqa[:, c, :], xnq(c), c == 0, c == 7, w0,
                   ("t", 1) if c == 7 else None)
            t_xa = P.val("t")
            E("scalar", lambda h: h.activation(out=xaf[:, :], in_=psB_[0:64, 0:512],
                                               func=AF.Identity),
              wait=[("t", t_xa)], inc=("a", 1))
            a_xa = P.val("a")
            for mi in range(4):
                w0 = [("a", a_xa)] if mi == 0 else None
                MM(ps4(mi), wqb[:, mi * 128:(mi + 1) * 128], xaf[:, :], True, True, w0,
                   ("t", 1) if mi == 3 else None)
            t_q2 = P.val("t")
            E("scalar", lambda h: h.activation(out=q2f[:, 0:2, :].rearrange("p a b -> p (a b)"),
                                               in_=psA_[:, :], func=AF.Identity, scale=0.125),
              wait=[("t", t_q2)])
            E("scalar", lambda h: h.activation(out=q2f[:, 2:4, :].rearrange("p a b -> p (a b)"),
                                               in_=psB_[:, :], func=AF.Identity, scale=0.125),
              inc=("a", 1))
            a_q2 = P.val("a")
            for c in range(8):
                w0 = [("a", a_q2)] if c == 0 else None
                MM(psB_[0:64, 0:512], wka[:, c, :], xnq(c), c == 0, c == 7, w0,
                   ("t", 1) if c == 7 else None)
            t_ka = P.val("t")
            E("scalar", lambda h: h.activation(out=xaf[:, :], in_=psB_[0:64, 0:512],
                                               func=AF.Identity),
              wait=[("t", t_ka)], inc=("a", 1))
            a_ka = P.val("a")
            v_k2 = None
            for ts in range(4):
                w0 = [("a", a_ka)] if ts == 0 else [("v", v_k2)]
                MM(psA_[:, 0:512], xaf[:, ts * 128:(ts + 1) * 128], wkb[:, :], True, True,
                   w0, ("t", 1))
                t_k2 = P.val("t")
                E("vector", lambda h: h.tensor_copy(out=sH[:, 0:512], in_=psA_[:, 0:512]),
                  wait=[("t", t_k2)])
                sh3 = sH[:, 0:512].rearrange("p (g w) -> p g w", w=64)
                E("vector", lambda h, sh3=sh3: h.tensor_reduce(out=sD[:, 0:8], in_=sh3,
                                                               axis=AX.X, op=OP.max))
                E("vector", lambda h, sh3=sh3: h.tensor_tensor(
                    out=sh3, in0=sh3, in1=sD[:, 0:8].unsqueeze(2).broadcast_to([128, 8, 64]),
                    op=OP.subtract), inc=("v", 1))
                v_sm = P.val("v")
                E("scalar", lambda h: h.activation(out=sH[:, 512:1024], in_=sH[:, 0:512],
                                                   func=AF.Exp),
                  wait=[("v", v_sm)], inc=("a", 1))
                a_sm = P.val("a")
                se3 = sH[:, 512:1024].rearrange("p (g w) -> p g w", w=64)
                E("vector", lambda h, se3=se3: h.tensor_reduce(out=sD[:, 8:16], in_=se3,
                                                               axis=AX.X, op=OP.add),
                  wait=[("a", a_sm)], inc=("v", 1))
                v_ks = P.val("v")
                E("vector", lambda h: h.reciprocal(out=sD[:, 16:24], in_=sD[:, 8:16]),
                  wait=[("v", v_ks)], inc=("v", 1))
                v_kr = P.val("v")
                E("vector", lambda h, se3=se3, ts=ts: h.tensor_tensor(
                    out=k2tm[:, ts, :].rearrange("p (g w) -> p g w", w=64), in0=se3,
                    in1=sD[:, 16:24].unsqueeze(2).broadcast_to([128, 8, 64]), op=OP.mult),
                  wait=[("v", v_kr)], inc=("v", 1))
                v_k2 = P.val("v")
            a_tp = None
            for ts in range(4):
                for mi in range(4):
                    w0 = None
                    if mi == 0:
                        w0 = [("v", v_k2)] if ts == 0 else []
                        if a_tp is not None:
                            w0.append(("a", a_tp))
                        w0 = w0 or None
                    E("tensor", lambda h, ts=ts, mi=mi: h.transpose(
                        psCb[:, mi * 128:(mi + 1) * 128],
                        k2tm[:, ts, mi * 128:(mi + 1) * 128], identB[:, :]),
                      wait=w0, inc=("t", 1) if mi == 3 else None)
                t_tp = P.val("t")
                E("scalar", lambda h, ts=ts: h.activation(
                    out=k2f[:, :, ts * 128:(ts + 1) * 128],
                    in_=psCb[:, 0:512].rearrange("p (a b) -> p a b", b=128),
                    func=AF.Identity),
                  wait=[("t", t_tp)], inc=("a", 1))
                a_tp = P.val("a")
            for c in range(8):
                w0 = [("a", a_tp)] if c == 0 else None
                MM(psB_[0:16, 0:512], wg1a[:, c, :], xnq(c), c == 0, c == 7, w0)
            for c in range(8):
                MM(psB_[0:16, 512:1024], wg2a[:, c, :], xnq(c), c == 0, c == 7,
                   None, ("t", 1) if c == 7 else None)
            t_ga = P.val("t")
            E("scalar", lambda h: h.activation(out=gaf[:, :], in_=psB_[0:16, :],
                                               func=AF.Identity),
              wait=[("t", t_ga)], inc=("a", 1))
            a_ga = P.val("a")
            v_gprev = None
            for gi, (wb, bgS, gkt) in enumerate(((wg1b, bg1S, gk1tm), (wg2b, bg2S, gk2tm))):
                for ts in range(4):
                    w0 = [("a", a_ga), ("v", v_params)] if (gi == 0 and ts == 0) else (
                        [("v", v_gprev)] if v_gprev else None)
                    MM(psA_[:, 0:512], ones1b[0:1, :], bgS[0:1, :], True, False, w0)
                    MM(psA_[:, 0:512], gaf[0:16, gi * 512 + ts * 128: gi * 512 + (ts + 1) * 128],
                       wb[:, :], False, True, None, ("t", 1))
                    t_z = P.val("t")
                    E("scalar", lambda h: h.activation(out=sH[:, 512:1024], in_=psA_[:, 0:512],
                                                       func=AF.Exp, scale=-1.0),
                      wait=[("t", t_z)], inc=("a", 1))
                    a_e1 = P.val("a")
                    E("scalar", lambda h: h.activation(out=sH[:, 512:1024], in_=sH[:, 512:1024],
                                                       func=AF.Ln, bias=1.0),
                      wait=[("a", a_e1)], inc=("a", 1))
                    a_lg = P.val("a")
                    E("vector", lambda h, gkt=gkt, ts=ts: h.tensor_scalar(
                        out=gkt[:, ts, :], in0=sH[:, 512:1024], scalar1=-1.0 / 16.0,
                        scalar2=None, op0=OP.mult),
                      wait=[("a", a_lg)], inc=("v", 1))
                    v_gprev = P.val("v")
            v_rw = v_gprev
            for ts in range(4):
                ssl = slice(q0 + ts * 128, q0 + (ts + 1) * 128)
                for c in range(8):
                    w0 = [("v", v_rw)] if c == 0 else None
                    MM(psC_[:, 0:4], xn[:, c, ssl], weS[:, c, :], c == 0, c == 7, w0,
                       ("t", 1) if c == 7 else None)
                t_e = P.val("t")
                E("vector", lambda h: h.tensor_copy(out=sD[:, 0:4], in_=psC_[:, 0:4]),
                  wait=[("t", t_e)], inc=("v", 1))
                v_ecp = P.val("v")
                E("vector", lambda h: h.tensor_reduce(out=sD[:, 4:5], in_=sD[:, 0:4],
                                                      axis=AX.X, op=OP.max), inc=("v", 1))
                v_em = P.val("v")
                E("vector", lambda h: h.tensor_scalar(out=sD[:, 0:4], in0=sD[:, 0:4],
                                                      scalar1=sD[:, 4:5], scalar2=None,
                                                      op0=OP.subtract),
                  wait=[("v", v_em)], inc=("v", 1))
                v_e1 = P.val("v")
                E("scalar", lambda h: h.activation(out=sD[:, 0:4], in_=sD[:, 0:4], func=AF.Exp),
                  wait=[("v", v_e1)], inc=("a", 1))
                a_e2 = P.val("a")
                E("vector", lambda h: h.tensor_reduce(out=sD[:, 4:5], in_=sD[:, 0:4],
                                                      axis=AX.X, op=OP.add),
                  wait=[("a", a_e2)], inc=("v", 1))
                v_es = P.val("v")
                E("vector", lambda h: h.reciprocal(out=sD[:, 5:6], in_=sD[:, 4:5]),
                  wait=[("v", v_es)], inc=("v", 1))
                v_er = P.val("v")
                E("vector", lambda h: h.tensor_scalar(out=sD[:, 0:4], in0=sD[:, 0:4],
                                                      scalar1=sD[:, 5:6], scalar2=None,
                                                      op0=OP.mult),
                  wait=[("v", v_er)], inc=("v", 1))
                v_en = P.val("v")
                E("vector", lambda h: h.tensor_reduce(out=sD[:, 4:5], in_=sD[:, 0:4],
                                                      axis=AX.X, op=OP.max),
                  wait=[("v", v_en)], inc=("v", 1))
                v_em2 = P.val("v")
                E("vector", lambda h, ts=ts: h.tensor_scalar(
                    out=rwm[:, ts, 4:8], in0=sD[:, 0:4], scalar1=sD[:, 4:5], scalar2=None,
                    op0=OP.is_ge),
                  wait=[("v", v_em2)], inc=("v", 1))
                v_m = P.val("v")
                E("vector", lambda h, ts=ts: h.tensor_tensor(out=rwm[:, ts, 0:4],
                                                             in0=sD[:, 0:4],
                                                             in1=rwm[:, ts, 4:8], op=OP.mult),
                  wait=[("v", v_m)], inc=("v", 1))
                v_rw = P.val("v")
                E("tensor", lambda h, ts=ts: h.transpose(
                    psC_[0:4, 128:256], rwm[:, ts, 0:4], identF[:, :]),
                  wait=[("v", v_rw)], inc=("t", 1))
                t_rt = P.val("t")
                E("vector", lambda h, ts=ts: h.tensor_copy(out=rwT[0:4, ts * 128:(ts + 1) * 128],
                                                           in_=psC_[0:4, 128:256]),
                  wait=[("t", t_rt)], inc=("v", 1))
                v_rw = P.val("v")
            v_rwb_prev = v_rw
            for ck in range(4):
                for e in range(4):
                    w0 = [("v", v_rwb_prev)] if e == 0 else None
                    MM(psC_[:, 0:512][:, e * 128:(e + 1) * 128], esel[:, e, :],
                       rwT[0:4, ck * 128:(ck + 1) * 128], True, True, w0,
                       ("t", 1) if e == 3 else None)
                t_rb = P.val("t")
                E("vector", lambda h, ck=ck: h.tensor_copy(out=rwb[:, ck, :], in_=psC_[:, 0:512]),
                  wait=[("t", t_rb)], inc=("v", 1))
                v_rwb_prev = P.val("v")
            v_rwb = P.val("v")
            for c in range(8):
                w0 = [("v", v_rwb)] if c == 0 else None
                MM(psB_[0:64, 0:512], wga[:, c, :], xnq(c), c == 0, c == 7, w0,
                   ("t", 1) if c == 7 else None)
            t_xg = P.val("t")
            E("scalar", lambda h: h.activation(out=xaf[:, :], in_=psB_[0:64, 0:512],
                                               func=AF.Identity),
              wait=[("t", t_xg)], inc=("a", 1))
            a_xg = P.val("a")
            for mi in range(4):
                w0 = [("a", a_xg)] if mi == 0 else None
                MM(ps4(mi), wgb[:, mi * 128:(mi + 1) * 128], xaf[:, :], True, True, w0,
                   ("t", 1) if mi == 3 else None)
            t_g = P.val("t")
            E("scalar", lambda h: h.activation(out=gf[:, 0:2, :].rearrange("p a b -> p (a b)"),
                                               in_=psA_[:, :], func=AF.Identity),
              wait=[("t", t_g)])
            E("scalar", lambda h: h.activation(out=gf[:, 2:4, :].rearrange("p a b -> p (a b)"),
                                               in_=psB_[:, :], func=AF.Identity),
              inc=("a", 1))
            a_gf = P.val("a")
            sHb = SP[:, 4096:6144]
            E("scalar", lambda h, sHb=sHb: h.activation(
                out=sHb, in_=gf[:, :, :].rearrange("p a b -> p (a b)"), func=AF.Tanh, scale=0.5),
              wait=[("a", a_gf)], inc=("a", 1))
            a_th = P.val("a")
            E("vector", lambda h, sHb=sHb: h.tensor_scalar(
                out=sHb, in0=sHb, scalar1=1.0, scalar2=0.5, op0=OP.add, op1=OP.mult),
              wait=[("a", a_th)], inc=("v", 1))
            v_t2 = P.val("v")
            E("vector", lambda h, sHb=sHb: h.tensor_tensor(
                out=sgf[:, :, :].rearrange("p a b -> p (a b)"),
                in0=gf[:, :, :].rearrange("p a b -> p (a b)"), in1=sHb, op=OP.mult),
              wait=[("v", v_t2)], inc=("v", 1))
            v_sg = P.val("v")

            if dbg and s == 0 and q == DQ:
                for nm, sb in (("q1f", q1f), ("q2f", q2f), ("k1f", k1f), ("k2f", k2f),
                               ("k1tm", k1tm), ("k2tm", k2tm), ("vt", vt),
                               ("gk1tm", gk1tm), ("gk2tm", gk2tm), ("rwm", rwm),
                               ("rwb", rwb), ("sgf", sgf)):
                    E("sync", lambda h, nm=nm, sb=sb: h.dma_start(out=dbgT[nm], in_=sb[:, :, :]),
                      wait=[("v", v_sg)] if nm == "q1f" else None, inc=("st", 16))
            for ck in range(NCK):
                for p in range(NPAIR_F):
                    u = state.get("l1f_u", 0)
                    psG = psA_ if u % 2 == 0 else psE_
                    psl = slice(p * 128, (p + 1) * 128)
                    cksl = slice(ck * 128, (ck + 1) * 128)
                    w0 = [("v", v_sg)] if (ck == 0 and p == 0) else None
                    E("vector", lambda h, ck=ck, psl=psl: h.tensor_copy(
                        out=gkc[:, 0:128], in_=gk1tm[:, ck, psl]),
                      wait=(w0 or []) + [("t", state.get("l1f_csg", 0))])
                    E("vector", lambda h, ck=ck, psl=psl: h.tensor_tensor(
                        out=gkc[:, 128:640].rearrange("p (e w) -> p e w", w=128),
                        in0=gk2tm[:, ck, psl].unsqueeze(1).broadcast_to([128, 4, 128]),
                        in1=rwm[:, ck, 4:8].unsqueeze(2).broadcast_to([128, 4, 128]),
                        op=OP.mult), inc=("v", 1))
                    v_gkc = P.val("v")
                    for j in range(5):
                        w0 = [("v", v_gkc)]
                        if j == 0:
                            w0.append(("a", state.get(("gfmw", id(psG)), 0)))
                            w0.append(("v", state.get(("gfmv", id(psG)), 0)))
                        MM(psG[:, 0:640][:, j * 128:(j + 1) * 128],
                           gkc[:, j * 128:(j + 1) * 128], Lb[:, :], True, True,
                           w0 if j == 0 else None, ("t", 1) if j == 4 else None)
                    t_cs = P.val("t")
                    if state.get("l1f_pend") is not None:
                        state["l1f_pend"]()
                        state["l1f_pend"] = None
                    E("scalar", lambda h, psG=psG, u=u: h.activation(
                        out=dec[:, u % 2, :].unsqueeze(2),
                        in_=psG[:, 0:640].rearrange("p (j w) -> p j w", w=128)[:, :, 127:128],
                        func=AF.Exp),
                      wait=[("t", t_cs)])
                    E("scalar", lambda h, psG=psG: h.activation(out=expg[:, :], in_=psG[:, 0:640],
                                                                func=AF.Exp))
                    E("scalar", lambda h, psG=psG: h.activation(out=negg[:, :], in_=psG[:, 0:640],
                                                                func=AF.Exp, scale=-1.0),
                      inc=("a", 1))
                    a_ex = P.val("a")
                    state[("gfmw", id(psG))] = a_ex
                    E("vector", lambda h, p=p, cksl=cksl: h.tensor_copy(
                        out=qcat[:, 0:128], in_=q1f[:, p, cksl]),
                      wait=[("a", a_ex), ("t", state.get("l1f_Aw", 0))])
                    E("vector", lambda h, p=p, ck=ck, cksl=cksl: h.tensor_tensor(
                        out=qcat[:, 128:640].rearrange("p (e w) -> p e w", w=128),
                        in0=q2f[:, p, cksl].unsqueeze(1).broadcast_to([128, 4, 128]),
                        in1=rwb[:, ck, :].rearrange("p (e w) -> p e w", w=128), op=OP.mult))
                    E("vector", lambda h: h.tensor_tensor(out=qcat[:, :], in0=qcat[:, :],
                                                          in1=expg[:, :], op=OP.mult))
                    E("vector", lambda h, p=p, cksl=cksl: h.tensor_copy(
                        out=kcat[:, 0:128], in_=k1f[:, p, cksl]))
                    E("vector", lambda h, p=p, ck=ck, cksl=cksl: h.tensor_tensor(
                        out=kcat[:, 128:640].rearrange("p (e w) -> p e w", w=128),
                        in0=k2f[:, p, cksl].unsqueeze(1).broadcast_to([128, 4, 128]),
                        in1=rwb[:, ck, :].rearrange("p (e w) -> p e w", w=128), op=OP.mult))
                    E("vector", lambda h: h.tensor_tensor(out=kcat[:, :], in0=kcat[:, :],
                                                          in1=negg[:, :], op=OP.mult))
                    E("vector", lambda h, p=p: h.tensor_copy(
                        out=Sbf[0:64, :, 0:64], in_=Sst[0:64, p, :, 0:64]),
                      wait=[("t", state.get("t_oi", 0))])
                    E("vector", lambda h, p=p: h.tensor_copy(
                        out=Sbf[64:128, :, 64:128], in_=Sst[64:128, p, :, 64:128]),
                      inc=("v", 1))
                    v_bld = P.val("v")

                    def pend(p=p, ck=ck, cksl=cksl, psl=psl, v_bld=v_bld, psG=psG, u=u):
                        psBb = psB_[:, :].bitcast(BF16)
                        for j in range(5):
                            w0 = None
                            if j == 0:
                                w0 = [("v", v_bld), ("v", state.get("l1f_tpw", 0)),
                                      ("g", g_c)]
                            E("tensor", lambda h, j=j, psBb=psBb: h.transpose(
                                psBb[:, j * 128:(j + 1) * 128],
                                kcat[:, j * 128:(j + 1) * 128], identB[:, :]),
                              wait=w0, inc=("t", 1) if j == 4 else None)
                        t_tp5 = P.val("t")
                        E("vector", lambda h, psBb=psBb: h.tensor_copy(
                            out=kctm[:, 0:640], in_=psBb[:, 0:640]),
                          wait=[("t", t_tp5)], inc=("v", 1))
                        state["l1f_tpw"] = P.val("v")
                        for j in range(5):
                            w0 = [("v", v_bld)] if j == 0 else None
                            MM(psC_[:, 0:128], kcat[0:64, j * 128:(j + 1) * 128],
                               qcat[0:64, j * 128:(j + 1) * 128], j == 0, j == 4, w0,
                               None, (0, 0))
                            MM(psD_[:, 0:128], kcat[64:128, j * 128:(j + 1) * 128],
                               qcat[64:128, j * 128:(j + 1) * 128], j == 0, j == 4, None,
                               ("t", 1) if j == 4 else None, (64, 0))
                        t_A = P.val("t")
                        E("vector", lambda h: h.tensor_tensor(
                            out=Am[:, 0:128], in0=psC_[:, 0:128], in1=Lf[:, :], op=OP.mult),
                          wait=[("t", t_A)])
                        E("vector", lambda h: h.tensor_tensor(
                            out=Am[:, 128:256], in0=psD_[:, 0:128], in1=Lf[:, :], op=OP.mult),
                          inc=("v", 1))
                        v_am = P.val("v")
                        MM(psC_[0:64, 128:256], vt[:, ck, p * 128:p * 128 + 64], Am[:, 0:128],
                           True, False, [("v", v_am)], None, (0, 0))
                        MM(psD_[64:128, 128:256], vt[:, ck, p * 128 + 64:(p + 1) * 128],
                           Am[:, 128:256], True, False, None, None, (0, 64))
                        for j in range(5):
                            MM(psC_[0:64, 128:256], Sbf[:, j, 0:64],
                               qcat[:, j * 128:(j + 1) * 128], False, j == 4, None, None,
                               (0, 0))
                            MM(psD_[64:128, 128:256], Sbf[:, j, 64:128],
                               qcat[:, j * 128:(j + 1) * 128], False, j == 4, None,
                               ("t", 1) if j == 4 else None, (0, 64))
                        t_oi = P.val("t")
                        state["t_oi"] = t_oi
                        for j in range(5):
                            w0 = [("v", state["l1f_tpw"])] if j == 0 else None
                            MM(psG[:, 0:640][:, j * 128:(j + 1) * 128],
                               kctm[:, j * 128:(j + 1) * 128], vt[:, ck, psl], True, True,
                               w0, ("t", 1) if j == 4 else None)
                        t_kv2 = P.val("t")
                        state["l1f_Aw"] = t_kv2
                        E("vector", lambda h, p=p, psG=psG: h.tensor_tensor(
                            out=Sst[:, p, :, :].rearrange("p a b -> p (a b)"),
                            in0=Sst[:, p, :, :].rearrange("p a b -> p (a b)"),
                            in1=psG[:, 0:640], op=OP.add),
                          wait=[("t", t_kv2)])
                        E("vector", lambda h, p=p, u=u: h.tensor_tensor(
                            out=Sst[:, p, :, :], in0=Sst[:, p, :, :],
                            in1=dec[:, u % 2, :].unsqueeze(2).broadcast_to([128, 5, 128]),
                            op=OP.mult), inc=("v", 1))
                        state[("gfmv", id(psG))] = P.val("v")
                        E("vector", lambda h, p=p, cksl=cksl: h.tensor_copy(
                            out=ob[0:64, p, cksl], in_=psC_[0:64, 128:256]),
                          wait=[("t", t_oi)])
                        E("vector", lambda h, p=p, cksl=cksl: h.tensor_copy(
                            out=ob[64:128, p, cksl], in_=psD_[64:128, 128:256]),
                          inc=("v", 1))
                    state["l1f_pend"] = pend
                    state["l1f_u"] = u + 1
            if state.get("l1f_pend") is not None:
                state["l1f_pend"]()
                state["l1f_pend"] = None
            v_p2 = P.val("v")

            obf = ob[:, :, :].rearrange("p a b -> p (a b)")
            E("vector", lambda h, obf=obf: h.tensor_tensor(out=SP[:, 0:2048], in0=obf, in1=obf,
                                                           op=OP.mult),
              wait=[("v", v_p2)], inc=("v", 1))
            v_o2 = P.val("v")
            for j in range(4):
                w0 = None
                if j == 0:
                    w0 = [("v", v_o2), ("a", state.get("p2_ps_war", 0))]
                MM(ps4(j), OSEL[:, :], SP[:, j * 512:(j + 1) * 512], True, True, w0,
                   ("t", 1) if j == 3 else None)
            t_rms = P.val("t")
            lnbuf = SP[:, 4096:8192].bitcast(F32)
            E("scalar", lambda h, lnbuf=lnbuf: h.activation(out=lnbuf[:, 0:1024], in_=psA_[:, :],
                                               func=AF.Ln, scale=1.0 / 64.0, bias=1e-5),
              wait=[("t", t_rms)])
            E("scalar", lambda h, lnbuf=lnbuf: h.activation(out=lnbuf[:, 1024:2048], in_=psB_[:, :],
                                               func=AF.Ln, scale=1.0 / 64.0, bias=1e-5),
              inc=("a", 1))
            a_l = P.val("a")
            E("scalar", lambda h, lnbuf=lnbuf: h.activation(out=SP[:, 2048:4096], in_=lnbuf[:, 0:2048],
                                               func=AF.Exp, scale=-0.5),
              wait=[("a", a_l)], inc=("a", 1))
            a_rstd = P.val("a")
            state["p3_ps_war"] = a_rstd
            E("vector", lambda h, obf=obf: h.tensor_tensor(out=obf, in0=obf,
                                                           in1=SP[:, 2048:4096], op=OP.mult),
              wait=[("a", a_rstd)])
            E("vector", lambda h, obf=obf: h.tensor_scalar(out=obf, in0=obf, scalar1=nwS[:, 0:1],
                                                           scalar2=None, op0=OP.mult),
              wait=[("lp", lp)])
            E("vector", lambda h, obf=obf: h.tensor_tensor(
                out=obf, in0=obf, in1=sgf[:, :, :].rearrange("p a b -> p (a b)"), op=OP.mult),
              inc=("v", 1))
            v_of = P.val("v")
            v_ep = None
            for m in range(8):
                for c in range(4):
                    w0 = None
                    if c == 0:
                        w0 = [("v", v_of)]
                        if v_ep is not None:
                            w0 = [("v", v_ep)]
                    MM(psC_[:, 0:512], wo[:, c, m * 128:(m + 1) * 128], ob[:, c, :],
                       c == 0, c == 3, w0, ("t", 1) if c == 3 else None)
                t_wo = P.val("t")
                st_now = P.val("st")
                w_x1 = [("st", st_now), ("v", state.get("ep_v", 0))]
                if m == 0:
                    w_x1.append(("v", v_of))
                E("sync", lambda h, m=m, s=s, q0=q0: h.dma_start(
                    out=U[:, 0, :], in_=x1T[s, m * 128:(m + 1) * 128, q0:q0 + QT]),
                  wait=w_x1, inc=("ld", 16))
                E("sync", lambda h, m=m, s=s, q0=q0: h.dma_start(
                    out=U[:, 1, :], in_=attnT[s, m * 128:(m + 1) * 128, q0:q0 + QT]),
                  inc=("ld", 16))
                ld_r = P.val("ld")
                E("vector", lambda h: h.tensor_copy(out=ot[:, :], in_=psC_[:, 0:512]),
                  wait=[("t", t_wo), ("ld", ld_r)], inc=("v", 1))
                v_ep = P.val("v")
                E("vector", lambda h: h.tensor_tensor(out=ot[:, :], in0=ot[:, :],
                                                      in1=U[:, 0, :], op=OP.add))
                E("vector", lambda h: h.tensor_tensor(out=ot[:, :], in0=ot[:, :],
                                                      in1=U[:, 1, :], op=OP.add),
                  inc=("v", 1))
                state["ep_v"] = P.val("v")
                v_ot = P.val("v")
                E("sync", lambda h, m=m, s=s, q0=q0: h.dma_start(
                    out=out2T[s, m * 128:(m + 1) * 128, q0:q0 + QT], in_=ot[:, :]),
                  wait=[("v", v_ot)], inc=("st", 16))
            state["p1_war"] = (P.val("a"), P.val("v"))
        state["ln_war"] = P.val("t")
        state["ps_war"] = P.val("a")



_CACHE = {}

T_ = 2048


def _program():
    if "p" in _CACHE:
        return _CACHE["p"]
    nc = bass.Bass("TRN2", target_bir_lowering=False)
    P = Plan(nc)
    F = F32
    d = {}

    def din(name, shape):
        d[name] = nc.dram_tensor(name, shape, F, kind="ExternalInput").ap()
    din("xT", [2, 1024, T_])
    din("wqkvT", [1024, 3072]); din("woutT", [1024, 1024]); din("cbT", [1024, 1024])
    din("bout", [128, 8]); din("scb", [128, 8]); din("fbias", [128, 8])
    din("wqT", [1024, 1024]); din("wkT", [1024, 1024]); din("wvT", [1024, 1024])
    din("woT", [1024, 1024]); din("bqf", [128, 8]); din("bkf", [128, 8])
    din("bvr", [1, 1024]); din("bof", [128, 8])
    din("wq", [1024, 512]); din("wk", [1024, 512]); din("wv", [1024, 512])
    din("wqa", [1024, 64]); din("wqb", [64, 512]); din("wka", [1024, 64])
    din("wkb", [64, 512]); din("wg1a", [1024, 16]); din("wg1b", [16, 512])
    din("bg1", [1, 512]); din("wg2a", [1024, 16]); din("wg2b", [16, 512])
    din("bg2", [1, 512]); din("we", [1024, 4]); din("wga", [1024, 64])
    din("wgb", [64, 512]); din("nw", [128, 1]); din("wo", [512, 1024])
    d["x1T"] = nc.dram_tensor("x1T", [2, 1024, T_], F, kind="Internal").ap()
    d["attnT"] = nc.dram_tensor("attnT", [2, 1024, T_], F, kind="Internal").ap()
    d["out2T"] = nc.dram_tensor("out2T", [2, 1024, T_], F, kind="ExternalOutput").ap()

    for s in ("ld", "lp", "w", "st", "t", "v", "a", "g"):
        P.sem(s)
    cpool = P.sbuf("cpool_root", [128, 4], F32)
    for i, cv in enumerate((1e-5, 1e-6, 1e-30, 1.0)):
        nc.const_aps.register(F32, cv, cpool[:, i:i + 1])
        P.emit("vector", lambda h, i=i, cv=cv: h.memset(cpool[:, i:i + 1], cv),
               inc=("v", 1) if i == 3 else None)

    with P.scope():
        emit_l0(nc, P, d, 2)
    P.barrier()
    d1 = dict(d)
    d1["xT"] = d["x1T"]
    with P.scope():
        emit_l1a(nc, P, d1, 2)
    P.barrier()
    with P.scope():
        emit_l1f(nc, P, d, 2)
    P.wait("sync", ("st", P.val("st")))
    P.finalize()
    _CACHE["p"] = nc
    return nc


def kernel(L0_attn_x, L0_attn_Wqkv, L0_attn_Wout, L0_attn_bout, L0_ffn_CB, L0_ffn_SCB,
           L0_ffn_bias, L1_attn_W_Q, L1_attn_b_Q, L1_attn_W_K, L1_attn_b_K, L1_attn_W_V,
           L1_attn_b_V, L1_attn_W_O, L1_attn_b_O, L1_ffn_Wq, L1_ffn_Wk, L1_ffn_Wv,
           L1_ffn_Wqa, L1_ffn_Wqb, L1_ffn_Wka, L1_ffn_Wkb, L1_ffn_Wg1a, L1_ffn_Wg1b,
           L1_ffn_bg1, L1_ffn_Wg2a, L1_ffn_Wg2b, L1_ffn_bg2, L1_ffn_We, L1_ffn_Wga,
           L1_ffn_Wgb, L1_ffn_norm_w, L1_ffn_Wo):
    C = np.ascontiguousarray
    ncores = 8
    x = np.asarray(L0_attn_x)
    fm8 = lambda b: C(np.asarray(b).reshape(8, 128).T)

    nc = _program()
    w = {
        "wqkvT": C(np.asarray(L0_attn_Wqkv).T),
        "woutT": C(np.asarray(L0_attn_Wout).T),
        "cbT": C(np.asarray(L0_ffn_CB).T),
        "bout": fm8(L0_attn_bout),
        "scb": fm8(L0_ffn_SCB),
        "fbias": fm8(L0_ffn_bias),
        "wqT": C(np.asarray(L1_attn_W_Q).transpose(2, 0, 1).reshape(1024, 1024)),
        "wkT": C(np.asarray(L1_attn_W_K).transpose(2, 0, 1).reshape(1024, 1024)),
        "wvT": C(np.asarray(L1_attn_W_V).transpose(2, 0, 1).reshape(1024, 1024)),
        "woT": C(np.asarray(L1_attn_W_O).transpose(0, 2, 1).reshape(1024, 1024)),
        "bqf": fm8(np.asarray(L1_attn_b_Q).reshape(1024)),
        "bkf": fm8(np.asarray(L1_attn_b_K).reshape(1024)),
        "bvr": C(np.asarray(L1_attn_b_V).reshape(1, 1024)),
        "bof": fm8(L1_attn_b_O),
        "wq": C(np.asarray(L1_ffn_Wq)), "wk": C(np.asarray(L1_ffn_Wk)),
        "wv": C(np.asarray(L1_ffn_Wv)), "wqa": C(np.asarray(L1_ffn_Wqa)),
        "wqb": C(np.asarray(L1_ffn_Wqb)), "wka": C(np.asarray(L1_ffn_Wka)),
        "wkb": C(np.asarray(L1_ffn_Wkb)), "wg1a": C(np.asarray(L1_ffn_Wg1a)),
        "wg1b": C(np.asarray(L1_ffn_Wg1b)), "bg1": C(np.asarray(L1_ffn_bg1).reshape(1, 512)),
        "wg2a": C(np.asarray(L1_ffn_Wg2a)), "wg2b": C(np.asarray(L1_ffn_Wg2b)),
        "bg2": C(np.asarray(L1_ffn_bg2).reshape(1, 512)), "we": C(np.asarray(L1_ffn_We)),
        "wga": C(np.asarray(L1_ffn_Wga)), "wgb": C(np.asarray(L1_ffn_Wgb)),
        "nw": C(np.tile(np.asarray(L1_ffn_norm_w), 2).reshape(128, 1)),
        "wo": C(np.asarray(L1_ffn_Wo)),
    }
    in_maps = [dict(w, xT=C(x[2 * i:2 * i + 2].transpose(0, 2, 1))) for i in range(ncores)]
    res = run_bass_kernel_spmd(nc, in_maps, core_ids=list(range(ncores)))
    out = np.concatenate([np.asarray(res.results[i]["out2T"]).transpose(0, 2, 1)
                          for i in range(ncores)], axis=0)
    return out.astype(np.float32)


# version 20
# speedup vs baseline: 1.0728x; 1.0275x over previous
import numpy as np
import concourse.bass as bass
import concourse.mybir as mybir
from concourse.alu_op_type import AluOpType as OP
from contextlib import ExitStack
from concourse.bass_utils import run_bass_kernel_spmd

F32 = mybir.dt.float32
BF16 = mybir.dt.bfloat16
AF = mybir.ActivationFunctionType
AX = mybir.AxisListType


ENGS = ("sync", "tensor", "vector", "scalar", "gpsimd")


def _reg_const(nc, val, ap):
    try:
        nc.const_aps.register(F32, val, ap)
    except Exception:
        pass


class Plan:

    def __init__(self, nc):
        self.nc = nc
        self.q = {e: [] for e in ENGS}
        self.sems = {}
        self.semv = {}
        self.stack = ExitStack()
        self.root_stack = self.stack
        self.scope_id = 0
        self.fence = None

    def scope(self):
        from contextlib import contextmanager

        @contextmanager
        def _s():
            old = self.stack
            self.scope_id += 1
            self.stack = ExitStack()
            try:
                yield
            finally:
                self.stack.close()
                self.stack = old
        return _s()

    def barrier(self):
        if self.fence is None:
            old = self.stack
            self.stack = self.root_stack
            self.fence = self.sbuf("fencetile", [1, 2], F32)
            self.stack = old
        ft = self.fence
        self.emit("vector", lambda h: h.memset(ft[0:1, 0:1], 0.0), inc=("v", 1))
        self.emit("scalar", lambda h: h.activation(out=ft[0:1, 1:2], in_=ft[0:1, 0:1],
                                                   func=AF.Identity), inc=("a", 1))
        self.emit("gpsimd", lambda h: h.memset(ft[0:1, 1:2], 0.0), inc=("g", 1))
        pairs = [(s, v) for s, v in self.semv.items()]
        for e in ENGS:
            self.wait(e, *pairs)

    def sem(self, name):
        h = self.stack.enter_context(self.nc.semaphore(name))
        self.sems[name] = h
        self.semv[name] = 0
        return h

    def sbuf(self, name, shape, dtype):
        return self.stack.enter_context(
            self.nc.sbuf_tensor(f"s{self.scope_id}_{name}", shape, dtype))

    def psum(self, name, shape, dtype=F32):
        return self.stack.enter_context(
            self.nc.psum_tensor(f"s{self.scope_id}_{name}", shape, dtype))

    def emit(self, eng, fn, wait=None, inc=None):
        if inc is not None:
            self.semv[inc[0]] += inc[1]
        sems = self.sems

        def f(h):
            if wait:
                for s, v in wait:
                    h.wait_ge(sems[s], v)
            r = fn(h)
            if inc is not None:
                r.then_inc(sems[inc[0]], inc[1])
        self.q[eng].append(f)

    def wait(self, eng, *pairs):
        sems = self.sems

        def f(h):
            for s, v in pairs:
                h.wait_ge(sems[s], v)
        self.q[eng].append(f)

    def val(self, semname):
        return self.semv[semname]

    def after_all(self, semname):
        return (semname, self.semv[semname])

    def finalize(self):
        nc = self.nc
        q = self.q
        with nc.Block() as block:
            @block.sync
            def _(h):
                for f in q["sync"]:
                    f(h)

            @block.tensor
            def _(h):
                for f in q["tensor"]:
                    f(h)

            @block.vector
            def _(h):
                for f in q["vector"]:
                    f(h)

            @block.scalar
            def _(h):
                for f in q["scalar"]:
                    f(h)

            @block.gpsimd
            def _(h):
                for f in q["gpsimd"]:
                    f(h)
        self.stack.close()


T = 2048
NCH = 8
NTT = 4
NTS = 16
NPAIR = 8


def emit_l0(nc, P, d, nseq=2):
    xT = d["xT"]
    wqkvT = d["wqkvT"]
    woutT = d["woutT"]
    cbT = d["cbT"]
    bout = d["bout"]
    scb = d["scb"]
    fbias = d["fbias"]
    x1T = d["x1T"]

    xn = P.sbuf("xn", [128, NCH, T], BF16)
    wbuf = P.sbuf("wbuf", [128, NCH, 2048], BF16)
    qp = P.sbuf("qp", [128, NCH, T], BF16)
    kp = P.sbuf("kp", [128, NTS, 1024], BF16)
    vt = P.sbuf("vt", [128, NTS, 16 * 65], BF16)
    kvs = P.sbuf("kvs", [128, NPAIR * 130], BF16)
    U = P.sbuf("U", [128, NCH, 512], F32)
    SP = P.sbuf("SP", [128, 8192], BF16)
    sD = P.sbuf("sD", [128, 512], F32)
    sE = P.sbuf("sE", [128, 512], F32)
    sF = P.sbuf("sF", [128, 512], F32)
    sG = P.sbuf("sG", [128, 512], F32)
    ones128 = P.sbuf("ones128", [128, 128], BF16)
    ones1f = P.sbuf("ones1f", [1, 64], F32)
    ksumC = P.sbuf("ksumC", [128, NPAIR, 64], BF16)
    boutS = P.sbuf("boutS", [128, 8], F32)
    scbS = P.sbuf("scbS", [128, 8], F32)
    fbS = P.sbuf("fbS", [128, 8], F32)
    ea = sF
    ef = sG

    xtile = U
    sA = U[:, 0:2, :].rearrange("p a b -> p (a b)")
    xb = SP[:, 0:4096].rearrange("p (a b) -> p a b", b=512)
    sqb = SP[:, 4096:8192].rearrange("p (a b) -> p a b", b=512)
    sB = SP[:, 0:1024]
    sC = SP[:, 1024:2048]

    psBig = P.psum("psBig", [128, 2048])
    psB2 = P.psum("psB2", [128, 1024])
    psQ1 = P.psum("psQ1", [128, 512])
    psQ2 = P.psum("psQ2", [128, 512])

    E = P.emit

    E("vector", lambda h: h.memset(ones128[:, :], 1.0))
    E("vector", lambda h: h.memset(ones1f[:, :], 1.0))
    E("sync", lambda h: h.dma_start(out=boutS[:, :], in_=bout), inc=("lp", 16))
    E("sync", lambda h: h.dma_start(out=scbS[:, :], in_=scb), inc=("lp", 16))
    E("sync", lambda h: h.dma_start(out=fbS[:, :], in_=fbias), inc=("lp", 16))
    ld_params = P.val("lp")
    E("vector", lambda h: h.tensor_scalar(out=scbS[:, :], in0=scbS[:, :], scalar1=1.0 / 127.0,
                                          scalar2=None, op0=OP.mult),
      wait=[("lp", ld_params)], inc=("v", 1))
    v_scb = P.val("v")
    ln_war = 0
    vt_ones_v = None
    a_kvfree = None
    v_vt = None

    for s in range(nseq):
        t_wb = P.val("t")
        for c in range(8):
            E("gpsimd", lambda h, c=c: h.dma_start(out=wbuf[:, c, :],
                                                   in_=wqkvT[c * 128:(c + 1) * 128, 1024:3072]),
              wait=[("t", t_wb)] if (c == 0 and s > 0) else None, inc=("w", 16))
        w_kv = P.val("w")
        for tt in range(NTT):
            tsl = slice(tt * 512, (tt + 1) * 512)
            st_now = P.val("st")
            v_now = P.val("v")
            for c in range(8):
                E("sync", lambda h, c=c, s=s, tsl=tsl: h.dma_start(
                    out=xtile[:, c, :], in_=xT[s, c * 128:(c + 1) * 128, tsl]),
                  wait=[("st", st_now), ("v", v_now)] if c == 0 else None, inc=("ld", 16))
            ld_x = P.val("ld")
            for c in range(8):
                w0 = [("ld", ld_x), ("t", ln_war)] if c == 0 else None
                E("vector", lambda h, c=c: h.tensor_copy(out=xb[:, c, :], in_=xtile[:, c, :]), wait=w0)
                E("vector", lambda h, c=c: h.tensor_tensor(
                    out=sqb[:, c, :], in0=xb[:, c, :], in1=xb[:, c, :], op=OP.mult),
                  inc=("v", 1) if c == 7 else None)
            v_cast = P.val("v")
            for c in range(8):
                E("tensor", lambda h, c=c: h.matmul(psB2[:, 0:512], ones128[:, :], xb[:, c, :],
                                                    start=(c == 0), stop=(c == 7)),
                  wait=[("v", v_cast)] if c == 0 else None)
            for c in range(8):
                E("tensor", lambda h, c=c: h.matmul(psB2[:, 512:1024], ones128[:, :], sqb[:, c, :],
                                                    start=(c == 0), stop=(c == 7)),
                  inc=("t", 1) if c == 7 else None)
            t_stats = P.val("t")
            ln_war = t_stats
            E("vector", lambda h: h.tensor_scalar(out=sD[:, :], in0=psB2[:, 0:512],
                                                  scalar1=1.0 / 1024.0, scalar2=None, op0=OP.mult),
              wait=[("t", t_stats)])
            E("vector", lambda h: h.tensor_scalar(out=sG[:, :], in0=psB2[:, 512:1024],
                                                  scalar1=1.0 / 1024.0, scalar2=None, op0=OP.mult))
            E("vector", lambda h: h.tensor_tensor(out=sE[:, :], in0=sD[:, :], in1=sD[:, :], op=OP.mult))
            E("vector", lambda h: h.tensor_tensor(out=sE[:, :], in0=sG[:, :], in1=sE[:, :],
                                                  op=OP.subtract), inc=("v", 1))
            v_var = P.val("v")
            E("scalar", lambda h: h.activation(out=sG[:, :], in_=sE[:, :], func=AF.Ln, bias=1e-5),
              wait=[("v", v_var)], inc=("a", 1))
            a_ln = P.val("a")
            E("scalar", lambda h: h.activation(out=sE[:, :], in_=sG[:, :], func=AF.Exp, scale=-0.5),
              wait=[("a", a_ln)], inc=("a", 1))
            a_rstd = P.val("a")
            E("vector", lambda h: h.tensor_tensor(out=sF[:, :], in0=sD[:, :], in1=sE[:, :], op=OP.mult),
              wait=[("a", a_rstd)])
            for c in range(8):
                E("vector", lambda h, c=c: h.tensor_tensor(out=xtile[:, c, :], in0=xtile[:, c, :],
                                                           in1=sE[:, :], op=OP.mult))
                E("vector", lambda h, c=c, tsl=tsl: h.tensor_tensor(
                    out=xn[:, c, tsl], in0=xtile[:, c, :], in1=sF[:, :], op=OP.subtract),
                  inc=("v", 1) if c == 7 else None)
        v_xn = P.val("v")
        if vt_ones_v is None:
            E("vector", lambda h: h.memset(
                vt[:, :, :].rearrange("p a (h w) -> p a h w", w=65)[:, :, :, 64:65], 1.0),
              inc=("v", 1))
            vt_ones_v = P.val("v")

        kv_add_prev = v_xn
        for ts in range(NTS):
            ssl = slice(ts * 128, (ts + 1) * 128)
            for g4 in range(4):
                for c in range(8):
                    w0 = None
                    if c == 0 and g4 == 0:
                        if ts == 0:
                            w0 = [("v", v_xn), ("w", w_kv)]
                            if a_kvfree is not None:
                                w0 += [("a", a_kvfree)]
                        else:
                            w0 = [("a", a_kvfree)]
                    if c == 0 and g4 == 2 and v_vt is not None:
                        w0 = [("v", v_vt)]
                    E("tensor", lambda h, c=c, g4=g4, ssl=ssl: h.matmul(
                        psBig[:, g4 * 512:(g4 + 1) * 512], xn[:, c, ssl],
                        wbuf[:, c, g4 * 512:(g4 + 1) * 512],
                        start=(c == 0), stop=(c == 7)), wait=w0,
                        inc=("t", 1) if (c == 7 and g4 in (1, 3)) else None)
                if g4 == 1:
                    t_kh = P.val("t")
                    E("scalar", lambda h: h.activation(out=sA[:, :], in_=psBig[:, 0:1024],
                                                       func=AF.Relu, scale=-1.0),
                      wait=[("t", t_kh), ("v", kv_add_prev)], inc=("a", 1))
                    a_r1 = P.val("a")
                    E("scalar", lambda h: h.activation(out=sC[:, :], in_=psBig[:, 0:1024],
                                                       func=AF.Relu),
                      inc=("a", 1))
                    a_kvfree = P.val("a")
                    E("scalar", lambda h: h.activation(out=sB[:, :], in_=sA[:, :],
                                                       func=AF.Exp, scale=-1.0),
                      wait=[("a", a_r1)], inc=("a", 1))
                    a_exp = P.val("a")
                    E("vector", lambda h, ts=ts: h.tensor_tensor(out=kp[:, ts, :], in0=sB[:, :],
                                                                 in1=sC[:, :], op=OP.add),
                      wait=[("a", a_exp), ("v", vt_ones_v)], inc=("v", 1))
                    kv_add_prev = P.val("v")
            t_kv = P.val("t")
            E("vector", lambda h, ts=ts: h.tensor_copy(
                out=vt[:, ts, :].rearrange("p (h w) -> p h w", w=65)[:, :, 0:64],
                in_=psBig[:, 1024:2048].rearrange("p (h w) -> p h w", w=64)),
              wait=[("t", t_kv)], inc=("v", 1))
            v_vt = P.val("v")
        v_kv_done = P.val("v")

        t_all_kv = P.val("t")
        for c in range(8):
            E("gpsimd", lambda h, c=c: h.dma_start(out=wbuf[:, c, 0:1024],
                                                   in_=wqkvT[c * 128:(c + 1) * 128, 0:1024]),
              wait=[("t", t_all_kv)] if c == 0 else None, inc=("w", 16))
        w_q = P.val("w")
        a_start = P.val("a")
        ps_free = {id(psQ1): a_start, id(psQ2): a_start}
        q_add_prev = v_kv_done
        for m in range(8):
            for tt in range(NTT):
                ps = psQ1 if (m * NTT + tt) % 2 == 0 else psQ2
                tsl = slice(tt * 512, (tt + 1) * 512)
                for c in range(8):
                    w0 = None
                    if c == 0:
                        w0 = [("w", w_q)]
                        if ps_free[id(ps)] is not None:
                            w0.append(("a", ps_free[id(ps)]))
                    E("tensor", lambda h, c=c, m=m, tsl=tsl, ps=ps: h.matmul(
                        ps[:, :], wbuf[:, c, m * 128:(m + 1) * 128], xn[:, c, tsl],
                        start=(c == 0), stop=(c == 7)), wait=w0,
                        inc=("t", 1) if c == 7 else None)
                t_q = P.val("t")
                E("scalar", lambda h, ps=ps: h.activation(out=sA[:, 0:512], in_=ps[:, :],
                                                          func=AF.Relu, scale=-1.0),
                  wait=[("t", t_q), ("v", q_add_prev)], inc=("a", 1))
                a_r1 = P.val("a")
                E("scalar", lambda h, ps=ps: h.activation(out=sC[:, 0:512], in_=ps[:, :], func=AF.Relu),
                  inc=("a", 1))
                ps_free[id(ps)] = P.val("a")
                E("scalar", lambda h: h.activation(out=sB[:, 0:512], in_=sA[:, 0:512],
                                                   func=AF.Exp, scale=-1.0),
                  wait=[("a", a_r1)], inc=("a", 1))
                a_q = P.val("a")
                E("vector", lambda h, m=m, tsl=tsl: h.tensor_tensor(
                    out=qp[:, m, tsl], in0=sB[:, 0:512], in1=sC[:, 0:512], op=OP.add),
                  wait=[("a", a_q)], inc=("v", 1))
                q_add_prev = P.val("v")
        v_q_done = P.val("v")
        a_q_done = P.val("a")

        kvs_evac = None
        for p in range(NPAIR):
            csl = slice(p * 130, (p + 1) * 130)
            for ts in range(NTS):
                w0 = None
                if ts == 0:
                    w0 = [("v", v_kv_done)]
                    if kvs_evac is not None:
                        w0.append(("v", kvs_evac))
                E("tensor", lambda h, p=p, ts=ts, csl=csl: h.matmul(
                    psB2[:, 0:130], kp[:, ts, p * 128:(p + 1) * 128], vt[:, ts, csl],
                    start=(ts == 0), stop=(ts == NTS - 1)), wait=w0,
                    inc=("t", 1) if ts == NTS - 1 else None)
            t_p = P.val("t")
            E("vector", lambda h, csl=csl: h.tensor_copy(out=kvs[:, csl], in_=psB2[:, 0:130]),
              wait=[("t", t_p)], inc=("v", 1))
            kvs_evac = P.val("v")
        v_kvs = P.val("v")

        for p in range(NPAIR):
            E("vector", lambda h, p=p: h.tensor_copy(
                out=ksumC[:, p, :],
                in_=kvs[:, p * 130 + 64:p * 130 + 65].broadcast_to([128, 64])),
              wait=[("v", v_kvs)] if p == 0 else None, inc=("v", 1) if p == 7 else None)
        v_ksumC = P.val("v")
        mult_done = [None] * 4
        addz_done = [None] * 2
        for p in range(NPAIR):
            for tt in range(NTT):
                idx = p * NTT + tt
                tsl = slice(tt * 512, (tt + 1) * 512)
                psO = psBig[:, (idx % 4) * 512:(idx % 4) * 512 + 512]
                psZ2 = psB2[:, (idx % 2) * 512:(idx % 2) * 512 + 512]
                sZ = sD if idx % 2 == 0 else sE
                w0 = [("v", v_ksumC), ("a", a_q_done)] if idx == 0 else []
                if addz_done[idx % 2] is not None:
                    w0.append(("a", addz_done[idx % 2]))
                E("tensor", lambda h, p=p, tsl=tsl, psZ2=psZ2: h.matmul(
                    psZ2[0:64, :], ksumC[0:64, p, :], qp[0:64, p, tsl],
                    start=True, stop=True, tile_position=(0, 0)), wait=w0 or None)
                E("tensor", lambda h, p=p, tsl=tsl, psZ2=psZ2: h.matmul(
                    psZ2[64:128, :], ksumC[64:128, p, :], qp[64:128, p, tsl],
                    start=True, stop=True, tile_position=(64, 64)), inc=("t", 1))
                t_z = P.val("t")
                w1 = []
                if mult_done[idx % 4] is not None:
                    w1.append(("v", mult_done[idx % 4]))
                E("tensor", lambda h, p=p, tsl=tsl, psO=psO: h.matmul(
                    psO[0:64, :], kvs[0:64, p * 130:p * 130 + 64], qp[0:64, p, tsl],
                    start=True, stop=True, tile_position=(0, 0)), wait=w1 or None)
                E("tensor", lambda h, p=p, tsl=tsl, psO=psO: h.matmul(
                    psO[64:128, :], kvs[64:128, p * 130 + 65:p * 130 + 129], qp[64:128, p, tsl],
                    start=True, stop=True, tile_position=(64, 64)), inc=("t", 1))
                t_o = P.val("t")
                E("scalar", lambda h, psZ2=psZ2, sZ=sZ: h.activation(
                    out=sZ[:, :], in_=psZ2[:, :], func=AF.Ln, bias=1e-6),
                  wait=[("t", t_z)])
                E("scalar", lambda h, sZ=sZ: h.activation(
                    out=sZ[:, :], in_=sZ[:, :], func=AF.Exp, scale=-1.0), inc=("a", 1))
                addz_done[idx % 2] = P.val("a")
                a_z = addz_done[idx % 2]
                E("vector", lambda h, p=p, tsl=tsl, psO=psO, sZ=sZ: h.tensor_tensor(
                    out=qp[:, p, tsl], in0=psO[:, :], in1=sZ[:, :], op=OP.mult),
                  wait=[("t", t_o), ("a", a_z)], inc=("v", 1))
                mult_done[idx % 4] = P.val("v")
        v_o_done = P.val("v")

        t_now = P.val("t")
        for c in range(8):
            E("gpsimd", lambda h, c=c: h.dma_start(out=kp[:, c, :],
                                                   in_=woutT[c * 128:(c + 1) * 128, :]),
              wait=[("t", t_now), ("v", v_o_done)] if c == 0 else None, inc=("w", 16))
            E("gpsimd", lambda h, c=c: h.dma_start(out=kp[:, 8 + c, :],
                                                   in_=cbT[c * 128:(c + 1) * 128, :]),
              inc=("w", 16))
        w_epi = P.val("w")
        epi_free = None
        epi_v = None
        for tt in range(NTT):
            tsl = slice(tt * 512, (tt + 1) * 512)
            st_now = P.val("st")
            for c in range(8):
                w0 = [("st", st_now), ("a", a_q_done)] if c == 0 else None
                E("sync", lambda h, c=c, s=s, tsl=tsl: h.dma_start(
                    out=xtile[:, c, :], in_=xT[s, c * 128:(c + 1) * 128, tsl]),
                  wait=w0, inc=("ld", 16))
            ld_x2 = P.val("ld")
            for m in range(8):
                for c in range(8):
                    w0 = None
                    if c == 0:
                        w0 = [("w", w_epi)]
                        if epi_free is not None:
                            w0.append(("a", epi_free))
                    E("tensor", lambda h, c=c, m=m, tsl=tsl: h.matmul(
                        psQ1[:, :], kp[:, c, m * 128:(m + 1) * 128], qp[:, c, tsl],
                        start=(c == 0), stop=(c == 7)), wait=w0)
                    E("tensor", lambda h, c=c, m=m, tsl=tsl: h.matmul(
                        psQ2[:, :], kp[:, 8 + c, m * 128:(m + 1) * 128], xn[:, c, tsl],
                        start=(c == 0), stop=(c == 7)),
                        inc=("t", 1) if c == 7 else None)
                t_epi = P.val("t")
                w_ea = [("t", t_epi), ("v", v_scb)]
                if epi_v is not None:
                    w_ea.append(("v", epi_v))
                E("scalar", lambda h, m=m: h.activation(out=ea[:, :], in_=psQ1[:, :], func=AF.Identity,
                                                        bias=boutS[:, m:m + 1]),
                  wait=w_ea, inc=("a", 1))
                E("scalar", lambda h, m=m: h.activation(out=ef[:, :], in_=psQ2[:, :], func=AF.Identity,
                                                        scale=scbS[:, m:m + 1], bias=fbS[:, m:m + 1]),
                  wait=[("v", v_scb)], inc=("a", 1))
                epi_free = P.val("a")
                a_e = P.val("a")
                E("vector", lambda h, m=m: h.tensor_tensor(out=xtile[:, m, :], in0=xtile[:, m, :],
                                                           in1=ea[:, :], op=OP.add),
                  wait=[("a", a_e), ("ld", ld_x2)])
                E("vector", lambda h, m=m: h.tensor_tensor(out=xtile[:, m, :], in0=xtile[:, m, :],
                                                           in1=ef[:, :], op=OP.add), inc=("v", 1))
                epi_v = P.val("v")
                v_e = P.val("v")
                E("sync", lambda h, m=m, s=s, tsl=tsl: h.dma_start(
                    out=x1T[s, m * 128:(m + 1) * 128, tsl], in_=xtile[:, m, :]),
                  wait=[("v", v_e)], inc=("st", 16))
        ln_war = P.val("t")





def emit_ln(P, nc, xT, s, xtile, xb, sqb, sD, sE, sF, sG, xn, ones128, psSt, state, W=512):
    E = P.emit
    for tt in range(T // W):
        tsl = slice(tt * W, (tt + 1) * W)
        st_now = P.val("st")
        v_now = P.val("v")
        for c in range(8):
            E("sync", lambda h, c=c, s=s, tsl=tsl: h.dma_start(
                out=xtile[:, c, :], in_=xT[s, c * 128:(c + 1) * 128, tsl]),
              wait=[("st", st_now), ("v", v_now)] if c == 0 else None, inc=("ld", 16))
        ld_x = P.val("ld")
        for c in range(8):
            w0 = [("ld", ld_x), ("t", state.get("ln_war", 0))] if c == 0 else None
            E("vector", lambda h, c=c: h.tensor_copy(out=xb[:, c, :], in_=xtile[:, c, :]), wait=w0)
            E("vector", lambda h, c=c: h.tensor_tensor(
                out=sqb[:, c, :], in0=xb[:, c, :], in1=xb[:, c, :], op=OP.mult),
              inc=("v", 1) if c == 7 else None)
        v_cast = P.val("v")
        for c in range(8):
            E("tensor", lambda h, c=c: h.matmul(psSt[:, 0:W], ones128[:, :], xb[:, c, :],
                                                start=(c == 0), stop=(c == 7)),
              wait=[("v", v_cast), ("a", state.get("ps_war", 0))] if c == 0 else None)
        for c in range(8):
            E("tensor", lambda h, c=c: h.matmul(psSt[:, 512:512 + W], ones128[:, :], sqb[:, c, :],
                                                start=(c == 0), stop=(c == 7)),
              inc=("t", 1) if c == 7 else None)
        t_stats = P.val("t")
        state["ln_war"] = t_stats
        E("vector", lambda h: h.tensor_scalar(out=sD[:, 0:W], in0=psSt[:, 0:W],
                                              scalar1=1.0 / 1024.0, scalar2=None, op0=OP.mult),
          wait=[("t", t_stats)])
        E("vector", lambda h: h.tensor_scalar(out=sG[:, 0:W], in0=psSt[:, 512:512 + W],
                                              scalar1=1.0 / 1024.0, scalar2=None, op0=OP.mult))
        E("vector", lambda h: h.tensor_tensor(out=sE[:, 0:W], in0=sD[:, 0:W], in1=sD[:, 0:W],
                                              op=OP.mult))
        E("vector", lambda h: h.tensor_tensor(out=sE[:, 0:W], in0=sG[:, 0:W], in1=sE[:, 0:W],
                                              op=OP.subtract), inc=("v", 1))
        v_var = P.val("v")
        E("scalar", lambda h: h.activation(out=sG[:, 0:W], in_=sE[:, 0:W], func=AF.Ln, bias=1e-5),
          wait=[("v", v_var)], inc=("a", 1))
        a_ln = P.val("a")
        E("scalar", lambda h: h.activation(out=sE[:, 0:W], in_=sG[:, 0:W], func=AF.Exp, scale=-0.5),
          wait=[("a", a_ln)], inc=("a", 1))
        a_rstd = P.val("a")
        E("vector", lambda h: h.tensor_tensor(out=sF[:, 0:W], in0=sD[:, 0:W], in1=sE[:, 0:W],
                                              op=OP.mult),
          wait=[("a", a_rstd)])
        for c in range(8):
            E("vector", lambda h, c=c: h.tensor_tensor(out=xtile[:, c, :], in0=xtile[:, c, :],
                                                       in1=sE[:, 0:W], op=OP.mult))
            E("vector", lambda h, c=c, tsl=tsl: h.tensor_tensor(
                out=xn[:, c, tsl], in0=xtile[:, c, :], in1=sF[:, 0:W], op=OP.subtract),
              inc=("v", 1) if c == 7 else None)
    return P.val("v")


def emit_normalize(P, ps, zps, dnb, rcb, zbb, out_ap, ones1f, t_o, eps):
    E = P.emit
    E("vector", lambda h, ps=ps, dnb=dnb: h.tensor_copy(out=dnb[0:1, :], in_=ps[64:65, :]),
      wait=[("t", t_o)], inc=("v", 1))
    v_dn = P.val("v")
    E("scalar", lambda h, dnb=dnb, rcb=rcb: h.activation(
        out=rcb[0:1, :], in_=dnb[0:1, :], func=AF.Ln, bias=eps),
      wait=[("v", v_dn)], inc=("a", 1))
    a_ln2 = P.val("a")
    E("scalar", lambda h, rcb=rcb: h.activation(
        out=rcb[0:1, :], in_=rcb[0:1, :], func=AF.Exp, scale=-1.0),
      wait=[("a", a_ln2)], inc=("a", 1))
    a_rc = P.val("a")
    E("tensor", lambda h, rcb=rcb, zps=zps: h.matmul(
        zps[0:64, :], ones1f[0:1, 0:64], rcb[0:1, :], start=True, stop=True),
      wait=[("a", a_rc)], inc=("t", 1))
    t_z = P.val("t")
    E("vector", lambda h, zbb=zbb, zps=zps: h.tensor_copy(out=zbb[:, :], in_=zps[0:64, :]),
      wait=[("t", t_z)], inc=("v", 1))
    E("vector", lambda h, ps=ps, zbb=zbb, out_ap=out_ap: h.tensor_tensor(
        out=out_ap, in0=ps[0:64, :], in1=zbb[:, :], op=OP.mult), inc=("v", 1))
    return P.val("v")


DBGP = 1

def emit_l1a(nc, P, d, nseq=2):
    xT = d["xT"]
    wqT = d["wqT"]
    wkT = d["wkT"]
    wvT = d["wvT"]
    woT = d["woT"]
    bqf = d["bqf"]
    bkf = d["bkf"]
    bvr = d["bvr"]
    bof = d["bof"]
    attnT = d["attnT"]

    xn = P.sbuf("xn", [128, NCH, T], BF16)
    wq = P.sbuf("wq", [128, NCH, 256], BF16)
    wk = P.sbuf("wk", [128, NCH, 256], BF16)
    wv = P.sbuf("wv", [128, NCH, 1024], BF16)
    qf = P.sbuf("qf", [128, T], BF16)
    kf = P.sbuf("kf", [128, T], BF16)
    vt2 = P.sbuf("vt2", [128, 16, 448], BF16)
    PT1 = P.sbuf("PT1", [128, 8192], BF16)
    PT2 = P.sbuf("PT2", [128, 8192], BF16)
    PT3 = P.sbuf("PT3", [128, 8192], BF16)
    PT4 = P.sbuf("PT4", [128, 8192], BF16)
    Obuf = P.sbuf("Obuf", [128, NCH, T], BF16)
    U = P.sbuf("U", [128, NCH, 256], F32)
    SP = P.sbuf("SP", [128, 4096], BF16)
    sD = P.sbuf("sD", [128, 512], F32)
    sE = P.sbuf("sE", [128, 512], F32)
    sF = P.sbuf("sF", [128, 512], F32)
    sG = P.sbuf("sG", [128, 512], F32)
    zbuf = P.sbuf("zbuf", [128, 512], F32)
    bvB = P.sbuf("bvB", [128, 1024], BF16)
    ones128 = P.sbuf("ones128", [128, 128], BF16)
    ones1f = P.sbuf("ones1f", [1, 128], F32)
    bqS = P.sbuf("bqS", [128, 8], F32)
    bkS = P.sbuf("bkS", [128, 8], F32)
    boS = P.sbuf("boS", [128, 8], F32)
    ident = P.sbuf("ident", [128, 128], F32)
    swapF = P.sbuf("swapF", [128, 128], F32)
    mx = P.sbuf("mx", [128, 2], F32)
    mrow = P.sbuf("mrow", [1, 128], F32)
    mrow2 = P.sbuf("mrow2", [1, 128], F32)
    cb = P.sbuf("cb", [128, 1], F32)

    xtile = U
    xb = SP[:, 0:2048].rearrange("p (a b) -> p a b", b=256)
    sqb = SP[:, 2048:4096].rearrange("p (a b) -> p a b", b=256)
    bvF = PT1[:, :].bitcast(F32)[0:1, 0:1024]

    psA = P.psum("psA", [128, 2048])
    psB = P.psum("psB", [128, 2048])
    SLOT = [psA[:, 0:1024], psA[:, 1024:2048]]
    psAVE = psB[:, 1024:1536]
    psAVO = psB[:, 1536:2048]
    psW = psB[:, 0:512]
    psTR1 = psB[0:1, 1024:1152]
    psTR2 = psB[0:1, 1152:1280]
    psCB = psB[:, 1280:1281]

    E = P.emit
    state = {}

    E("vector", lambda h: h.memset(ones128[:, :], 1.0))
    E("vector", lambda h: h.memset(ones1f[:, :], 1.0))
    E("gpsimd", lambda h: h.memset(ident[:, :], 0.0), inc=("g", 1))
    E("gpsimd", lambda h: h.affine_select(out=ident[:, :], in_=ident[:, :],
                                          compare_op=OP.not_equal, fill=1.0, base=0,
                                          pattern=[[-1, 128]], channel_multiplier=1),
      wait=[("g", P.val("g"))], inc=("g", 1))
    g_id = P.val("g")
    E("gpsimd", lambda h: h.memset(swapF[:, :], 0.0))
    E("gpsimd", lambda h: h.affine_select(out=swapF[:, :], in_=swapF[:, :],
                                          compare_op=OP.not_equal, fill=1.0, base=64,
                                          pattern=[[-1, 128]], channel_multiplier=1))
    E("gpsimd", lambda h: h.affine_select(out=swapF[:, :], in_=swapF[:, :],
                                          compare_op=OP.not_equal, fill=1.0, base=-64,
                                          pattern=[[-1, 128]], channel_multiplier=1),
      inc=("g", 1))
    g_sw = P.val("g")
    vt7 = vt2[:, :, :].rearrange("p m (k w) -> p m k w", w=64)
    E("vector", lambda h: h.memset(vt7[:, :, 0:1, :], 1.0))
    E("vector", lambda h: h.memset(vt7[:, :, 3:4, :], 1.0))
    E("vector", lambda h: h.memset(vt7[:, :, 6:7, :], 1.0), inc=("v", 1))
    E("sync", lambda h: h.dma_start(out=bqS[:, :], in_=bqf), inc=("lp", 16))
    E("sync", lambda h: h.dma_start(out=bkS[:, :], in_=bkf), inc=("lp", 16))
    E("sync", lambda h: h.dma_start(out=boS[:, :], in_=bof), inc=("lp", 16))
    E("sync", lambda h: h.dma_start(out=bvF, in_=bvr), inc=("lp", 16))
    lp = P.val("lp")
    E("vector", lambda h: h.tensor_scalar(out=bqS[:, :], in0=bqS[:, :], scalar1=0.125,
                                          scalar2=None, op0=OP.mult),
      wait=[("lp", lp)], inc=("v", 1))
    v_params = P.val("v")
    for c in range(8):
        E("gpsimd", lambda h, c=c: h.dma_start(out=wq[:, c, 0:128],
                                               in_=wqT[c * 128:(c + 1) * 128, 0:128]),
          inc=("w", 16))
        E("gpsimd", lambda h, c=c: h.dma_start(out=wk[:, c, 0:128],
                                               in_=wkT[c * 128:(c + 1) * 128, 0:128]),
          inc=("w", 16))
    w_pair = {0: P.val("w")}
    tqp_hist = {}
    tkp_hist = {}
    E("tensor", lambda h: h.matmul(psB[:, 0:512], ones1f[0:1, 0:128], bvF[:, 0:512],
                                   start=True, stop=True),
      wait=[("lp", lp), ("v", v_params)])
    E("tensor", lambda h: h.matmul(psB[:, 512:1024], ones1f[0:1, 0:128], bvF[:, 512:1024],
                                   start=True, stop=True), inc=("t", 1))
    t_bvb = P.val("t")
    E("vector", lambda h: h.tensor_copy(out=bvB[:, :], in_=psB[:, 0:1024]),
      wait=[("t", t_bvb)], inc=("v", 1))
    v_bvb = P.val("v")

    hz = {
        "slot": [0, 0],
        "t_pt": [0, 0],
        "t_avden": 0,
        "v_norm": 0,
        "a_rc": 0,
        "t_swap": 0,
        "a_epi": 0,
    }

    for s in range(nseq):
        t_wv = P.val("t")
        for c in range(8):
            E("gpsimd", lambda h, c=c: h.dma_start(out=wv[:, c, :],
                                                   in_=wvT[c * 128:(c + 1) * 128, :]),
              wait=[("t", t_wv)] if (c == 0 and s > 0) else None, inc=("w", 16))
        w_v = P.val("w")

        v_xn = emit_ln(P, nc, xT, s, xtile, xb, sqb, sD, sE, sF, sG, xn, ones128, psA, state,
                       W=256)
        for g in range(4):
            pend0 = state.pop("l1a_pend", None) or []
            while pend0:
                pend0.pop(0)()
            fsl = slice(g * 256, (g + 1) * 256)
            v_evac_prev = [None, None]
            for ts in range(16):
                ssl = slice(ts * 128, (ts + 1) * 128)
                psV = psB[:, (ts % 2) * 512:(ts % 2) * 512 + 256]
                w0 = []
                if ts == 0:
                    w0 += [("v", v_xn), ("w", w_v), ("a", hz["a_rc"]),
                           ("v", hz["v_norm"]), ("v", v_bvb)]
                if v_evac_prev[ts % 2] is not None:
                    w0.append(("v", v_evac_prev[ts % 2]))
                for c in range(8):
                    E("tensor", lambda h, c=c, ssl=ssl, fsl=fsl, psV=psV: h.matmul(
                        psV, xn[:, c, ssl], wv[:, c, fsl], start=(c == 0), stop=(c == 7)),
                        wait=(w0 or None) if c == 0 else None,
                        inc=("t", 1) if c == 7 else None)
                t_v = P.val("t")
                w1 = [("t", t_v)]
                if ts == 0:
                    w1.append(("t", hz["t_avden"]))
                E("vector", lambda h, ts=ts, psV=psV, g=g: h.tensor_tensor(
                    out=vt2[:, ts, 64:192], in0=psV[:, 0:128],
                    in1=bvB[:, g * 256:g * 256 + 128], op=OP.add),
                  wait=w1)
                E("vector", lambda h, ts=ts, psV=psV, g=g: h.tensor_tensor(
                    out=vt2[:, ts, 256:384], in0=psV[:, 128:256],
                    in1=bvB[:, g * 256 + 128:g * 256 + 256], op=OP.add),
                  inc=("v", 1))
                v_evac_prev[ts % 2] = P.val("v")
            v_vt2 = P.val("v")

            for pp in range(2):
                p = 2 * g + pp
                vi = s * 8 + p
                sl = vi % 2
                pend_prev = state.pop("l1a_pend", None) or []
                if vi + 1 < nseq * 8:
                    pn = (vi + 1) % 8
                    sln = (vi + 1) % 2
                    for c in range(8):
                        w0 = [("t", tqp_hist.get(vi - 1, 0))] if c == 0 else None
                        E("gpsimd", lambda h, c=c, pn=pn, sln=sln: h.dma_start(
                            out=wq[:, c, sln * 128:(sln + 1) * 128],
                            in_=wqT[c * 128:(c + 1) * 128, pn * 128:(pn + 1) * 128]),
                          wait=w0, inc=("w", 16))
                    for c in range(8):
                        w0 = [("t", tkp_hist.get(vi - 1, 0))] if c == 0 else None
                        E("gpsimd", lambda h, c=c, pn=pn, sln=sln: h.dma_start(
                            out=wk[:, c, sln * 128:(sln + 1) * 128],
                            in_=wkT[c * 128:(c + 1) * 128, pn * 128:(pn + 1) * 128]),
                          wait=w0, inc=("w", 16))
                    w_pair[vi + 1] = P.val("w")
                for g4 in range(4):
                    for c in range(8):
                        w0 = None
                        if g4 == 0 and c == 0:
                            w0 = [("w", w_pair[vi]), ("a", max(hz["slot"][0], hz["slot"][1])),
                                  ("a", hz["a_epi"])]
                        E("tensor", lambda h, c=c, g4=g4, sl=sl: h.matmul(
                            psA[:, g4 * 512:(g4 + 1) * 512], wq[:, c, sl * 128:(sl + 1) * 128],
                            xn[:, c, g4 * 512:(g4 + 1) * 512],
                            start=(c == 0), stop=(c == 7)), wait=w0,
                            inc=("t", 1) if (c == 7 and g4 == 3) else None)
                    for _ in range(4):
                        if pend_prev:
                            pend_prev.pop(0)()
                while pend_prev:
                    pend_prev.pop(0)()
                t_qp = P.val("t")
                tqp_hist[vi] = t_qp
                E("vector", lambda h, p=p: h.tensor_scalar(
                    out=qf[:, :], in0=psA[:, 0:2048], scalar1=0.125,
                    scalar2=bqS[:, p:p + 1], op0=OP.mult, op1=OP.add),
                  wait=[("t", t_qp), ("v", v_params)], inc=("v", 1))
                v_qe = P.val("v")
                for g4 in range(4):
                    for c in range(8):
                        w0 = None
                        if g4 == 0 and c == 0:
                            w0 = [("v", max(v_vt2, hz["v_norm"])),
                                  ("a", hz["a_rc"])]
                        E("tensor", lambda h, c=c, g4=g4, sl=sl: h.matmul(
                            psB[:, g4 * 512:(g4 + 1) * 512], wk[:, c, sl * 128:(sl + 1) * 128],
                            xn[:, c, g4 * 512:(g4 + 1) * 512],
                            start=(c == 0), stop=(c == 7)), wait=w0,
                            inc=("t", 1) if (c == 7 and g4 == 3) else None)
                t_kp = P.val("t")
                tkp_hist[vi] = t_kp
                E("vector", lambda h, p=p: h.tensor_scalar(
                    out=kf[:, :], in0=psB[:, 0:2048], scalar1=bkS[:, p:p + 1],
                    scalar2=None, op0=OP.add),
                  wait=[("t", t_kp)], inc=("v", 1))
                v_ke = P.val("v")
                E("vector", lambda h: h.tensor_reduce(out=mx[:, 0:1], in_=qf[:, :], axis=AX.X,
                                                      op=OP.max, apply_absolute_value=True))
                E("vector", lambda h: h.tensor_reduce(out=mx[:, 1:2], in_=kf[:, :], axis=AX.X,
                                                      op=OP.max, apply_absolute_value=True),
                  inc=("v", 1))
                v_mx = P.val("v")
                E("tensor", lambda h: h.transpose(psTR1, mx[:, 0:1], ident[:, :]),
                  wait=[("v", v_mx), ("g", g_id)])
                E("tensor", lambda h: h.transpose(psTR2, mx[:, 1:2], ident[:, :]),
                  inc=("t", 1))
                t_tr = P.val("t")
                E("vector", lambda h: h.tensor_copy(out=mrow[0:1, :], in_=psTR1),
                  wait=[("t", t_tr)])
                E("vector", lambda h: h.tensor_copy(out=mrow2[0:1, :], in_=psTR2))
                E("vector", lambda h: h.tensor_reduce(out=mrow[0:1, 0:1], in_=mrow[0:1, 0:128],
                                                      axis=AX.X, op=OP.max))
                E("vector", lambda h: h.tensor_reduce(out=mrow2[0:1, 0:1], in_=mrow2[0:1, 0:128],
                                                      axis=AX.X, op=OP.max))
                E("vector", lambda h: h.tensor_tensor(out=mrow[0:1, 0:1], in0=mrow[0:1, 0:1],
                                                      in1=mrow2[0:1, 0:1], op=OP.mult))
                E("vector", lambda h: h.tensor_scalar(out=mrow[0:1, 0:1], in0=mrow[0:1, 0:1],
                                                      scalar1=-64.0, scalar2=None, op0=OP.mult),
                  inc=("v", 1))
                v_negc = P.val("v")
                E("tensor", lambda h: h.matmul(psCB, ones1f[0:1, :], mrow[0:1, 0:1],
                                               start=True, stop=True),
                  wait=[("v", v_negc)], inc=("t", 1))
                t_cb = P.val("t")
                E("vector", lambda h: h.tensor_copy(out=cb[:, :], in_=psCB),
                  wait=[("t", t_cb)], inc=("v", 1))
                v_cb = P.val("v")

                PTbufs = [(PT1, PT2), (PT3, PT4)]

                def mk_pend(qb, pp=pp, p=p):
                    PTe, PTo = PTbufs[qb % 2]
                    a14 = hz["slot"][0]
                    a15 = hz["slot"][1]
                    v_norm_gate = hz["v_norm"]
                    v_dve_gate = v_cb
                    qsl_ = slice(qb * 512, (qb + 1) * 512)
                    thunks = []

                    def quad(m, a14=a14, a15=a15, v_norm_gate=v_norm_gate,
                             v_dve_gate=v_dve_gate, PTe=PTe, PTo=PTo, qb=qb):
                        w0h0 = w0h1 = None
                        if m == 0:
                            w0h0 = [("a", a14), ("v", v_dve_gate), ("v", v_norm_gate)]
                            w0h1 = [("a", a15)]
                        E("tensor", lambda h, m=m: h.matmul(
                            psAVE[:, :], vt2[:, m, pp * 192:pp * 192 + 128],
                            PTe[:, m * 512:(m + 1) * 512],
                            start=(m == 0), stop=(m == 15)),
                          wait=w0h0)
                        E("tensor", lambda h, m=m: h.matmul(
                            psAVO[:, :], vt2[:, m, pp * 192 + 128:pp * 192 + 256],
                            PTo[:, m * 512:(m + 1) * 512],
                            start=(m == 0), stop=(m == 15)),
                          wait=w0h1, inc=("t", 1) if m == 15 else None)
                        if m == 15:
                            tv = P.val("t")
                            hz["t_pt"][qb % 2] = tv
                            hz["t_avden"] = tv

                    for m in range(16):
                        thunks.append(lambda m=m: quad(m))

                    def tail(qb=qb, p=p, qsl_=qsl_):
                        t_qd = hz["t_avden"]
                        v_ng = hz["v_norm"]
                        dnb = sD
                        E("vector", lambda h: h.tensor_copy(
                            out=dnb[0:64, :], in_=psAVE[0:64, :]),
                          wait=[("t", t_qd), ("t", hz["t_swap"]),
                                ("st", P.val("st"))])
                        E("vector", lambda h: h.tensor_copy(
                            out=dnb[64:128, :], in_=psAVO[64:128, :]), inc=("v", 1))
                        v_dnb = P.val("v")
                        E("tensor", lambda h: h.matmul(
                            psW[:, :], swapF[:, :], dnb[:, :], start=True, stop=True),
                          wait=[("v", v_dnb), ("a", hz["a_rc"]), ("g", g_sw)],
                          inc=("t", 1))
                        hz["t_swap"] = P.val("t")
                        t_sw = hz["t_swap"]
                        E("scalar", lambda h: h.activation(
                            out=zbuf[:, :], in_=psW[:, :], func=AF.Ln, bias=1e-30),
                          wait=[("t", t_sw), ("v", v_ng)], inc=("a", 1))
                        hz["a_rc"] = P.val("a")
                        E("scalar", lambda h: h.activation(
                            out=zbuf[:, :], in_=zbuf[:, :], func=AF.Exp, scale=-1.0),
                          inc=("a", 1))
                        a_rc = P.val("a")
                        E("vector", lambda h: h.tensor_tensor(
                            out=Obuf[0:64, p, qsl_], in0=psAVO[0:64, :], in1=zbuf[0:64, :],
                            op=OP.mult),
                          wait=[("t", t_qd), ("a", a_rc)])
                        E("vector", lambda h: h.tensor_tensor(
                            out=Obuf[64:128, p, qsl_], in0=psAVE[64:128, :],
                            in1=zbuf[64:128, :], op=OP.mult), inc=("v", 1))
                        hz["v_norm"] = P.val("v")
                    thunks.append(tail)
                    return thunks

                pend = state.get("l1a_pend") or []
                for qb in range(NTT):
                    qsl = slice(qb * 512, (qb + 1) * 512)
                    a_slot = [0] * 16
                    for rr in range(8):
                        if rr > 0:
                            aw = (a_slot[2 * (rr - 1)], a_slot[2 * (rr - 1) + 1])
                        else:
                            aw = (hz["slot"][0], hz["slot"][1])
                        w0h = [[("a", aw[0])], [("a", aw[1])]]
                        if rr == 0 and qb == 0:
                            w0h[0].append(("v", v_ke))
                        for ii, (hh, j) in enumerate(((0, 0), (1, 0), (0, 1), (1, 1))):
                            m = 2 * rr + j
                            base = hh * 64
                            ps = SLOT[hh]
                            E("tensor", lambda h, m=m, j=j, ps=ps, base=base, qsl=qsl:
                              h.matmul(ps[:, j * 512:(j + 1) * 512],
                                       kf[base:base + 64, m * 128:(m + 1) * 128],
                                       qf[base:base + 64, qsl], start=True, stop=True,
                                       tile_position=(base, 0)),
                              wait=w0h[hh] if j == 0 else None,
                              inc=("t", 1) if ii in (2, 3) else None)
                        t_sc = P.val("t")
                        t_h0 = t_sc - 1
                        for hh in (0, 1):
                            PT = PTbufs[qb % 2][hh]
                            ps = SLOT[hh]
                            w0e = [("t", t_h0 if hh == 0 else t_sc)]
                            if rr == 0 and hh == 0:
                                w0e.append(("v", v_cb))
                                w0e.append(("t", hz["t_pt"][qb % 2]))
                            E("scalar", lambda h, rr=rr, ps=ps, PT=PT: h.activation(
                                out=PT[:, rr * 1024:(rr + 1) * 1024], in_=ps[:, 0:1024],
                                func=AF.Exp, bias=cb[:, 0:1]),
                              wait=w0e, inc=("a", 1))
                            a_slot[2 * rr + hh] = P.val("a")
                            hz["slot"][hh] = a_slot[2 * rr + hh]
                        for _ in range(2):
                            if pend:
                                pend.pop(0)()
                        if rr == 7:
                            while pend:
                                pend.pop(0)()
                    pend = mk_pend(qb)
                state["l1a_pend"] = pend

        pendx = state.pop("l1a_pend", None) or []
        while pendx:
            pendx.pop(0)()
        v_obuf = P.val("v")
        t_now2 = P.val("t")
        for c in range(8):
            E("gpsimd", lambda h, c=c: h.dma_start(out=wv[0:64, c, :],
                                                   in_=woT[c * 128 + 64:(c + 1) * 128, :]),
              wait=[("t", t_now2)] if c == 0 else None, inc=("w", 16))
            E("gpsimd", lambda h, c=c: h.dma_start(out=wv[64:128, c, :],
                                                   in_=woT[c * 128:c * 128 + 64, :]),
              inc=("w", 16))
        w_o = P.val("w")
        a_evac = [None, None]
        for tt in range(NTT):
            tsl = slice(tt * 512, (tt + 1) * 512)
            for m in range(8):
                mi = tt * 8 + m
                psE_ = psA[:, (mi % 2) * 512:(mi % 2) * 512 + 512]
                w0 = None
                if mi == 0:
                    w0 = [("w", w_o), ("v", v_obuf),
                          ("a", max(hz["slot"][0], hz["slot"][1]))]
                elif a_evac[mi % 2] is not None:
                    w0 = [("a", a_evac[mi % 2])]
                for c in range(8):
                    E("tensor", lambda h, c=c, m=m, tsl=tsl, psE_=psE_: h.matmul(
                        psE_, wv[:, c, m * 128:(m + 1) * 128], Obuf[:, c, tsl],
                        start=(c == 0), stop=(c == 7)), wait=w0 if c == 0 else None,
                        inc=("t", 1) if c == 7 else None)
                t_ep = P.val("t")
                w_ea = [("t", t_ep)]
                if "epi_st" in state:
                    w_ea.append(("st", state["epi_st"]))
                eb = sD if mi % 2 == 0 else sE
                E("scalar", lambda h, psE_=psE_, eb=eb, m=m: h.activation(
                    out=eb[:, :], in_=psE_, func=AF.Identity, bias=boS[:, m:m + 1]),
                  wait=w_ea, inc=("a", 1))
                a_evac[mi % 2] = P.val("a")
                a_ev = P.val("a")
                E("sync", lambda h, m=m, s=s, tsl=tsl, eb=eb: h.dma_start(
                    out=attnT[s, m * 128:(m + 1) * 128, tsl], in_=eb[:, :]),
                  wait=[("a", a_ev)], inc=("st", 16))
                state["epi_st"] = P.val("st")
        hz["a_epi"] = P.val("a")
        state["ps_war"] = hz["a_epi"]




NPAIR_F = 4
NQ = 4
QT = 512
NCK = 4





def emit_l1f(nc, P, d, nseq=2, dbg=False):
    x1T = d["x1T"]
    attnT = d["attnT"]
    wqD = d["wq"]
    wkD = d["wk"]
    wvD = d["wv"]
    wqaD = d["wqa"]
    wqbD = d["wqb"]
    wkaD = d["wka"]
    wkbD = d["wkb"]
    wg1aD = d["wg1a"]
    wg1bD = d["wg1b"]
    bg1D = d["bg1"]
    wg2aD = d["wg2a"]
    wg2bD = d["wg2b"]
    bg2D = d["bg2"]
    weD = d["we"]
    wgaD = d["wga"]
    wgbD = d["wgb"]
    nwD = d["nw"]
    woD = d["wo"]
    out2T = d["out2T"]
    dbgT = {}
    if dbg:
        for nm, shp, dt in (("q1f", [128, 4, 512], BF16), ("q2f", [128, 4, 512], BF16),
                            ("k1f", [128, 4, 512], BF16), ("k2f", [128, 4, 512], BF16),
                            ("k1tm", [128, 4, 512], BF16), ("k2tm", [128, 4, 512], BF16),
                            ("vt", [128, 4, 512], BF16), ("gk1tm", [128, 4, 512], BF16),
                            ("gk2tm", [128, 4, 512], BF16), ("rwm", [128, 4, 8], F32),
                            ("rwb", [128, 4, 512], BF16), ("sgf", [128, 4, 512], BF16),
                            ("gkc", [128, 640], BF16), ("expg", [128, 640], BF16),
                            ("netm", [128, 640], BF16), ("qcat", [128, 640], BF16),
                            ("kcat", [128, 640], BF16), ("kctm", [128, 640], BF16),
                            ("dec", [128, 5], F32), ("Am", [128, 256], BF16),
                            ("ob", [128, 4, 512], BF16), ("Sst", [128, 4, 5, 128], F32),
                            ("Sbf", [128, 5, 128], BF16)):
            dbgT[nm] = nc.dram_tensor("d_" + nm, shp, dt, kind="ExternalOutput").ap()

    xn = P.sbuf("xn", [128, 8, T], BF16)
    wq = P.sbuf("wqS", [128, 8, 512], BF16)
    wk = P.sbuf("wkS", [128, 8, 512], BF16)
    wv = P.sbuf("wvS", [128, 8, 512], BF16)
    wqa = P.sbuf("wqaS", [128, 8, 64], BF16)
    wka = P.sbuf("wkaS", [128, 8, 64], BF16)
    wga = P.sbuf("wgaS", [128, 8, 64], BF16)
    wqb = P.sbuf("wqbS", [64, 512], BF16)
    wkb = P.sbuf("wkbS", [64, 512], BF16)
    wgb = P.sbuf("wgbS", [64, 512], BF16)
    wg1a = P.sbuf("wg1aS", [128, 8, 16], BF16)
    wg2a = P.sbuf("wg2aS", [128, 8, 16], BF16)
    wg1b = P.sbuf("wg1bS", [16, 512], BF16)
    wg2b = P.sbuf("wg2bS", [16, 512], BF16)
    weS = P.sbuf("weS", [128, 8, 4], BF16)
    wo = P.sbuf("woS", [128, 4, 1024], BF16)
    bg1S = P.sbuf("bg1S", [1, 512], BF16)
    bg2S = P.sbuf("bg2S", [1, 512], BF16)
    bgF = P.sbuf("bgF", [1, 1024], F32)
    nwS = P.sbuf("nwS", [128, 1], F32)
    q1f = P.sbuf("q1f", [128, 4, QT], BF16)
    q2f = P.sbuf("q2f", [128, 4, QT], BF16)
    k1f = P.sbuf("k1f", [128, 4, QT], BF16)
    k2f = P.sbuf("k2f", [128, 4, QT], BF16)
    k1tm = P.sbuf("k1tm", [128, 4, 512], BF16)
    k2tm = P.sbuf("k2tm", [128, 4, 512], BF16)
    vt = P.sbuf("vt", [128, 4, 512], BF16)
    gk1tm = P.sbuf("gk1tm", [128, 4, 512], BF16)
    gk2tm = P.sbuf("gk2tm", [128, 4, 512], BF16)
    gf = P.sbuf("gf", [128, 4, QT], BF16)
    sgf = P.sbuf("sgf", [128, 4, QT], BF16)
    ob = P.sbuf("ob", [128, 4, QT], BF16)
    rwb = P.sbuf("rwb", [128, 4, 512], BF16)
    rwm = P.sbuf("rwm", [128, 4, 8], F32)
    xaf = P.sbuf("xaf", [64, 512], BF16)
    gaf = P.sbuf("gaf", [16, 1024], BF16)
    expg = P.sbuf("expg", [128, 640], BF16)
    negg = P.sbuf("negg", [128, 640], BF16)
    netm = P.sbuf("netm", [128, 640], BF16)
    qcat = P.sbuf("qcat", [128, 640], BF16)
    kcat = P.sbuf("kcat", [128, 640], BF16)
    kctm = P.sbuf("kctm", [128, 640], BF16)
    gkc = P.sbuf("gkc", [128, 640], BF16)
    dec = P.sbuf("dec", [128, 2, 5], F32)
    Am = P.sbuf("Am", [128, 256], BF16)
    Sst = P.sbuf("Sst", [128, 4, 5, 128], F32)
    Sbf = P.sbuf("Sbf", [128, 5, 128], BF16)
    U = P.sbuf("U", [128, 8, 512], F32)
    SP = P.sbuf("SP", [128, 8192], BF16)
    sD = P.sbuf("sD", [128, 512], F32)
    sE = P.sbuf("sE", [128, 512], F32)
    sF = P.sbuf("sF", [128, 512], F32)
    sG = P.sbuf("sG", [128, 512], F32)
    sH = P.sbuf("sH", [128, 1536], F32)
    ones128 = P.sbuf("ones128", [128, 128], BF16)
    ones1b = P.sbuf("ones1b", [1, 128], BF16)
    Lb = P.sbuf("Lb", [128, 128], BF16)
    Lf = P.sbuf("Lf", [128, 128], F32)
    identB = P.sbuf("identB", [128, 128], BF16)
    identF = P.sbuf("identF", [128, 128], F32)
    OSEL = P.sbuf("OSEL", [128, 128], BF16)
    esel = P.sbuf("esel", [4, 4, 128], F32)
    rwT = P.sbuf("rwT", [4, 512], F32)
    ot = U[:, 2, :]
    xb = SP[:, 0:4096].rearrange("p (a b) -> p a b", b=512)
    sqb = SP[:, 4096:8192].rearrange("p (a b) -> p a b", b=512)

    psA_ = P.psum("psA_", [128, 1024])
    psB_ = P.psum("psB_", [128, 1024])
    psC_ = P.psum("psC_", [128, 512])
    psD_ = P.psum("psD_", [128, 512])
    psE_ = P.psum("psE_", [128, 1024])
    psCb = psC_[:, :].bitcast(BF16)

    E = P.emit
    state = {}

    E("vector", lambda h: h.memset(ones128[:, :], 1.0))
    E("vector", lambda h: h.memset(ones1b[:, :], 1.0))
    E("vector", lambda h: h.memset(Sbf[:, :, :], 0.0))
    E("vector", lambda h: h.memset(OSEL[:, :], 0.0))
    E("vector", lambda h: h.memset(OSEL[0:64, 0:64], 1.0))
    E("vector", lambda h: h.memset(OSEL[64:128, 64:128], 1.0), inc=("v", 1))
    E("gpsimd", lambda h: h.memset(Lb[:, :], 1.0))
    E("gpsimd", lambda h: h.affine_select(out=Lb[:, :], in_=Lb[:, :], compare_op=OP.is_ge,
                                          fill=0.0, base=0, pattern=[[1, 128]],
                                          channel_multiplier=-1))
    E("gpsimd", lambda h: h.memset(Lf[:, :], 1.0))
    E("gpsimd", lambda h: h.affine_select(out=Lf[:, :], in_=Lf[:, :], compare_op=OP.is_ge,
                                          fill=0.0, base=0, pattern=[[1, 128]],
                                          channel_multiplier=-1))
    E("gpsimd", lambda h: h.memset(identB[:, :], 0.0))
    E("gpsimd", lambda h: h.affine_select(out=identB[:, :], in_=identB[:, :],
                                          compare_op=OP.not_equal, fill=1.0, base=0,
                                          pattern=[[-1, 128]], channel_multiplier=1))
    E("gpsimd", lambda h: h.memset(identF[:, :], 0.0))
    E("gpsimd", lambda h: h.affine_select(out=identF[:, :], in_=identF[:, :],
                                          compare_op=OP.not_equal, fill=1.0, base=0,
                                          pattern=[[-1, 128]], channel_multiplier=1))
    for e in range(4):
        E("gpsimd", lambda h, e=e: h.memset(esel[:, e, :], 1.0))
        E("gpsimd", lambda h, e=e: h.affine_select(out=esel[:, e, :], in_=esel[:, e, :],
                                                   compare_op=OP.is_equal, fill=0.0,
                                                   base=-e, pattern=[[0, 128]],
                                                   channel_multiplier=1),
          inc=("g", 1) if e == 3 else None)
    g_c = P.val("g")

    E("sync", lambda h: h.dma_start(out=bgF[0:1, 0:512], in_=bg1D), inc=("lp", 16))
    E("sync", lambda h: h.dma_start(out=bgF[0:1, 512:1024], in_=bg2D), inc=("lp", 16))
    E("sync", lambda h: h.dma_start(out=nwS[:, :], in_=nwD), inc=("lp", 16))
    lp = P.val("lp")
    E("vector", lambda h: h.tensor_copy(out=bg1S[0:1, :], in_=bgF[0:1, 0:512]),
      wait=[("lp", lp)])
    E("vector", lambda h: h.tensor_copy(out=bg2S[0:1, :], in_=bgF[0:1, 512:1024]),
      inc=("v", 1))
    v_params = P.val("v")
    wlist = [(wq, wqD, 8, 512), (wk, wkD, 8, 512), (wv, wvD, 8, 512),
             (wqa, wqaD, 8, 64), (wka, wkaD, 8, 64), (wga, wgaD, 8, 64),
             (wg1a, wg1aD, 8, 16), (wg2a, wg2aD, 8, 16), (weS, weD, 8, 4),
             (wo, woD, 4, 1024)]
    for sb, dr, nchk, wdt in wlist:
        for c in range(nchk):
            E("gpsimd", lambda h, sb=sb, dr=dr, c=c: h.dma_start(
                out=sb[:, c, :], in_=dr[c * 128:(c + 1) * 128, :]), inc=("w", 16))
    for sb, dr in ((wqb, wqbD), (wkb, wkbD), (wgb, wgbD), (wg1b, wg1bD), (wg2b, wg2bD)):
        E("gpsimd", lambda h, sb=sb, dr=dr: h.dma_start(out=sb[:, :], in_=dr), inc=("w", 16))
    w_all = P.val("w")

    def MM(ps, lhsT, rhs, start, stop, w0=None, inc=None, tp=None):
        E("tensor", lambda h, ps=ps, lhsT=lhsT, rhs=rhs, start=start, stop=stop, tp=tp:
          h.matmul(ps, lhsT, rhs, start=start, stop=stop,
                   **({"tile_position": tp} if tp else {})),
          wait=w0, inc=inc)

    def ps4(mi):
        return (psA_ if mi < 2 else psB_)[:, (mi % 2) * 512:(mi % 2) * 512 + 512]

    for s in range(nseq):
        v_xn = emit_ln(P, nc, x1T, s, U, xb, sqb, sD, sE, sF, sG, xn, ones128, psA_, state)
        E("vector", lambda h: h.memset(Sst[:, :, :, :], 0.0), inc=("v", 1))
        v_s0 = P.val("v")
        for q in range(NQ):
            q0 = q * QT
            xnq = lambda c: xn[:, c, q0:q0 + QT]
            pv = state.get("p1_war")
            for mi in range(4):
                for c in range(8):
                    w0 = None
                    if mi == 0 and c == 0:
                        w0 = [("v", v_xn), ("w", w_all), ("v", v_s0), ("g", g_c)]
                        if pv:
                            w0 += [("a", pv[0]), ("v", pv[1])]
                    MM(ps4(mi), wq[:, c, mi * 128:(mi + 1) * 128], xnq(c),
                       c == 0, c == 7, w0, ("t", 1) if (c == 7 and mi == 3) else None)
            t_q1 = P.val("t")
            E("scalar", lambda h: h.activation(out=q1f[:, 0:2, :].rearrange("p a b -> p (a b)"),
                                               in_=psA_[:, :], func=AF.Identity, scale=0.125),
              wait=[("t", t_q1)])
            E("scalar", lambda h: h.activation(out=q1f[:, 2:4, :].rearrange("p a b -> p (a b)"),
                                               in_=psB_[:, :], func=AF.Identity, scale=0.125),
              inc=("a", 1))
            a_q1 = P.val("a")
            for mi in range(4):
                for c in range(8):
                    w0 = [("a", a_q1)] if (mi == 0 and c == 0) else None
                    MM(ps4(mi), wk[:, c, mi * 128:(mi + 1) * 128], xnq(c),
                       c == 0, c == 7, w0, ("t", 1) if (c == 7 and mi == 3) else None)
            t_k1 = P.val("t")
            E("scalar", lambda h: h.activation(out=k1f[:, 0:2, :].rearrange("p a b -> p (a b)"),
                                               in_=psA_[:, :], func=AF.Identity),
              wait=[("t", t_k1)])
            E("scalar", lambda h: h.activation(out=k1f[:, 2:4, :].rearrange("p a b -> p (a b)"),
                                               in_=psB_[:, :], func=AF.Identity),
              inc=("a", 1))
            a_prev = P.val("a")
            for ts in range(4):
                ssl = slice(q0 + ts * 128, q0 + (ts + 1) * 128)
                for c in range(8):
                    w0 = [("a", a_prev)] if c == 0 else None
                    MM(psA_[:, 0:512], xn[:, c, ssl], wk[:, c, :], c == 0, c == 7, w0)
                    MM(psA_[:, 512:1024], xn[:, c, ssl], wv[:, c, :], c == 0, c == 7,
                       None, ("t", 1) if c == 7 else None)
                t_kv = P.val("t")
                E("scalar", lambda h, ts=ts: h.activation(out=k1tm[:, ts, :], in_=psA_[:, 0:512],
                                                          func=AF.Identity),
                  wait=[("t", t_kv)])
                E("scalar", lambda h, ts=ts: h.activation(out=vt[:, ts, :], in_=psA_[:, 512:1024],
                                                          func=AF.Identity),
                  inc=("a", 1))
                a_prev = P.val("a")
            for c in range(8):
                w0 = [("a", a_prev)] if c == 0 else None
                MM(psB_[0:64, 0:512], wqa[:, c, :], xnq(c), c == 0, c == 7, w0,
                   ("t", 1) if c == 7 else None)
            t_xa = P.val("t")
            E("scalar", lambda h: h.activation(out=xaf[:, :], in_=psB_[0:64, 0:512],
                                               func=AF.Identity),
              wait=[("t", t_xa)], inc=("a", 1))
            a_xa = P.val("a")
            for mi in range(4):
                w0 = [("a", a_xa)] if mi == 0 else None
                MM(ps4(mi), wqb[:, mi * 128:(mi + 1) * 128], xaf[:, :], True, True, w0,
                   ("t", 1) if mi == 3 else None)
            t_q2 = P.val("t")
            E("scalar", lambda h: h.activation(out=q2f[:, 0:2, :].rearrange("p a b -> p (a b)"),
                                               in_=psA_[:, :], func=AF.Identity, scale=0.125),
              wait=[("t", t_q2)])
            E("scalar", lambda h: h.activation(out=q2f[:, 2:4, :].rearrange("p a b -> p (a b)"),
                                               in_=psB_[:, :], func=AF.Identity, scale=0.125),
              inc=("a", 1))
            a_q2 = P.val("a")
            for c in range(8):
                w0 = [("a", a_q2)] if c == 0 else None
                MM(psB_[0:64, 0:512], wka[:, c, :], xnq(c), c == 0, c == 7, w0,
                   ("t", 1) if c == 7 else None)
            t_ka = P.val("t")
            E("scalar", lambda h: h.activation(out=xaf[:, :], in_=psB_[0:64, 0:512],
                                               func=AF.Identity),
              wait=[("t", t_ka)], inc=("a", 1))
            a_ka = P.val("a")
            v_k2 = None
            for ts in range(4):
                w0 = [("a", a_ka)] if ts == 0 else [("v", v_k2)]
                MM(psA_[:, 0:512], xaf[:, ts * 128:(ts + 1) * 128], wkb[:, :], True, True,
                   w0, ("t", 1))
                t_k2 = P.val("t")
                E("vector", lambda h: h.tensor_copy(out=sH[:, 0:512], in_=psA_[:, 0:512]),
                  wait=[("t", t_k2)])
                sh3 = sH[:, 0:512].rearrange("p (g w) -> p g w", w=64)
                E("vector", lambda h, sh3=sh3: h.tensor_reduce(out=sD[:, 0:8], in_=sh3,
                                                               axis=AX.X, op=OP.max))
                E("vector", lambda h, sh3=sh3: h.tensor_tensor(
                    out=sh3, in0=sh3, in1=sD[:, 0:8].unsqueeze(2).broadcast_to([128, 8, 64]),
                    op=OP.subtract), inc=("v", 1))
                v_sm = P.val("v")
                E("scalar", lambda h: h.activation(out=sH[:, 512:1024], in_=sH[:, 0:512],
                                                   func=AF.Exp),
                  wait=[("v", v_sm)], inc=("a", 1))
                a_sm = P.val("a")
                se3 = sH[:, 512:1024].rearrange("p (g w) -> p g w", w=64)
                E("vector", lambda h, se3=se3: h.tensor_reduce(out=sD[:, 8:16], in_=se3,
                                                               axis=AX.X, op=OP.add),
                  wait=[("a", a_sm)], inc=("v", 1))
                v_ks = P.val("v")
                E("vector", lambda h: h.reciprocal(out=sD[:, 16:24], in_=sD[:, 8:16]),
                  wait=[("v", v_ks)], inc=("v", 1))
                v_kr = P.val("v")
                E("vector", lambda h, se3=se3, ts=ts: h.tensor_tensor(
                    out=k2tm[:, ts, :].rearrange("p (g w) -> p g w", w=64), in0=se3,
                    in1=sD[:, 16:24].unsqueeze(2).broadcast_to([128, 8, 64]), op=OP.mult),
                  wait=[("v", v_kr)], inc=("v", 1))
                v_k2 = P.val("v")
            a_tp = None
            for ts in range(4):
                for mi in range(4):
                    w0 = None
                    if mi == 0:
                        w0 = [("v", v_k2)] if ts == 0 else []
                        if a_tp is not None:
                            w0.append(("a", a_tp))
                        w0 = w0 or None
                    E("tensor", lambda h, ts=ts, mi=mi: h.transpose(
                        psCb[:, mi * 128:(mi + 1) * 128],
                        k2tm[:, ts, mi * 128:(mi + 1) * 128], identB[:, :]),
                      wait=w0, inc=("t", 1) if mi == 3 else None)
                t_tp = P.val("t")
                E("scalar", lambda h, ts=ts: h.activation(
                    out=k2f[:, :, ts * 128:(ts + 1) * 128],
                    in_=psCb[:, 0:512].rearrange("p (a b) -> p a b", b=128),
                    func=AF.Identity),
                  wait=[("t", t_tp)], inc=("a", 1))
                a_tp = P.val("a")
            for c in range(8):
                w0 = [("a", a_tp)] if c == 0 else None
                MM(psB_[0:16, 0:512], wg1a[:, c, :], xnq(c), c == 0, c == 7, w0)
            for c in range(8):
                MM(psB_[0:16, 512:1024], wg2a[:, c, :], xnq(c), c == 0, c == 7,
                   None, ("t", 1) if c == 7 else None)
            t_ga = P.val("t")
            E("scalar", lambda h: h.activation(out=gaf[:, :], in_=psB_[0:16, :],
                                               func=AF.Identity),
              wait=[("t", t_ga)], inc=("a", 1))
            a_ga = P.val("a")
            v_gprev = None
            for gi, (wb, bgS, gkt) in enumerate(((wg1b, bg1S, gk1tm), (wg2b, bg2S, gk2tm))):
                for ts in range(4):
                    w0 = [("a", a_ga), ("v", v_params)] if (gi == 0 and ts == 0) else (
                        [("v", v_gprev)] if v_gprev else None)
                    MM(psA_[:, 0:512], ones1b[0:1, :], bgS[0:1, :], True, False, w0)
                    MM(psA_[:, 0:512], gaf[0:16, gi * 512 + ts * 128: gi * 512 + (ts + 1) * 128],
                       wb[:, :], False, True, None, ("t", 1))
                    t_z = P.val("t")
                    E("scalar", lambda h: h.activation(out=sH[:, 512:1024], in_=psA_[:, 0:512],
                                                       func=AF.Exp, scale=-1.0),
                      wait=[("t", t_z)], inc=("a", 1))
                    a_e1 = P.val("a")
                    E("scalar", lambda h: h.activation(out=sH[:, 512:1024], in_=sH[:, 512:1024],
                                                       func=AF.Ln, bias=1.0),
                      wait=[("a", a_e1)], inc=("a", 1))
                    a_lg = P.val("a")
                    E("vector", lambda h, gkt=gkt, ts=ts: h.tensor_scalar(
                        out=gkt[:, ts, :], in0=sH[:, 512:1024], scalar1=-1.0 / 16.0,
                        scalar2=None, op0=OP.mult),
                      wait=[("a", a_lg)], inc=("v", 1))
                    v_gprev = P.val("v")
            v_rw = v_gprev
            for ts in range(4):
                ssl = slice(q0 + ts * 128, q0 + (ts + 1) * 128)
                for c in range(8):
                    w0 = [("v", v_rw)] if c == 0 else None
                    MM(psC_[:, 0:4], xn[:, c, ssl], weS[:, c, :], c == 0, c == 7, w0,
                       ("t", 1) if c == 7 else None)
                t_e = P.val("t")
                E("vector", lambda h: h.tensor_copy(out=sD[:, 0:4], in_=psC_[:, 0:4]),
                  wait=[("t", t_e)], inc=("v", 1))
                v_ecp = P.val("v")
                E("vector", lambda h: h.tensor_reduce(out=sD[:, 4:5], in_=sD[:, 0:4],
                                                      axis=AX.X, op=OP.max), inc=("v", 1))
                v_em = P.val("v")
                E("vector", lambda h: h.tensor_scalar(out=sD[:, 0:4], in0=sD[:, 0:4],
                                                      scalar1=sD[:, 4:5], scalar2=None,
                                                      op0=OP.subtract),
                  wait=[("v", v_em)], inc=("v", 1))
                v_e1 = P.val("v")
                E("scalar", lambda h: h.activation(out=sD[:, 0:4], in_=sD[:, 0:4], func=AF.Exp),
                  wait=[("v", v_e1)], inc=("a", 1))
                a_e2 = P.val("a")
                E("vector", lambda h: h.tensor_reduce(out=sD[:, 4:5], in_=sD[:, 0:4],
                                                      axis=AX.X, op=OP.add),
                  wait=[("a", a_e2)], inc=("v", 1))
                v_es = P.val("v")
                E("vector", lambda h: h.reciprocal(out=sD[:, 5:6], in_=sD[:, 4:5]),
                  wait=[("v", v_es)], inc=("v", 1))
                v_er = P.val("v")
                E("vector", lambda h: h.tensor_scalar(out=sD[:, 0:4], in0=sD[:, 0:4],
                                                      scalar1=sD[:, 5:6], scalar2=None,
                                                      op0=OP.mult),
                  wait=[("v", v_er)], inc=("v", 1))
                v_en = P.val("v")
                E("vector", lambda h: h.tensor_reduce(out=sD[:, 4:5], in_=sD[:, 0:4],
                                                      axis=AX.X, op=OP.max),
                  wait=[("v", v_en)], inc=("v", 1))
                v_em2 = P.val("v")
                E("vector", lambda h, ts=ts: h.tensor_scalar(
                    out=rwm[:, ts, 4:8], in0=sD[:, 0:4], scalar1=sD[:, 4:5], scalar2=None,
                    op0=OP.is_ge),
                  wait=[("v", v_em2)], inc=("v", 1))
                v_m = P.val("v")
                E("vector", lambda h, ts=ts: h.tensor_tensor(out=rwm[:, ts, 0:4],
                                                             in0=sD[:, 0:4],
                                                             in1=rwm[:, ts, 4:8], op=OP.mult),
                  wait=[("v", v_m)], inc=("v", 1))
                v_rw = P.val("v")
                E("tensor", lambda h, ts=ts: h.transpose(
                    psC_[0:4, 128:256], rwm[:, ts, 0:4], identF[:, :]),
                  wait=[("v", v_rw)], inc=("t", 1))
                t_rt = P.val("t")
                E("vector", lambda h, ts=ts: h.tensor_copy(out=rwT[0:4, ts * 128:(ts + 1) * 128],
                                                           in_=psC_[0:4, 128:256]),
                  wait=[("t", t_rt)], inc=("v", 1))
                v_rw = P.val("v")
            v_rwb_prev = v_rw
            for ck in range(4):
                for e in range(4):
                    w0 = [("v", v_rwb_prev)] if e == 0 else None
                    MM(psC_[:, 0:512][:, e * 128:(e + 1) * 128], esel[:, e, :],
                       rwT[0:4, ck * 128:(ck + 1) * 128], True, True, w0,
                       ("t", 1) if e == 3 else None)
                t_rb = P.val("t")
                E("vector", lambda h, ck=ck: h.tensor_copy(out=rwb[:, ck, :], in_=psC_[:, 0:512]),
                  wait=[("t", t_rb)], inc=("v", 1))
                v_rwb_prev = P.val("v")
            v_rwb = P.val("v")
            for c in range(8):
                w0 = [("v", v_rwb)] if c == 0 else None
                MM(psB_[0:64, 0:512], wga[:, c, :], xnq(c), c == 0, c == 7, w0,
                   ("t", 1) if c == 7 else None)
            t_xg = P.val("t")
            E("scalar", lambda h: h.activation(out=xaf[:, :], in_=psB_[0:64, 0:512],
                                               func=AF.Identity),
              wait=[("t", t_xg)], inc=("a", 1))
            a_xg = P.val("a")
            for mi in range(4):
                w0 = [("a", a_xg)] if mi == 0 else None
                MM(ps4(mi), wgb[:, mi * 128:(mi + 1) * 128], xaf[:, :], True, True, w0,
                   ("t", 1) if mi == 3 else None)
            t_g = P.val("t")
            E("scalar", lambda h: h.activation(out=gf[:, 0:2, :].rearrange("p a b -> p (a b)"),
                                               in_=psA_[:, :], func=AF.Identity),
              wait=[("t", t_g)])
            E("scalar", lambda h: h.activation(out=gf[:, 2:4, :].rearrange("p a b -> p (a b)"),
                                               in_=psB_[:, :], func=AF.Identity),
              inc=("a", 1))
            a_gf = P.val("a")
            sHb = SP[:, 4096:6144]
            E("scalar", lambda h, sHb=sHb: h.activation(
                out=sHb, in_=gf[:, :, :].rearrange("p a b -> p (a b)"), func=AF.Tanh, scale=0.5),
              wait=[("a", a_gf)], inc=("a", 1))
            a_th = P.val("a")
            E("vector", lambda h, sHb=sHb: h.tensor_scalar(
                out=sHb, in0=sHb, scalar1=1.0, scalar2=0.5, op0=OP.add, op1=OP.mult),
              wait=[("a", a_th)], inc=("v", 1))
            v_t2 = P.val("v")
            E("vector", lambda h, sHb=sHb: h.tensor_tensor(
                out=sgf[:, :, :].rearrange("p a b -> p (a b)"),
                in0=gf[:, :, :].rearrange("p a b -> p (a b)"), in1=sHb, op=OP.mult),
              wait=[("v", v_t2)], inc=("v", 1))
            v_sg = P.val("v")

            if dbg and s == 0 and q == DQ:
                for nm, sb in (("q1f", q1f), ("q2f", q2f), ("k1f", k1f), ("k2f", k2f),
                               ("k1tm", k1tm), ("k2tm", k2tm), ("vt", vt),
                               ("gk1tm", gk1tm), ("gk2tm", gk2tm), ("rwm", rwm),
                               ("rwb", rwb), ("sgf", sgf)):
                    E("sync", lambda h, nm=nm, sb=sb: h.dma_start(out=dbgT[nm], in_=sb[:, :, :]),
                      wait=[("v", v_sg)] if nm == "q1f" else None, inc=("st", 16))
            for ck in range(NCK):
                for p in range(NPAIR_F):
                    u = state.get("l1f_u", 0)
                    psG = psA_ if u % 2 == 0 else psE_
                    psl = slice(p * 128, (p + 1) * 128)
                    cksl = slice(ck * 128, (ck + 1) * 128)
                    w0 = [("v", v_sg)] if (ck == 0 and p == 0) else None
                    E("vector", lambda h, ck=ck, psl=psl: h.tensor_copy(
                        out=gkc[:, 0:128], in_=gk1tm[:, ck, psl]),
                      wait=(w0 or []) + [("t", state.get("l1f_csg", 0))])
                    E("vector", lambda h, ck=ck, psl=psl: h.tensor_tensor(
                        out=gkc[:, 128:640].rearrange("p (e w) -> p e w", w=128),
                        in0=gk2tm[:, ck, psl].unsqueeze(1).broadcast_to([128, 4, 128]),
                        in1=rwm[:, ck, 4:8].unsqueeze(2).broadcast_to([128, 4, 128]),
                        op=OP.mult), inc=("v", 1))
                    v_gkc = P.val("v")
                    for j in range(5):
                        w0 = [("v", v_gkc)]
                        if j == 0:
                            w0.append(("a", state.get(("gfmw", id(psG)), 0)))
                            w0.append(("v", state.get(("gfmv", id(psG)), 0)))
                        MM(psG[:, 0:640][:, j * 128:(j + 1) * 128],
                           gkc[:, j * 128:(j + 1) * 128], Lb[:, :], True, True,
                           w0 if j == 0 else None, ("t", 1) if j == 4 else None)
                    t_cs = P.val("t")
                    if state.get("l1f_pend") is not None:
                        state["l1f_pend"]()
                        state["l1f_pend"] = None
                    E("scalar", lambda h, psG=psG, u=u: h.activation(
                        out=dec[:, u % 2, :].unsqueeze(2),
                        in_=psG[:, 0:640].rearrange("p (j w) -> p j w", w=128)[:, :, 127:128],
                        func=AF.Exp),
                      wait=[("t", t_cs)])
                    E("scalar", lambda h, psG=psG: h.activation(out=expg[:, :], in_=psG[:, 0:640],
                                                                func=AF.Exp))
                    E("scalar", lambda h, psG=psG: h.activation(out=negg[:, :], in_=psG[:, 0:640],
                                                                func=AF.Exp, scale=-1.0),
                      inc=("a", 1))
                    a_ex = P.val("a")
                    state[("gfmw", id(psG))] = a_ex
                    E("vector", lambda h, p=p, cksl=cksl: h.tensor_copy(
                        out=qcat[:, 0:128], in_=q1f[:, p, cksl]),
                      wait=[("a", a_ex), ("t", state.get("l1f_Aw", 0))])
                    E("vector", lambda h, p=p, ck=ck, cksl=cksl: h.tensor_tensor(
                        out=qcat[:, 128:640].rearrange("p (e w) -> p e w", w=128),
                        in0=q2f[:, p, cksl].unsqueeze(1).broadcast_to([128, 4, 128]),
                        in1=rwb[:, ck, :].rearrange("p (e w) -> p e w", w=128), op=OP.mult))
                    E("vector", lambda h: h.tensor_tensor(out=qcat[:, :], in0=qcat[:, :],
                                                          in1=expg[:, :], op=OP.mult))
                    E("vector", lambda h, p=p, cksl=cksl: h.tensor_copy(
                        out=kcat[:, 0:128], in_=k1f[:, p, cksl]))
                    E("vector", lambda h, p=p, ck=ck, cksl=cksl: h.tensor_tensor(
                        out=kcat[:, 128:640].rearrange("p (e w) -> p e w", w=128),
                        in0=k2f[:, p, cksl].unsqueeze(1).broadcast_to([128, 4, 128]),
                        in1=rwb[:, ck, :].rearrange("p (e w) -> p e w", w=128), op=OP.mult))
                    E("vector", lambda h: h.tensor_tensor(out=kcat[:, :], in0=kcat[:, :],
                                                          in1=negg[:, :], op=OP.mult))
                    E("vector", lambda h, p=p: h.tensor_copy(
                        out=Sbf[0:64, :, 0:64], in_=Sst[0:64, p, :, 0:64]),
                      wait=[("t", state.get("t_oi", 0))])
                    E("vector", lambda h, p=p: h.tensor_copy(
                        out=Sbf[64:128, :, 64:128], in_=Sst[64:128, p, :, 64:128]),
                      inc=("v", 1))
                    v_bld = P.val("v")

                    def pend(p=p, ck=ck, cksl=cksl, psl=psl, v_bld=v_bld, psG=psG, u=u):
                        psBb = psB_[:, :].bitcast(BF16)
                        for j in range(5):
                            w0 = None
                            if j == 0:
                                w0 = [("v", v_bld), ("v", state.get("l1f_tpw", 0)),
                                      ("g", g_c)]
                            E("tensor", lambda h, j=j, psBb=psBb: h.transpose(
                                psBb[:, j * 128:(j + 1) * 128],
                                kcat[:, j * 128:(j + 1) * 128], identB[:, :]),
                              wait=w0, inc=("t", 1) if j == 4 else None)
                        t_tp5 = P.val("t")
                        E("vector", lambda h, psBb=psBb: h.tensor_copy(
                            out=kctm[:, 0:640], in_=psBb[:, 0:640]),
                          wait=[("t", t_tp5)], inc=("v", 1))
                        state["l1f_tpw"] = P.val("v")
                        for j in range(5):
                            w0 = [("v", v_bld)] if j == 0 else None
                            MM(psC_[:, 0:128], kcat[0:64, j * 128:(j + 1) * 128],
                               qcat[0:64, j * 128:(j + 1) * 128], j == 0, j == 4, w0,
                               None, (0, 0))
                            MM(psD_[:, 0:128], kcat[64:128, j * 128:(j + 1) * 128],
                               qcat[64:128, j * 128:(j + 1) * 128], j == 0, j == 4, None,
                               ("t", 1) if j == 4 else None, (64, 0))
                        t_A = P.val("t")
                        E("vector", lambda h: h.tensor_tensor(
                            out=Am[:, 0:128], in0=psC_[:, 0:128], in1=Lf[:, :], op=OP.mult),
                          wait=[("t", t_A)])
                        E("vector", lambda h: h.tensor_tensor(
                            out=Am[:, 128:256], in0=psD_[:, 0:128], in1=Lf[:, :], op=OP.mult),
                          inc=("v", 1))
                        v_am = P.val("v")
                        MM(psC_[0:64, 128:256], vt[:, ck, p * 128:p * 128 + 64], Am[:, 0:128],
                           True, False, [("v", v_am)], None, (0, 0))
                        MM(psD_[64:128, 128:256], vt[:, ck, p * 128 + 64:(p + 1) * 128],
                           Am[:, 128:256], True, False, None, None, (0, 64))
                        for j in range(5):
                            MM(psC_[0:64, 128:256], Sbf[:, j, 0:64],
                               qcat[:, j * 128:(j + 1) * 128], False, j == 4, None, None,
                               (0, 0))
                            MM(psD_[64:128, 128:256], Sbf[:, j, 64:128],
                               qcat[:, j * 128:(j + 1) * 128], False, j == 4, None,
                               ("t", 1) if j == 4 else None, (0, 64))
                        t_oi = P.val("t")
                        state["t_oi"] = t_oi
                        for j in range(5):
                            w0 = [("v", state["l1f_tpw"])] if j == 0 else None
                            MM(psG[:, 0:640][:, j * 128:(j + 1) * 128],
                               kctm[:, j * 128:(j + 1) * 128], vt[:, ck, psl], True, True,
                               w0, ("t", 1) if j == 4 else None)
                        t_kv2 = P.val("t")
                        state["l1f_Aw"] = t_kv2
                        E("vector", lambda h, p=p, psG=psG: h.tensor_tensor(
                            out=Sst[:, p, :, :].rearrange("p a b -> p (a b)"),
                            in0=Sst[:, p, :, :].rearrange("p a b -> p (a b)"),
                            in1=psG[:, 0:640], op=OP.add),
                          wait=[("t", t_kv2)])
                        E("vector", lambda h, p=p, u=u: h.tensor_tensor(
                            out=Sst[:, p, :, :], in0=Sst[:, p, :, :],
                            in1=dec[:, u % 2, :].unsqueeze(2).broadcast_to([128, 5, 128]),
                            op=OP.mult), inc=("v", 1))
                        state[("gfmv", id(psG))] = P.val("v")
                        E("vector", lambda h, p=p, cksl=cksl: h.tensor_copy(
                            out=ob[0:64, p, cksl], in_=psC_[0:64, 128:256]),
                          wait=[("t", t_oi)])
                        E("vector", lambda h, p=p, cksl=cksl: h.tensor_copy(
                            out=ob[64:128, p, cksl], in_=psD_[64:128, 128:256]),
                          inc=("v", 1))
                    state["l1f_pend"] = pend
                    state["l1f_u"] = u + 1
            if state.get("l1f_pend") is not None:
                state["l1f_pend"]()
                state["l1f_pend"] = None
            v_p2 = P.val("v")

            obf = ob[:, :, :].rearrange("p a b -> p (a b)")
            E("vector", lambda h, obf=obf: h.tensor_tensor(out=SP[:, 0:2048], in0=obf, in1=obf,
                                                           op=OP.mult),
              wait=[("v", v_p2)], inc=("v", 1))
            v_o2 = P.val("v")
            for j in range(4):
                w0 = None
                if j == 0:
                    w0 = [("v", v_o2), ("a", state.get("p2_ps_war", 0))]
                MM(ps4(j), OSEL[:, :], SP[:, j * 512:(j + 1) * 512], True, True, w0,
                   ("t", 1) if j == 3 else None)
            t_rms = P.val("t")
            lnbuf = SP[:, 4096:8192].bitcast(F32)
            E("scalar", lambda h, lnbuf=lnbuf: h.activation(out=lnbuf[:, 0:1024], in_=psA_[:, :],
                                               func=AF.Ln, scale=1.0 / 64.0, bias=1e-5),
              wait=[("t", t_rms)])
            E("scalar", lambda h, lnbuf=lnbuf: h.activation(out=lnbuf[:, 1024:2048], in_=psB_[:, :],
                                               func=AF.Ln, scale=1.0 / 64.0, bias=1e-5),
              inc=("a", 1))
            a_l = P.val("a")
            E("scalar", lambda h, lnbuf=lnbuf: h.activation(out=SP[:, 2048:4096], in_=lnbuf[:, 0:2048],
                                               func=AF.Exp, scale=-0.5),
              wait=[("a", a_l)], inc=("a", 1))
            a_rstd = P.val("a")
            state["p3_ps_war"] = a_rstd
            E("vector", lambda h, obf=obf: h.tensor_tensor(out=obf, in0=obf,
                                                           in1=SP[:, 2048:4096], op=OP.mult),
              wait=[("a", a_rstd)])
            E("vector", lambda h, obf=obf: h.tensor_scalar(out=obf, in0=obf, scalar1=nwS[:, 0:1],
                                                           scalar2=None, op0=OP.mult),
              wait=[("lp", lp)])
            E("vector", lambda h, obf=obf: h.tensor_tensor(
                out=obf, in0=obf, in1=sgf[:, :, :].rearrange("p a b -> p (a b)"), op=OP.mult),
              inc=("v", 1))
            v_of = P.val("v")
            v_ep = None
            for m in range(8):
                for c in range(4):
                    w0 = None
                    if c == 0:
                        w0 = [("v", v_of)]
                        if v_ep is not None:
                            w0 = [("v", v_ep)]
                    MM(psC_[:, 0:512], wo[:, c, m * 128:(m + 1) * 128], ob[:, c, :],
                       c == 0, c == 3, w0, ("t", 1) if c == 3 else None)
                t_wo = P.val("t")
                st_now = P.val("st")
                w_x1 = [("st", st_now), ("v", state.get("ep_v", 0))]
                if m == 0:
                    w_x1.append(("v", v_of))
                E("sync", lambda h, m=m, s=s, q0=q0: h.dma_start(
                    out=U[:, 0, :], in_=x1T[s, m * 128:(m + 1) * 128, q0:q0 + QT]),
                  wait=w_x1, inc=("ld", 16))
                E("sync", lambda h, m=m, s=s, q0=q0: h.dma_start(
                    out=U[:, 1, :], in_=attnT[s, m * 128:(m + 1) * 128, q0:q0 + QT]),
                  inc=("ld", 16))
                ld_r = P.val("ld")
                E("vector", lambda h: h.tensor_copy(out=ot[:, :], in_=psC_[:, 0:512]),
                  wait=[("t", t_wo), ("ld", ld_r)], inc=("v", 1))
                v_ep = P.val("v")
                E("vector", lambda h: h.tensor_tensor(out=ot[:, :], in0=ot[:, :],
                                                      in1=U[:, 0, :], op=OP.add))
                E("vector", lambda h: h.tensor_tensor(out=ot[:, :], in0=ot[:, :],
                                                      in1=U[:, 1, :], op=OP.add),
                  inc=("v", 1))
                state["ep_v"] = P.val("v")
                v_ot = P.val("v")
                E("sync", lambda h, m=m, s=s, q0=q0: h.dma_start(
                    out=out2T[s, m * 128:(m + 1) * 128, q0:q0 + QT], in_=ot[:, :]),
                  wait=[("v", v_ot)], inc=("st", 16))
            state["p1_war"] = (P.val("a"), P.val("v"))
        state["ln_war"] = P.val("t")
        state["ps_war"] = P.val("a")



_CACHE = {}

T_ = 2048


def _program():
    if "p" in _CACHE:
        return _CACHE["p"]
    nc = bass.Bass("TRN2", target_bir_lowering=False)
    P = Plan(nc)
    F = F32
    d = {}

    def din(name, shape):
        d[name] = nc.dram_tensor(name, shape, F, kind="ExternalInput").ap()
    din("xT", [2, 1024, T_])
    din("wqkvT", [1024, 3072]); din("woutT", [1024, 1024]); din("cbT", [1024, 1024])
    din("bout", [128, 8]); din("scb", [128, 8]); din("fbias", [128, 8])
    din("wqT", [1024, 1024]); din("wkT", [1024, 1024]); din("wvT", [1024, 1024])
    din("woT", [1024, 1024]); din("bqf", [128, 8]); din("bkf", [128, 8])
    din("bvr", [1, 1024]); din("bof", [128, 8])
    din("wq", [1024, 512]); din("wk", [1024, 512]); din("wv", [1024, 512])
    din("wqa", [1024, 64]); din("wqb", [64, 512]); din("wka", [1024, 64])
    din("wkb", [64, 512]); din("wg1a", [1024, 16]); din("wg1b", [16, 512])
    din("bg1", [1, 512]); din("wg2a", [1024, 16]); din("wg2b", [16, 512])
    din("bg2", [1, 512]); din("we", [1024, 4]); din("wga", [1024, 64])
    din("wgb", [64, 512]); din("nw", [128, 1]); din("wo", [512, 1024])
    d["x1T"] = nc.dram_tensor("x1T", [2, 1024, T_], F, kind="Internal").ap()
    d["attnT"] = nc.dram_tensor("attnT", [2, 1024, T_], F, kind="Internal").ap()
    d["out2T"] = nc.dram_tensor("out2T", [2, 1024, T_], F, kind="ExternalOutput").ap()

    for s in ("ld", "lp", "w", "st", "t", "v", "a", "g"):
        P.sem(s)
    cpool = P.sbuf("cpool_root", [128, 4], F32)
    for i, cv in enumerate((1e-5, 1e-6, 1e-30, 1.0)):
        nc.const_aps.register(F32, cv, cpool[:, i:i + 1])
        P.emit("vector", lambda h, i=i, cv=cv: h.memset(cpool[:, i:i + 1], cv),
               inc=("v", 1) if i == 3 else None)

    with P.scope():
        emit_l0(nc, P, d, 2)
    P.barrier()
    d1 = dict(d)
    d1["xT"] = d["x1T"]
    with P.scope():
        emit_l1a(nc, P, d1, 2)
    P.barrier()
    with P.scope():
        emit_l1f(nc, P, d, 2)
    P.wait("sync", ("st", P.val("st")))
    P.finalize()
    _CACHE["p"] = nc
    return nc


def kernel(L0_attn_x, L0_attn_Wqkv, L0_attn_Wout, L0_attn_bout, L0_ffn_CB, L0_ffn_SCB,
           L0_ffn_bias, L1_attn_W_Q, L1_attn_b_Q, L1_attn_W_K, L1_attn_b_K, L1_attn_W_V,
           L1_attn_b_V, L1_attn_W_O, L1_attn_b_O, L1_ffn_Wq, L1_ffn_Wk, L1_ffn_Wv,
           L1_ffn_Wqa, L1_ffn_Wqb, L1_ffn_Wka, L1_ffn_Wkb, L1_ffn_Wg1a, L1_ffn_Wg1b,
           L1_ffn_bg1, L1_ffn_Wg2a, L1_ffn_Wg2b, L1_ffn_bg2, L1_ffn_We, L1_ffn_Wga,
           L1_ffn_Wgb, L1_ffn_norm_w, L1_ffn_Wo):
    C = np.ascontiguousarray
    ncores = 8
    x = np.asarray(L0_attn_x)
    fm8 = lambda b: C(np.asarray(b).reshape(8, 128).T)

    nc = _program()
    w = {
        "wqkvT": C(np.asarray(L0_attn_Wqkv).T),
        "woutT": C(np.asarray(L0_attn_Wout).T),
        "cbT": C(np.asarray(L0_ffn_CB).T),
        "bout": fm8(L0_attn_bout),
        "scb": fm8(L0_ffn_SCB),
        "fbias": fm8(L0_ffn_bias),
        "wqT": C(np.asarray(L1_attn_W_Q).transpose(2, 0, 1).reshape(1024, 1024)),
        "wkT": C(np.asarray(L1_attn_W_K).transpose(2, 0, 1).reshape(1024, 1024)),
        "wvT": C(np.asarray(L1_attn_W_V).transpose(2, 0, 1).reshape(1024, 1024)),
        "woT": C(np.asarray(L1_attn_W_O).transpose(0, 2, 1).reshape(1024, 1024)),
        "bqf": fm8(np.asarray(L1_attn_b_Q).reshape(1024)),
        "bkf": fm8(np.asarray(L1_attn_b_K).reshape(1024)),
        "bvr": C(np.asarray(L1_attn_b_V).reshape(1, 1024)),
        "bof": fm8(L1_attn_b_O),
        "wq": C(np.asarray(L1_ffn_Wq)), "wk": C(np.asarray(L1_ffn_Wk)),
        "wv": C(np.asarray(L1_ffn_Wv)), "wqa": C(np.asarray(L1_ffn_Wqa)),
        "wqb": C(np.asarray(L1_ffn_Wqb)), "wka": C(np.asarray(L1_ffn_Wka)),
        "wkb": C(np.asarray(L1_ffn_Wkb)), "wg1a": C(np.asarray(L1_ffn_Wg1a)),
        "wg1b": C(np.asarray(L1_ffn_Wg1b)), "bg1": C(np.asarray(L1_ffn_bg1).reshape(1, 512)),
        "wg2a": C(np.asarray(L1_ffn_Wg2a)), "wg2b": C(np.asarray(L1_ffn_Wg2b)),
        "bg2": C(np.asarray(L1_ffn_bg2).reshape(1, 512)), "we": C(np.asarray(L1_ffn_We)),
        "wga": C(np.asarray(L1_ffn_Wga)), "wgb": C(np.asarray(L1_ffn_Wgb)),
        "nw": C(np.tile(np.asarray(L1_ffn_norm_w), 2).reshape(128, 1)),
        "wo": C(np.asarray(L1_ffn_Wo)),
    }
    in_maps = [dict(w, xT=C(x[2 * i:2 * i + 2].transpose(0, 2, 1))) for i in range(ncores)]
    res = run_bass_kernel_spmd(nc, in_maps, core_ids=list(range(ncores)))
    out = np.concatenate([np.asarray(res.results[i]["out2T"]).transpose(0, 2, 1)
                          for i in range(ncores)], axis=0)
    return out.astype(np.float32)
